# version 06
# baseline (speedup 1.0000x reference)
import numpy as np
import ml_dtypes
import concourse.bass as bass
import concourse.mybir as mybir
from concourse.bass_utils import run_bass_kernel_spmd

F32 = mybir.dt.float32
BF16 = mybir.dt.bfloat16
FP16 = mybir.dt.float16
AF = mybir.ActivationFunctionType
ALU = mybir.AluOpType
AX = mybir.AxisListType

T = 2048
D = 1024
NCH = 8
TT = 512
NTT = T // TT
NEG = -3.0e38
EPS_LN = 1e-5


class _Call:
    __slots__ = ("name", "a", "k", "_then")
    def __init__(self, name, a, k):
        self.name = name; self.a = a; self.k = k; self._then = None
    def then_inc(self, sem, n=1):
        if self._then is not None:
            raise RuntimeError("then_inc already set on this op")
        self._then = (sem, n)
        return self
    def play(self, eng):
        try:
            r = getattr(eng, self.name)(*self.a, **self.k)
        except Exception as e:
            shapes = [getattr(a, "shape", None) for a in self.a]
            raise RuntimeError("op %s shapes=%s: %s" % (self.name, shapes, e)) from e
        if self._then is not None:
            r.then_inc(*self._then)


class _Rec:
    def __init__(self):
        self.ops = []
    def __getattr__(self, name):
        def call(*a, **k):
            c = _Call(name, a, k)
            self.ops.append(c)
            return c
        return call


class Prog:
    def __init__(self, nc):
        self.nc = nc
        self._sems = {}
        self.cnt = {}
        self.blkseg = ""

    def sem(self, name):
        if name not in self._sems:
            self._sems[name] = self.nc.semaphore(name).__enter__()
            self.cnt[name] = 0
        return self._sems[name]

    def bump(self, name, k=1):
        self.sem(name)
        self.cnt[name] += k
        return self.cnt[name]


class Ph:
    def __init__(self, prog, name):
        self.prog = prog
        self.name = name + prog.blkseg
        self.T = _Rec(); self.V = _Rec(); self.S = _Rec(); self.G = _Rec(); self.Y = _Rec()

    def emit(self):
        nc = self.prog.nc
        with nc.Block(self.name) as blk:
            if self.G.ops:
                @blk.gpsimd
                def _(e, ops=self.G.ops):
                    for c in ops: c.play(e)
            if self.T.ops:
                @blk.tensor
                def _(e, ops=self.T.ops):
                    for c in ops: c.play(e)
            if self.V.ops:
                @blk.vector
                def _(e, ops=self.V.ops):
                    for c in ops: c.play(e)
            if self.S.ops:
                @blk.scalar
                def _(e, ops=self.S.ops):
                    for c in ops: c.play(e)
            if self.Y.ops:
                @blk.sync
                def _(e, ops=self.Y.ops):
                    for c in ops: c.play(e)


class Banks:
    def __init__(self, tensors):
        self.b = [[t, None] for t in tensors]
        self.i = 0
    def take(self, ph, p):
        t, need = self.b[self.i]
        idx = self.i
        self.i = (self.i + 1) % len(self.b)
        if need is not None and need != "taken":
            ph.T.wait_ge(p.sem(need[0]), need[1])
        self.b[idx][1] = "taken"
        return idx, t
    def release(self, idx, sem_name, count):
        self.b[idx][1] = (sem_name, count)
    def reset(self):
        for e in self.b:
            e[1] = None
        self.i = 0


def wdma(ph, p, dst, src, sem="sW"):
    c = p.bump(sem, 16)
    ph.G.dma_start(dst, src).then_inc(p.sem(sem), 16)
    return c


def iodma(ph, p, dst, src, sem="sX"):
    c = p.bump(sem, 16)
    ph.G.dma_start(dst, src).then_inc(p.sem(sem), 16)
    return c



def host_constants():
    bf = ml_dtypes.bfloat16
    con = {}
    con["c_ident"] = np.eye(128, dtype=np.float32).astype(bf)
    m = np.zeros((2, 128, 512), np.float32)
    for r in range(2):
        il = r * 128 + np.arange(128)[:, None]
        jl = np.arange(512)[None, :]
        bad = (jl < il) | (jl > il + 256)
        m[r][bad] = NEG
    con["c_mask_w"] = m.reshape(256, 512)
    m0 = np.zeros((256, 256), np.float32)
    i0 = np.arange(256)[:, None]; j0 = np.arange(256)[None, :]
    m0[(j0 > i0) | (j0 < i0 - 256)] = NEG
    con["c_mask_w0"] = m0
    HD = 64; WINDOW = 256
    half = np.arange(0, HD, 2, dtype=np.float32)
    inv_freq = 1.0 / 10000.0 ** (half / HD)
    t = np.arange(T, dtype=np.float32)
    freqs = t[:, None] * inv_freq[None, :]
    freqs = np.concatenate([freqs, freqs], -1)
    base_scale = (half + 0.4 * HD) / (1.4 * HD)
    power = (t - T // 2) / (WINDOW // 2)
    scale = base_scale[None, :] ** power[:, None]
    scale = np.concatenate([scale, scale], -1)
    cos = np.cos(freqs); sin = np.sin(freqs)
    sign = np.ones(64, np.float32); sign[:32] = -1.0
    con["c_rot_qc"] = (cos * scale).T.astype(np.float32).copy()
    con["c_rot_qs"] = (sin * scale * sign[None, :]).T.astype(np.float32).copy()
    con["c_rot_kc"] = (cos / scale).T.astype(np.float32).copy()
    con["c_rot_ks"] = (sin / scale * sign[None, :]).T.astype(np.float32).copy()
    mc = np.zeros((4, 128, 512), np.float32)
    for r in range(4):
        qa = r * 128 + np.arange(128)[:, None]
        ka = np.arange(512)[None, :]
        mc[r][ka > qa] = NEG
    con["c_mask_causal"] = mc.reshape(512, 512)
    mcT = np.zeros((4, 128, 512), np.float32)
    for r in range(4):
        ja = r * 128 + np.arange(128)[:, None]
        ia = np.arange(512)[None, :]
        mcT[r][ja > ia] = NEG
    con["c_mask_causal_T"] = mcT.reshape(512, 512)
    con["c_iota8"] = np.broadcast_to(np.arange(8, dtype=np.float32), (128, 8)).copy()
    b8 = np.zeros((2, 128), np.float32); b8[0, 0:64] = 8.0; b8[1, 64:128] = 8.0
    b1 = np.zeros((2, 128), np.float32); b1[0, 0:64] = 1.0; b1[1, 64:128] = 1.0
    con["c_bsel8"] = b8.astype(ml_dtypes.bfloat16)
    con["c_bsel1"] = b1.astype(ml_dtypes.bfloat16)
    return con



def setup_consts(p, ph, sb):
    nc = p.nc
    ph.G.memset(sb["ones_bf"][:, :], 1.0)
    ph.G.memset(sb["ones_f"][:, :], 1.0)


def bcast_row(ph, p, banks, row_ap, sb, width=TT):
    idx, b = banks.take(ph, p)
    ct = p.bump("sT")
    ph.T.matmul(b[:, 0:width], sb["ones_f"][0:1, :], row_ap,
                start=True, stop=True).then_inc(p.sem("sT"), 1)
    return idx, b, ct




def emit_ln(p, ph, banks, sb, src_bf, sq, rows, apply_fn, m2row=None):
    sV, sS, sT = p.sem("sV"), p.sem("sS"), p.sem("sT")
    for c in range(NCH):
        cs = p.bump("sS")
        op = ph.S.activation(sq[:, c * T:(c + 1) * T], src_bf(c), AF.Square)
        op.then_inc(sS, 1)
    ph.T.wait_ge(sS, p.cnt["sS"])
    for tt in range(NTT):
        iS, bS = banks.take(ph, p)
        iQ, bQ = banks.take(ph, p)
        for c in range(NCH):
            ph.T.matmul(bS[0:1, 0:TT], sb["ones_bf"][:, 0:1],
                        src_bf(c)[:, tt * TT:(tt + 1) * TT],
                        start=(c == 0), stop=(c == NCH - 1))
        for c in range(NCH):
            op = ph.T.matmul(bQ[0:1, 0:TT], sb["ones_bf"][:, 0:1],
                             sq[:, c * T + tt * TT: c * T + (tt + 1) * TT],
                             start=(c == 0), stop=(c == NCH - 1))
        ct = p.bump("sT")
        op.then_inc(sT, 1)
        ph.V.wait_ge(sT, ct)
        mrow = rows[0:1, 0:TT]
        vrow = rows[0:1, TT:2 * TT]
        ph.V.tensor_scalar_mul(mrow, bS[0:1, 0:TT], 1.0 / D)
        cv = p.bump("sV")
        ph.V.tensor_scalar_mul(vrow, bQ[0:1, 0:TT], 1.0 / D).then_inc(sV, 1)
        ph.V.wait_ge(sV, cv)
        banks.release(iS, "sV", cv)
        banks.release(iQ, "sV", cv)
        ph.V.tensor_mul(m2row, mrow, mrow)
        cv = p.bump("sV")
        ph.V.scalar_tensor_tensor(vrow, vrow, EPS_LN, m2row,
                                  ALU.add, ALU.subtract).then_inc(sV, 1)
        ph.S.wait_ge(sV, cv)
        cs = p.bump("sS")
        ph.S.activation(vrow, vrow, AF.Sqrt).then_inc(sS, 1)
        ph.V.wait_ge(sS, cs)
        cv = p.bump("sV")
        ph.V.reciprocal(vrow, vrow).then_inc(sV, 1)
        ph.V.wait_ge(sV, cv)
        ph.T.wait_ge(sV, cv)
        im, bm, ctm = bcast_row(ph, p, banks, mrow, sb)
        ir, br, ctr = bcast_row(ph, p, banks, vrow, sb)
        ph.V.wait_ge(sT, ctr)
        for c in range(NCH):
            apply_fn(c, tt, bm[:, 0:TT], br[:, 0:TT])
        cv = p.bump("sV")
        ph.V.ops[-1].then_inc(sV, 1)
        banks.release(im, "sV", cv)
        banks.release(ir, "sV", cv)


def ao_slice(attn_out, hp, qt, h):
    return attn_out[h * 64:(h + 1) * 64, hp * T + qt * 128: hp * T + (qt + 1) * 128]


def emit_shift_cast(p, ph, sb, src, dst):
    for c in range(NCH):
        a = src[:, c * T:(c + 1) * T]
        d = dst[:, c * T:(c + 1) * T]
        if c < 4:
            ph.V.tensor_copy(d, a)
        else:
            ph.V.memset(d[:, 0:1], 0.0)
            ph.V.tensor_copy(d[:, 1:T], a[:, 0:T - 1])
    cv = p.bump("sV")
    ph.V.ops[-1].then_inc(p.sem("sV"), 1)
    return cv


def build_l0ffn(nl=6, stop_after=None, pair_level=9, dbg=None, npairs=8,
                nc=None, p=None, seg="", xin_ap=None, xout_ap=None, finalize=True):
    if nc is None:
        nc = bass.Bass("TRN2", target_bir_lowering=False,
                       dynamic_dma_scratch_size=16384)
        p = Prog(nc)
    p.blkseg = seg
    xin = xin_ap if xin_ap is not None else nc.dram_tensor(
        "xin" + seg, [D, T], F32, kind="ExternalInput").ap()
    xout = xout_ap if xout_ap is not None else nc.dram_tensor(
        "xout" + seg, [D, T], F32, kind="ExternalOutput").ap()
    wqkv = nc.dram_tensor("wqkv" + seg, [6, 8, 128, 3072], BF16, kind="ExternalInput").ap()
    wo = nc.dram_tensor("wo" + seg, [6, 8, 128, 1024], BF16, kind="ExternalInput").ap()
    w1a = nc.dram_tensor("w1a" + seg, [6, 22, 128, 1024], BF16, kind="ExternalInput").ap()
    w1g = nc.dram_tensor("w1g" + seg, [6, 22, 128, 1024], BF16, kind="ExternalInput").ap()
    w2 = nc.dram_tensor("w2" + seg, [6, 8, 128, 2816], BF16, kind="ExternalInput").ap()
    b1a = nc.dram_tensor("b1a" + seg, [6, 128, 22], F32, kind="ExternalInput").ap()
    b1g = nc.dram_tensor("b1g" + seg, [6, 128, 22], F32, kind="ExternalInput").ap()
    lng = nc.dram_tensor("lng" + seg, [6, 128, 8], F32, kind="ExternalInput").ap()
    lnb = nc.dram_tensor("lnb" + seg, [6, 128, 8], F32, kind="ExternalInput").ap()
    b2d = nc.dram_tensor("b2d" + seg, [6, 128, 8], F32, kind="ExternalInput").ap()
    qsd = nc.dram_tensor("qsd" + seg, [6, 64, 1], F32, kind="ExternalInput").ap()
    ksd = nc.dram_tensor("ksd" + seg, [6, 64, 1], F32, kind="ExternalInput").ap()
    identd = nc.dram_tensor("c_ident" + seg, [128, 128], BF16, kind="ExternalInput").ap()
    maskwd = nc.dram_tensor("c_mask_w" + seg, [128, 1024], F32, kind="ExternalInput").ap()
    bsel8d = nc.dram_tensor("c_bsel8" + seg, [2, 128], BF16, kind="ExternalInput").ap()
    bsel1d = nc.dram_tensor("c_bsel1" + seg, [2, 128], BF16, kind="ExternalInput").ap()
    mask0d = nc.dram_tensor("c_mask_w0" + seg, [128, 512], F32, kind="ExternalInput").ap()
    rotd = {}
    for nm in ("qc", "qs", "kc", "ks"):
        rotd[nm] = nc.dram_tensor("c_rot_" + nm + seg, [64, T], F32, kind="ExternalInput").ap()
    dbgd = [nc.dram_tensor("dbg%d" % i + seg, [128, T], F32, kind="ExternalOutput").ap()
            for i in range(4)] if dbg else None

    ctx = []
    def alloc(name, shape, dt, psum=False):
        name = name + seg
        h = nc.psum_tensor(name, shape, dt) if psum else nc.sbuf_tensor(name, shape, dt)
        t = h.__enter__()
        ctx.append(h)
        return t

    y = alloc("y", [128, NCH * T], F32)
    xs = alloc("xs", [128, NCH * T], BF16)
    arena = alloc("arena", [128, 45056], BF16)
    wbuf = alloc("wbuf", [128, 6144], BF16)
    rows = alloc("rows", [128, 2048], F32)
    colbuf = alloc("colbuf", [128, 80], F32)
    gtmp = alloc("gtmp", [128, 512], BF16)
    ident = alloc("ident", [128, 128], BF16)
    ones_bf = alloc("ones_bf", [128, 1], BF16)
    ones_f = alloc("ones_f", [1, 128], F32)
    sel2 = alloc("sel2", [128, 2], BF16)
    bsel8 = alloc("bsel8", [2, 128], BF16)
    bsel1 = alloc("bsel1", [2, 128], BF16)
    pb = [alloc("pb%d" % i, [128, TT], F32, psum=True) for i in range(7)]
    ptr = alloc("ptr", [128, 1024], BF16, psum=True)
    sb = dict(ones_bf=ones_bf, ones_f=ones_f)
    banks = Banks(pb)

    A = lambda off, ln: arena[:, off:off + ln]
    af32 = arena[:, :].bitcast(F32)
    qAB, kAB = A(0, T), A(T, T)
    qr, kr = A(2 * T, T), A(3 * T, T)
    swp = sqp = A(4 * T, T)
    vTf, vtok = A(5 * T, T), A(6 * T, T)
    P = A(7 * T, 512)
    PT = A(7 * T + 512, 512)
    msS = af32[:, 7 * T // 2 + 512: 7 * T // 2 + 1024]
    maskw = af32[:, 8192:9216]
    mask0 = af32[:, 9216:9728]
    attn_out = A(19456, NCH * T)
    qct, qst = A(36864, T), A(38912, T)
    kct, kst = A(40960, T), A(43008, T)


    sW, sX, sT, sV, sS = (p.sem(n) for n in ("sW", "sX", "sT", "sV", "sS"))
    state = dict(ptr_cv=0, P_free=0, ms_free=0, col_free=0)

    ph = Ph(p, "init")
    ph.G.memset(ones_bf[:, :], 1.0)
    ph.G.memset(ones_f[:, :], 1.0)
    ph.G.memset(sel2[:, :], 0.0)
    ph.G.memset(sel2[0:64, 0:1], 1.0)
    ph.G.memset(sel2[64:128, 1:2], 1.0)
    c1 = iodma(ph, p, ident[:, :], identd)
    iodma(ph, p, bsel8[:, :], bsel8d)
    iodma(ph, p, bsel1[:, :], bsel1d)
    c2 = iodma(ph, p, y[:, :].rearrange("p (c t) -> p c t", t=T),
               xin.rearrange("(c p) t -> p c t", p=128))
    ph.G.wait_ge(sX, p.cnt["sX"])
    ph.emit()

    ph = Ph(p, "outerln")
    cast = A(0, NCH * T)
    for c in range(NCH):
        ph.V.tensor_copy(cast[:, c * T:(c + 1) * T], y[:, c * T:(c + 1) * T])
    cv = p.bump("sV")
    ph.V.ops[-1].then_inc(sV, 1)
    ph.S.wait_ge(sV, cv)
    ph.T.wait_ge(sV, cv)

    def oln_apply(c, tt, mb, rb):
        sl = slice(c * T + tt * TT, c * T + (tt + 1) * TT)
        ph.V.tensor_sub(y[:, sl], y[:, sl], mb)
        ph.V.tensor_mul(y[:, sl], y[:, sl], rb)
    emit_ln(p, ph, banks, sb, lambda c: cast[:, c * T:(c + 1) * T],
            A(16384, NCH * T), rows, oln_apply, m2row=af32[0:1, 16384:16896])
    ph.emit()
    banks.reset()

    stops = {"ln": 0, "pre": 1, "pairs": 2, "wo": 3, "fln": 4, "fj": 5, "fw2": 6}
    stop = stops.get(stop_after, 99)
    for li in range(nl):
        if stop < 1:
            break
        ph = Ph(p, "a%d_pre" % li)
        cq = wdma(ph, p, colbuf[0:64, 0:1], qsd[li], sem="sWc")
        ck = wdma(ph, p, colbuf[0:64, 2:3], ksd[li], sem="sWc")
        crot = [wdma(ph, p, tbl[0:64, :], rotd[nm], sem="sWc")
                for tbl, nm in ((qct, "qc"), (qst, "qs"), (kct, "kc"), (kst, "ks"))]
        cm1 = iodma(ph, p, maskw, maskwd)
        cm2 = iodma(ph, p, mask0, mask0d)
        ph.G.wait_ge(sX, p.cnt["sX"])
        ph.V.wait_ge(p.sem("sWc"), p.cnt["sWc"])
        ph.V.tensor_copy(colbuf[0:32, 1:2], colbuf[32:64, 0:1])
        ph.V.tensor_copy(colbuf[32:64, 1:2], colbuf[0:32, 0:1])
        ph.V.tensor_scalar_mul(qct[0:64, :], qct[0:64, :], colbuf[0:64, 0:1])
        ph.V.tensor_scalar_mul(qst[0:64, :], qst[0:64, :], colbuf[0:64, 1:2])
        for tbl in (qct, qst, kct, kst):
            ph.V.tensor_copy(tbl[64:128, :], tbl[0:64, :])
        ph.V.tensor_copy(colbuf[0:32, 3:4], colbuf[32:64, 2:3])
        ph.V.tensor_copy(colbuf[32:64, 3:4], colbuf[0:32, 2:3])
        ph.V.tensor_scalar_mul(kct[0:64, :], kct[0:64, :], colbuf[0:64, 2:3])
        ph.V.tensor_scalar_mul(kst[0:64, :], kst[0:64, :], colbuf[0:64, 3:4])
        ph.V.tensor_copy(kct[64:128, :], kct[0:64, :])
        ph.V.tensor_copy(kst[64:128, :], kst[0:64, :])
        emit_shift_cast(p, ph, sb, y[:, :], xs[:, :])
        ph.emit()
        banks.reset()
        if stop < 2:
            break

        ph = Ph(p, "a%d_pairs" % li)
        state["ptr_cv"] = 0; state["P_free"] = 0; state["ms_free"] = 0; state["PT_free"] = 0

        def pair_dma(hp, slot):
            return wdma(ph, p, wbuf[:, slot * 3072:(slot + 1) * 3072],
                        wqkv[li, hp], sem="sW%d" % slot)

        wq_ct = {0: pair_dma(0, 0), 1: pair_dma(1, 1)}
        for hp in range(npairs):
            slot = hp % 2
            w4 = wbuf[:, slot * 3072:(slot + 1) * 3072].rearrange(
                "p (k n) -> p k n", n=384)
            ph.T.wait_ge(p.sem("sW%d" % slot), wq_ct[hp])
            evac_jobs = []
            for half in range(3):
                for tt in range(NTT):
                    bi, bk = banks.take(ph, p)
                    for kc in range(NCH):
                        lt = w4[:, kc, half * 128:(half + 1) * 128]
                        op = ph.T.matmul(bk[:, 0:TT], lt,
                                         xs[:, kc * T + tt * TT: kc * T + (tt + 1) * TT],
                                         start=(kc == 0), stop=(kc == NCH - 1))
                    ct = p.bump("sT")
                    op.then_inc(sT, 1)
                    sl = slice(tt * TT, (tt + 1) * TT)
                    ph.V.wait_ge(sT, ct)
                    if half == 0:
                        ph.V.tensor_copy(qAB[0:64, sl], bk[0:64, 0:TT])
                        ph.V.tensor_copy(kAB[0:64, sl], bk[64:128, 0:TT])
                    elif half == 1:
                        ph.V.tensor_copy(qAB[64:128, sl], bk[0:64, 0:TT])
                        ph.V.tensor_copy(kAB[64:128, sl], bk[64:128, 0:TT])
                    else:
                        ph.V.tensor_copy(vTf[:, sl], bk[:, 0:TT])
                    cv = p.bump("sV")
                    ph.V.ops[-1].then_inc(sV, 1)
                    banks.release(bi, "sV", cv)
            mm_done = p.cnt["sT"]
            qkv_evac = p.cnt["sV"]
            if dbg == "qkv" and hp == 0:
                ph.G.wait_ge(sV, qkv_evac)
                for i, src in enumerate((qAB, kAB, vTf)):
                    wdma(ph, p, dbgd[i], src)
            if hp + 2 < 8:
                ph.G.wait_ge(sT, mm_done)
                wq_ct[hp + 2] = pair_dma(hp + 2, slot)
            if pair_level < 2:
                continue

            for grp in range(2):
                if state["ptr_cv"]:
                    ph.T.wait_ge(sV, state["ptr_cv"])
                ph.T.wait_ge(sV, qkv_evac)
                for t8 in range(8):
                    ti = grp * 8 + t8
                    op = ph.T.transpose(ptr[:, t8 * 128:(t8 + 1) * 128],
                                        vTf[:, ti * 128:(ti + 1) * 128], ident[:, :])
                ct = p.bump("sT")
                op.then_inc(sT, 1)
                ph.V.wait_ge(sT, ct)
                for t8 in range(8):
                    ti = grp * 8 + t8
                    ph.V.tensor_copy(vtok[:, ti * 128:(ti + 1) * 128],
                                     ptr[:, t8 * 128:(t8 + 1) * 128])
                cv = p.bump("sV")
                ph.V.ops[-1].then_inc(sV, 1)
                state["ptr_cv"] = cv

            if pair_level < 3:
                continue
            for (src, dst, ctab, stab, bsel) in (
                    (qAB, qr, qct, qst, bsel8),
                    (kAB, kr, kct, kst, bsel1)):
                ph.S.wait_ge(sV, max(qkv_evac, p.cnt["sV"]))
                cs = p.bump("sS")
                ph.S.activation(sqp[:, :], src[:, :], AF.Square).then_inc(sS, 1)
                ph.T.wait_ge(sS, cs)
                nb = []
                for tt in range(NTT):
                    bi, bk = banks.take(ph, p)
                    op = ph.T.matmul(bk[0:2, 0:TT], sel2[:, :],
                                     sqp[:, tt * TT:(tt + 1) * TT],
                                     start=True, stop=True)
                    nb.append((bi, bk))
                ct_nm = p.bump("sT")
                op.then_inc(sT, 1)
                ph.V.wait_ge(sT, ct_nm)
                for tt in range(NTT):
                    bi, bk = nb[tt]
                    cv = p.bump("sV")
                    ph.V.tensor_copy(rows[0:2, tt * TT:(tt + 1) * TT],
                                     bk[0:2, 0:TT]).then_inc(sV, 1)
                    banks.release(bi, "sV", cv)
                ph.S.wait_ge(sV, p.cnt["sV"])
                cs = p.bump("sS")
                ph.S.activation(rows[0:2, 0:T], rows[0:2, 0:T], AF.Sqrt).then_inc(sS, 1)
                ph.V.wait_ge(sS, cs)
                cvm = p.bump("sV")
                ph.V.tensor_scalar_max(rows[0:2, 0:T], rows[0:2, 0:T],
                                       1e-12).then_inc(sV, 1)
                ph.V.wait_ge(sV, cvm)
                cv = p.bump("sV")
                ph.V.reciprocal(rows[0:2, 0:T], rows[0:2, 0:T]).then_inc(sV, 1)
                ph.V.wait_ge(sV, cv)
                rn_done = cv
                ph.G.wait_ge(sT, ct_nm)
                ph.G.wait_ge(sV, p.cnt["sV"])
                for (a, b) in ((0, 32), (32, 0), (64, 96), (96, 64)):
                    ph.G.tensor_copy(swp[a:a + 32, :], src[b:b + 32, :])
                cg = p.bump("sG")
                ph.G.ops[-1].then_inc(p.sem("sG"), 1)
                ph.V.wait_ge(p.sem("sG"), cg)
                ph.V.tensor_mul(dst[:, :], src[:, :], ctab[:, :])
                ph.V.tensor_mul(swp[:, :], swp[:, :], stab[:, :])
                cvd = p.bump("sV")
                ph.V.tensor_add(swp[:, :], dst[:, :], swp[:, :]).then_inc(sV, 1)
                ph.V.wait_ge(sV, cvd)
                ph.T.wait_ge(sV, rn_done)
                for tt in range(NTT):
                    bi, bk = banks.take(ph, p)
                    sl = slice(tt * TT, (tt + 1) * TT)
                    gslot = gtmp[0:2, 0:TT]
                    if tt > 0:
                        ph.V.wait_ge(sT, p.cnt["sT"])
                    cv0 = p.bump("sV")
                    ph.V.tensor_copy(gslot, rows[0:2, sl]).then_inc(sV, 1)
                    ph.T.wait_ge(sV, cv0)
                    ct = p.bump("sT")
                    ph.T.matmul(bk[:, 0:TT], bsel[:, :], gslot,
                                start=True, stop=True).then_inc(sT, 1)
                    ph.V.wait_ge(sT, ct)
                    cv = p.bump("sV")
                    ph.V.tensor_mul(dst[:, sl], swp[:, sl], bk[:, 0:TT]).then_inc(sV, 1)
                    banks.release(bi, "sV", cv)
            stage_done = p.cnt["sV"]
            if dbg == "stage" and hp == 0:
                ph.G.wait_ge(sV, stage_done)
                wdma(ph, p, dbgd[0], qr)
                wdma(ph, p, dbgd[1], kr)
                wdma(ph, p, dbgd[2], vtok)
            if pair_level < 5:
                continue
            ph.T.wait_ge(sV, stage_done)
            P_s = (P, A(4 * T, 512))
            PT_s = (PT, A(4 * T + 512, 512))
            ms_s = (msS, af32[:, 2 * T + 512: 2 * T + 1024])
            units = [(qt, hh) for qt in ([0] if dbg == "sm" else range(16))
                     for hh in ([0] if dbg == "sm" else range(2))]
            st1 = {}
            P_free = [0, 0]
            PDEPTH = int((dbg or "x")[2:]) if (dbg or "").startswith("pd") else 1
            for i in range(len(units) + PDEPTH):
                if i < len(units):
                    qt, h = units[i]
                    w = qt // 2
                    r = qt % 2
                    if w == 0:
                        span, koff = 256, 0
                        mview = mask0[:, r * 256:(r + 1) * 256]
                    else:
                        span, koff = 512, (w - 1) * 256
                        mview = maskw[:, r * 512:(r + 1) * 512]
                    sl1 = i % 2
                    hs = slice(h * 64, (h + 1) * 64)
                    bi, bk_ = banks.take(ph, p)
                    ct1 = p.bump("sT")
                    ph.T.matmul(bk_[:, 0:span], qr[hs, qt * 128:(qt + 1) * 128],
                                kr[hs, koff:koff + span],
                                start=True, stop=True).then_inc(sT, 1)
                    ph.V.wait_ge(sT, ct1)
                    if st1.get("ms_busy%d" % sl1):
                        ph.V.wait_ge(p.sem("sS"), st1["ms_busy%d" % sl1])
                    ph.V.tensor_add(ms_s[sl1][:, 0:span], bk_[:, 0:span], mview)
                    cv1 = p.bump("sV")
                    ph.V.reduce_max(colbuf[:, 9 + 4 * sl1:10 + 4 * sl1],
                                    ms_s[sl1][:, 0:span], AX.X,
                                    negate=True).then_inc(sV, 1)
                    ph.V.wait_ge(sV, cv1)
                    banks.release(bi, "sV", cv1)
                    st1[i] = (cv1, span, koff)
                if i >= PDEPTH:
                    j = i - PDEPTH
                    qt, h = units[j]
                    sl = j % 2
                    cv1, span, koff = st1[j]
                    hs = slice(h * 64, (h + 1) * 64)
                    ph.S.wait_ge(sV, cv1)
                    if P_free[sl]:
                        ph.S.wait_ge(sT, P_free[sl])
                    cs1 = p.bump("sS")
                    ph.S.activation(P_s[sl][:, 0:span], ms_s[sl][:, 0:span], AF.Exp,
                                    bias=colbuf[:, 9 + 4 * sl:10 + 4 * sl],
                                    accum_out=colbuf[:, 10 + 4 * sl:11 + 4 * sl]
                                    ).then_inc(sS, 1)
                    st1["ms_busy%d" % sl] = cs1
                    ph.V.wait_ge(sS, cs1)
                    cvr = p.bump("sV")
                    ph.V.reciprocal(colbuf[:, 11 + 4 * sl:12 + 4 * sl],
                                    colbuf[:, 10 + 4 * sl:11 + 4 * sl]).then_inc(sV, 1)
                    ph.V.wait_ge(sV, cvr)
                    cv2 = p.bump("sV")
                    ph.V.tensor_scalar_mul(P_s[sl][:, 0:span], P_s[sl][:, 0:span],
                                           colbuf[:, 11 + 4 * sl:12 + 4 * sl]
                                           ).then_inc(sV, 1)
                    ph.T.wait_ge(sV, cv2)
                    if state["ptr_cv"]:
                        ph.T.wait_ge(sV, state["ptr_cv"])
                    nk = span // 128
                    for kc in range(nk):
                        op = ph.T.transpose(ptr[:, kc * 128:(kc + 1) * 128],
                                            P_s[sl][:, kc * 128:(kc + 1) * 128],
                                            ident[:, :])
                    ct2 = p.bump("sT")
                    op.then_inc(sT, 1)
                    P_free[sl] = ct2
                    ph.V.wait_ge(sT, ct2)
                    if st1.get("PT_free%d" % sl):
                        ph.V.wait_ge(sT, st1["PT_free%d" % sl])
                    for kc in range(nk):
                        ph.V.tensor_copy(PT_s[sl][:, kc * 128:(kc + 1) * 128],
                                         ptr[:, kc * 128:(kc + 1) * 128])
                    cv3 = p.bump("sV")
                    ph.V.ops[-1].then_inc(sV, 1)
                    state["ptr_cv"] = cv3
                    ph.T.wait_ge(sV, cv3)
                    bo_i, bo_bk = banks.take(ph, p)
                    for kc in range(nk):
                        kb = koff // 128 + kc
                        op = ph.T.matmul(bo_bk[0:64, 0:128],
                                         vtok[:, kb * 128 + h * 64: kb * 128 + h * 64 + 64],
                                         PT_s[sl][:, kc * 128:(kc + 1) * 128],
                                         start=(kc == 0), stop=(kc == nk - 1))
                    ct3 = p.bump("sT")
                    op.then_inc(sT, 1)
                    st1["PT_free%d" % sl] = ct3
                    ph.V.wait_ge(sT, ct3)
                    cv4 = p.bump("sV")
                    ph.V.tensor_copy(ao_slice(attn_out, hp, qt, h),
                                     bo_bk[0:64, 0:128]).then_inc(sV, 1)
                    banks.release(bo_i, "sV", cv4)

        if dbg == "ypre":
            ph.G.wait_ge(sV, p.cnt["sV"])
            wdma(ph, p, dbgd[0], y[:, 0:T])
            wdma(ph, p, dbgd[1], y[:, T:2 * T])
            wdma(ph, p, dbgd[2], y[:, 2 * T:3 * T])
        if dbg in ("attn", "attn1"):
            off = T if dbg == "attn1" else 0
            ph.G.wait_ge(sV, p.cnt["sV"])
            for i in range(4):
                wdma(ph, p, dbgd[i],
                     arena[:, 19456 + i * 2 * T + off: 19456 + i * 2 * T + off + T])
        ph.emit()
        banks.reset()
        if stop < 3:
            break

        ph = Ph(p, "a%d_wo" % li)
        def wo_dma(nn, slot):
            return wdma(ph, p, wbuf[:, slot * 1024:(slot + 1) * 1024],
                        wo[li, nn], sem="sW%d" % slot)
        wct = {0: wo_dma(0, 0), 1: wo_dma(1, 1)}
        if dbg == "wo":
            ph.G.wait_ge(p.sem("sW0"), wct[0])
            wdma(ph, p, dbgd[0][:, 0:1024], wbuf[:, 0:1024])
            ph.G.wait_ge(p.sem("sW1"), wct[1])
            wdma(ph, p, dbgd[1][:, 0:1024], wbuf[:, 1024:2048])
        for nn in range(NCH):
            slot = nn % 2
            w3 = wbuf[:, slot * 1024:(slot + 1) * 1024].rearrange("p (k n) -> p k n", n=128)
            ph.T.wait_ge(p.sem("sW%d" % slot), wct[nn])
            for tt in range(NTT):
                bi, bk = banks.take(ph, p)
                for kc in range(NCH):
                    op = ph.T.matmul(bk[:, 0:TT], w3[:, kc, :],
                                     attn_out[:, kc * T + tt * TT: kc * T + (tt + 1) * TT],
                                     start=(kc == 0), stop=(kc == NCH - 1))
                ct = p.bump("sT")
                op.then_inc(sT, 1)
                sl = slice(nn * T + tt * TT, nn * T + (tt + 1) * TT)
                ph.V.wait_ge(sT, ct)
                cv = p.bump("sV")
                ph.V.tensor_add(y[:, sl], y[:, sl], bk[:, 0:TT]).then_inc(sV, 1)
                banks.release(bi, "sV", cv)
            if nn + 2 < NCH:
                ph.G.wait_ge(sT, p.cnt["sT"])
                wct[nn + 2] = wo_dma(nn + 2, slot)
        ph.emit()
        banks.reset()
        if stop < 4:
            break

        ph = Ph(p, "f%d_ln" % li)
        cb1 = wdma(ph, p, colbuf[:, 32:54], b1a[li], sem="sWc")
        cb2 = wdma(ph, p, colbuf[:, 54:76], b1g[li], sem="sWc")
        cb3 = wdma(ph, p, colbuf[:, 8:16], lng[li], sem="sWc")
        cb4 = wdma(ph, p, colbuf[:, 16:24], lnb[li], sem="sWc")
        cb5 = wdma(ph, p, colbuf[:, 24:32], b2d[li], sem="sWc")
        cshift = emit_shift_cast(p, ph, sb, y[:, :], xs[:, :])
        ph.V.wait_ge(p.sem("sWc"), p.cnt["sWc"])
        ph.S.wait_ge(sV, cshift)
        ph.T.wait_ge(sV, cshift)

        def fln_apply(c, tt, mb, rb):
            sl = slice(c * T + tt * TT, c * T + (tt + 1) * TT)
            ph.V.tensor_sub(msS[:, 0:TT], xs[:, sl], mb)
            ph.V.tensor_mul(msS[:, 0:TT], msS[:, 0:TT], rb)
            ph.V.tensor_scalar(xs[:, sl], msS[:, 0:TT], colbuf[:, 8 + c:9 + c],
                               colbuf[:, 16 + c:17 + c], ALU.mult, ALU.add)
        emit_ln(p, ph, banks, sb, lambda c: xs[:, c * T:(c + 1) * T],
                A(16384, NCH * T), rows, fln_apply, m2row=af32[0:1, 16384:16896])
        if dbg == "fln":
            ph.G.wait_ge(sV, p.cnt["sV"])
            for i in range(4):
                wdma(ph, p, dbgd[i], xs[:, i * T:(i + 1) * T])
        if dbg == "ytrace":
            ph.G.wait_ge(sV, p.cnt["sV"])
            wdma(ph, p, dbgd[1], y[:, 0:T])
        ph.emit()
        banks.reset()
        if stop < 5:
            break

        ph = Ph(p, "f%d_j" % li)
        NJ = 22

        def w1_dma(j, slot):
            wdma(ph, p, wbuf[:, slot * 2048:slot * 2048 + 1024],
                 w1a[li, j], sem="sW%d" % slot)
            return wdma(ph, p, wbuf[:, slot * 2048 + 1024:(slot + 1) * 2048],
                        w1g[li, j], sem="sW%d" % slot)

        w1ct = {0: w1_dma(0, 0), 1: w1_dma(1, 1)}
        gslot_cv = [0, 0]
        fjl = 9
        if dbg and dbg.startswith("fj") and len(dbg) == 3:
            fjl = int(dbg[2])
        for j in range(NJ):
            slot = j % 2
            wa = wbuf[:, slot * 2048:slot * 2048 + 1024].rearrange("p (k n) -> p k n", n=128)
            wg = wbuf[:, slot * 2048 + 1024:(slot + 1) * 2048].rearrange("p (k n) -> p k n", n=128)
            ph.T.wait_ge(p.sem("sW%d" % slot), w1ct[j])
            for tt in range(NTT):
                if fjl < 2:
                    continue
                bia, bka = banks.take(ph, p)
                for kc in range(NCH):
                    ph.T.matmul(bka[:, 0:TT], wa[:, kc, :],
                                xs[:, kc * T + tt * TT: kc * T + (tt + 1) * TT],
                                start=(kc == 0), stop=(kc == NCH - 1))
                big, bkg = banks.take(ph, p)
                for kc in range(NCH):
                    op = ph.T.matmul(bkg[:, 0:TT], wg[:, kc, :],
                                     xs[:, kc * T + tt * TT: kc * T + (tt + 1) * TT],
                                     start=(kc == 0), stop=(kc == NCH - 1))
                ct = p.bump("sT")
                op.then_inc(sT, 1)
                gs = 0
                if fjl < 3:
                    cv = p.bump("sV")
                    ph.V.wait_ge(sT, ct)
                    ph.V.tensor_copy(arena[:, j * T + tt * TT: j * T + (tt + 1) * TT],
                                     bka[:, 0:TT]).then_inc(sV, 1)
                    banks.release(bia, "sV", cv)
                    banks.release(big, "sV", cv)
                    continue
                ph.S.wait_ge(sT, ct)
                if gslot_cv[gs]:
                    ph.S.wait_ge(sV, gslot_cv[gs])
                cs = p.bump("sS")
                ph.S.activation(gtmp[:, gs * TT:(gs + 1) * TT], bkg[:, 0:TT], AF.Gelu,
                                bias=colbuf[:, 54 + j:55 + j]).then_inc(sS, 1)
                banks.release(big, "sS", cs)
                ph.V.wait_ge(sS, cs)
                cv = p.bump("sV")
                if fjl < 4:
                    ph.V.tensor_copy(arena[:, j * T + tt * TT: j * T + (tt + 1) * TT],
                                     gtmp[:, gs * TT:(gs + 1) * TT]).then_inc(sV, 1)
                else:
                    ph.V.scalar_tensor_tensor(
                        arena[:, j * T + tt * TT: j * T + (tt + 1) * TT],
                        bka[:, 0:TT], colbuf[:, 32 + j:33 + j],
                        gtmp[:, gs * TT:(gs + 1) * TT],
                        ALU.add, ALU.mult).then_inc(sV, 1)
                banks.release(bia, "sV", cv)
                gslot_cv[gs] = cv
            if j + 2 < NJ:
                ph.G.wait_ge(sT, p.cnt["sT"])
                w1ct[j + 2] = w1_dma(j + 2, slot)
        if dbg == "fj":
            ph.G.wait_ge(sV, p.cnt["sV"])
            for i in range(4):
                wdma(ph, p, dbgd[i], arena[:, i * T:(i + 1) * T])
        if dbg == "ytrace" or (dbg and dbg.startswith("fj") and len(dbg) == 3):
            ph.G.wait_ge(sV, p.cnt["sV"])
            ph.G.wait_ge(sT, p.cnt["sT"])
            wdma(ph, p, dbgd[2], y[:, 0:T])
        ph.emit()
        banks.reset()
        if stop < 6:
            break

        ph = Ph(p, "f%d_w2" % li)

        def w2_dma(nn, slot):
            return wdma(ph, p, wbuf[:, slot * 2816:(slot + 1) * 2816],
                        w2[li, nn], sem="sW%d" % slot)

        w2ct = {0: w2_dma(0, 0), 1: w2_dma(1, 1)}
        for nn in range(NCH):
            slot = nn % 2
            wv = wbuf[:, slot * 2816:(slot + 1) * 2816].rearrange("p (k n) -> p k n", n=128)
            ph.T.wait_ge(p.sem("sW%d" % slot), w2ct[nn])
            for tt in range(NTT):
                bi, bk = banks.take(ph, p)
                for kc in range(NJ):
                    op = ph.T.matmul(bk[:, 0:TT], wv[:, kc, :],
                                     arena[:, kc * T + tt * TT: kc * T + (tt + 1) * TT],
                                     start=(kc == 0), stop=(kc == NJ - 1))
                ct = p.bump("sT")
                op.then_inc(sT, 1)
                sl = slice(nn * T + tt * TT, nn * T + (tt + 1) * TT)
                ph.V.wait_ge(sT, ct)
                cv = p.bump("sV")
                ph.V.scalar_tensor_tensor(y[:, sl], bk[:, 0:TT],
                                          colbuf[:, 24 + nn:25 + nn], y[:, sl],
                                          ALU.add, ALU.add).then_inc(sV, 1)
                banks.release(bi, "sV", cv)
            if nn + 2 < NCH:
                ph.G.wait_ge(sT, p.cnt["sT"])
                w2ct[nn + 2] = w2_dma(nn + 2, slot)
        ph.emit()
        banks.reset()

    ph = Ph(p, "final")
    xi = xin.rearrange("(c p) t -> p c t", p=128)
    xo = xout.rearrange("(c p) t -> p c t", p=128)
    xtmp = af32[:, 0:2 * T]
    out_c = af32[:, 2 * T:4 * T]
    st_cnt = {}
    for c in range(NCH):
        s2 = (c % 2) * T
        cx = iodma(ph, p, xtmp[:, s2:s2 + T], xi[:, c, :], sem="sXin")
        ph.V.wait_ge(p.sem("sXin"), cx)
        if c >= 2:
            ph.V.wait_ge(p.sem("sXout"), st_cnt[c - 1])
        cv = p.bump("sV")
        ph.V.scalar_tensor_tensor(out_c[:, s2:s2 + T], y[:, c * T:(c + 1) * T], 0.5,
                                  xtmp[:, s2:s2 + T], ALU.mult, ALU.add).then_inc(sV, 1)
        ph.G.wait_ge(sV, cv)
        if dbg == "fin" and c == 0:
            wdma(ph, p, dbgd[0], xtmp[:, 0:T])
            wdma(ph, p, dbgd[1], y[:, 0:T])
            wdma(ph, p, dbgd[2], out_c[:, 0:T])
        st_cnt[c] = iodma(ph, p, xo[:, c, :], out_c[:, s2:s2 + T], sem="sXout")
    ph.G.wait_ge(p.sem("sXout"), p.cnt["sXout"])
    ph.emit()

    for h in reversed(ctx):
        h.__exit__(None, None, None)
    if finalize:
        nc.finalize()
    return nc



def _col6(a, nj):
    a = np.asarray(a, np.float32)
    return np.ascontiguousarray(a.reshape(6, nj, 128).transpose(0, 2, 1))


def prep_l0ffn_weights(ins):
    f32 = np.float32
    W1 = np.asarray(ins["L0_ffn_W1"], f32)
    b1 = np.asarray(ins["L0_ffn_b1"], f32)
    W2 = np.asarray(ins["L0_ffn_W2"], f32)
    w1a = np.zeros((6, D, 2816), f32); w1a[:, :, :2730] = W1[:, :, :2730]
    w1g = np.zeros((6, D, 2816), f32); w1g[:, :, :2730] = W1[:, :, 2730:]
    b1a = np.zeros((6, 2816), f32); b1a[:, :2730] = b1[:, :2730]
    b1g = np.zeros((6, 2816), f32); b1g[:, :2730] = b1[:, 2730:]
    w2p = np.zeros((6, 2816, D), f32); w2p[:, :2730] = W2
    con = host_constants()
    Wqkv = np.asarray(ins["L0_ffn_Wqkv"], f32)
    q_ = Wqkv[:, :, 0:1024].reshape(6, 8, 128, 16, 64)
    k_ = Wqkv[:, :, 1024:2048].reshape(6, 8, 128, 16, 64)
    v_ = Wqkv[:, :, 2048:3072].reshape(6, 8, 128, 16, 64)
    wq_t = np.empty((6, 8, 128, 8, 384), f32)
    for hp in range(8):
        wq_t[:, hp, :, :, 0:64] = q_[:, :, :, 2 * hp].transpose(0, 2, 1, 3)
        wq_t[:, hp, :, :, 64:128] = k_[:, :, :, 2 * hp].transpose(0, 2, 1, 3)
        wq_t[:, hp, :, :, 128:192] = q_[:, :, :, 2 * hp + 1].transpose(0, 2, 1, 3)
        wq_t[:, hp, :, :, 192:256] = k_[:, :, :, 2 * hp + 1].transpose(0, 2, 1, 3)
        wq_t[:, hp, :, :, 256:320] = v_[:, :, :, 2 * hp].transpose(0, 2, 1, 3)
        wq_t[:, hp, :, :, 320:384] = v_[:, :, :, 2 * hp + 1].transpose(0, 2, 1, 3)
    Wo_ = np.asarray(ins["L0_ffn_Wo"], f32)
    wo_t = np.ascontiguousarray(
        Wo_.reshape(6, 8, 128, 8, 128).transpose(0, 3, 2, 1, 4)).reshape(6, 8, 128, 1024)
    w1a_t = np.ascontiguousarray(
        w1a.reshape(6, 8, 128, 22, 128).transpose(0, 3, 2, 1, 4)).reshape(6, 22, 128, 1024)
    w1g_t = np.ascontiguousarray(
        w1g.reshape(6, 8, 128, 22, 128).transpose(0, 3, 2, 1, 4)).reshape(6, 22, 128, 1024)
    w2_t = np.ascontiguousarray(
        w2p.reshape(6, 22, 128, 8, 128).transpose(0, 3, 2, 1, 4)).reshape(6, 8, 128, 2816)
    bf = ml_dtypes.bfloat16
    d = dict(
        wqkv=np.ascontiguousarray(wq_t).reshape(6, 8, 128, 3072).astype(bf),
        wo=wo_t.astype(bf),
        w1a=w1a_t.astype(bf), w1g=w1g_t.astype(bf), w2=w2_t.astype(bf),
        b1a=_col6(b1a, 22), b1g=_col6(b1g, 22),
        lng=_col6(ins["L0_ffn_ln_g"], 8), lnb=_col6(ins["L0_ffn_ln_b"], 8),
        b2d=_col6(ins["L0_ffn_b2"], 8),
        qsd=np.ascontiguousarray(np.asarray(ins["L0_ffn_q_scale"], f32).reshape(6, 64, 1)),
        ksd=np.ascontiguousarray(np.asarray(ins["L0_ffn_k_scale"], f32).reshape(6, 64, 1)),
        c_ident=con["c_ident"],
        c_mask_w=np.ascontiguousarray(
            con["c_mask_w"].reshape(2, 128, 512).transpose(1, 0, 2).reshape(128, 1024)),
        c_mask_w0=np.ascontiguousarray(
            con["c_mask_w0"].reshape(2, 128, 256).transpose(1, 0, 2).reshape(128, 512)),
        c_rot_qc=con["c_rot_qc"], c_rot_qs=con["c_rot_qs"],
        c_rot_kc=con["c_rot_kc"], c_rot_ks=con["c_rot_ks"],
        c_bsel8=con["c_bsel8"], c_bsel1=con["c_bsel1"],
    )
    return d


def emit_ln_stream(p, ph, banks, sb, xi_r, rows, xt_slots, cbuf, sq4, ln_out,
                   m2row, extra_fn=None, ln_sem="sXin", fp32_stats=False,
                   sq4f=None, onesc_f=None):
    sV, sS, sT = p.sem("sV"), p.sem("sS"), p.sem("sT")
    single = xt_slots[0] is xt_slots[1]
    prev_cv = 0
    for tt in range(NTT):
        slot = 0 if single else tt % 2
        xt = xt_slots[slot]
        if single and prev_cv:
            ph.G.wait_ge(sV, prev_cv)
        cx = iodma(ph, p, xt[:, :].rearrange("p (c t) -> p c t", t=TT),
                   xi_r[:, :, tt * TT:(tt + 1) * TT], sem=ln_sem + str(slot))
        ph.V.wait_ge(p.sem(ln_sem + str(slot)), cx)
        cv = p.bump("sV")
        ph.V.tensor_copy(cbuf[:, :], xt[:, :]).then_inc(sV, 1)
        ph.S.wait_ge(sV, cv)
        iS, bS = banks.take(ph, p)
        iQ, bQ = banks.take(ph, p)
        if fp32_stats:
            for c in range(NCH):
                if c > 0:
                    ph.S.wait_ge(sT, p.cnt["sT"])
                cs = p.bump("sS")
                ph.S.activation(sq4f[:, 0:TT], xt[:, c * TT:(c + 1) * TT],
                                AF.Square).then_inc(sS, 1)
                ph.T.wait_ge(sS, cs)
                ph.T.matmul(bS[0:1, 0:TT], onesc_f[:, 0:1],
                            xt[:, c * TT:(c + 1) * TT],
                            start=(c == 0), stop=(c == NCH - 1))
                ctq = p.bump("sT")
                ph.T.matmul(bQ[0:1, 0:TT], onesc_f[:, 0:1],
                            sq4f[:, 0:TT],
                            start=(c == 0), stop=(c == NCH - 1)).then_inc(sT, 1)
            op = ph.T.nop()
            ct = ctq
        else:
            cs = p.bump("sS")
            ph.S.activation(sq4[:, :], cbuf[:, :], AF.Square).then_inc(sS, 1)
            ph.T.wait_ge(sS, cs)
            for c in range(NCH):
                ph.T.matmul(bS[0:1, 0:TT], sb["ones_bf"][:, 0:1],
                            cbuf[:, c * TT:(c + 1) * TT],
                            start=(c == 0), stop=(c == NCH - 1))
            for c in range(NCH):
                op = ph.T.matmul(bQ[0:1, 0:TT], sb["ones_bf"][:, 0:1],
                                 sq4[:, c * TT:(c + 1) * TT],
                                 start=(c == 0), stop=(c == NCH - 1))
            ct = p.bump("sT")
            op.then_inc(sT, 1)
        ph.V.wait_ge(sT, ct)
        mrow = rows[0:1, 0:TT]
        vrow = rows[0:1, TT:2 * TT]
        ph.V.tensor_scalar_mul(mrow, bS[0:1, 0:TT], 1.0 / D)
        cv = p.bump("sV")
        ph.V.tensor_scalar_mul(vrow, bQ[0:1, 0:TT], 1.0 / D).then_inc(sV, 1)
        ph.V.wait_ge(sV, cv)
        banks.release(iS, "sV", cv)
        banks.release(iQ, "sV", cv)
        ph.V.tensor_mul(m2row, mrow, mrow)
        cv = p.bump("sV")
        ph.V.scalar_tensor_tensor(vrow, vrow, EPS_LN, m2row,
                                  ALU.add, ALU.subtract).then_inc(sV, 1)
        ph.S.wait_ge(sV, cv)
        cs = p.bump("sS")
        ph.S.activation(vrow, vrow, AF.Sqrt).then_inc(sS, 1)
        ph.V.wait_ge(sS, cs)
        cv = p.bump("sV")
        ph.V.reciprocal(vrow, vrow).then_inc(sV, 1)
        ph.V.wait_ge(sV, cv)
        ph.T.wait_ge(sV, cv)
        im, bm, ctm = bcast_row(ph, p, banks, mrow, sb)
        ir, br, ctr = bcast_row(ph, p, banks, vrow, sb)
        ph.V.wait_ge(sT, ctr)
        for c in range(NCH):
            sl = slice(c * T + tt * TT, c * T + (tt + 1) * TT)
            ph.V.tensor_sub(sq4[:, c * TT:(c + 1) * TT], xt[:, c * TT:(c + 1) * TT],
                            bm[:, 0:TT])
            ph.V.tensor_mul(ln_out[:, sl], sq4[:, c * TT:(c + 1) * TT], br[:, 0:TT])
        cv = p.bump("sV")
        ph.V.ops[-1].then_inc(sV, 1)
        banks.release(im, "sV", cv)
        banks.release(ir, "sV", cv)
        if extra_fn is not None:
            extra_fn(tt, xt, mrow, vrow)
        prev_cv = p.cnt["sV"]


def build_l0attn(dbg=None,
                 nc=None, p=None, seg="", xin_ap=None, xout_ap=None, finalize=True):
    if nc is None:
        nc = bass.Bass("TRN2", target_bir_lowering=False)
        p = Prog(nc)
    HD = 64
    p.blkseg = seg
    xin = xin_ap if xin_ap is not None else nc.dram_tensor(
        "xin" + seg, [D, T], F32, kind="ExternalInput").ap()
    xout = xout_ap if xout_ap is not None else nc.dram_tensor(
        "xout" + seg, [D, T], F32, kind="ExternalOutput").ap()
    wqkv = nc.dram_tensor("wqkv" + seg, [16, 128, 1536], BF16, kind="ExternalInput").ap()
    bqkv = nc.dram_tensor("bqkv" + seg, [16, 64, 4], F32, kind="ExternalInput").ap()
    wod = nc.dram_tensor("wod" + seg, [8, 128, 1024], BF16, kind="ExternalInput").ap()
    bod = nc.dram_tensor("bod" + seg, [128, 8], F32, kind="ExternalInput").ap()
    identd = nc.dram_tensor("c_ident" + seg, [128, 128], BF16, kind="ExternalInput").ap()
    dbgd = [nc.dram_tensor("dbg%d" % i + seg, [128, T], F32, kind="ExternalOutput").ap()
            for i in range(4)] if dbg else None

    ctx = []
    def alloc(name, shape, dt, psum=False):
        name = name + seg
        h = nc.psum_tensor(name, shape, dt) if psum else nc.sbuf_tensor(name, shape, dt)
        t = h.__enter__()
        ctx.append(h)
        return t

    ln = alloc("ln", [128, NCH * T], BF16)
    PS = alloc("PS", [128, 16 * T], FP16)
    dist32 = alloc("dist32", [128, T], F32)
    qst = alloc("qst", [128, T], BF16)
    kst = alloc("kst", [128, T], BF16)
    sq = alloc("sq", [128, T], BF16)
    vtok = alloc("vtok", [128, 1024], BF16)
    vp65 = alloc("vp65", [128, 16 * 72], BF16)
    ao = alloc("ao", [128, NCH * T], BF16)
    whbuf = alloc("whbuf", [128, 3072], BF16)
    wobuf = alloc("wobuf", [128, 2048], BF16)
    xta = alloc("xta", [128, NCH * TT], F32)
    cbuf = alloc("cbuf", [128, NCH * TT], BF16)
    sq4 = alloc("sq4", [128, NCH * TT], BF16)
    rows = alloc("rows", [128, 1024], F32)
    msS = alloc("msS", [128, 512], F32)
    colbuf = alloc("colbuf", [128, 32], F32)
    ident = alloc("ident", [128, 128], BF16)
    ones_bf = alloc("ones_bf", [128, 1], BF16)
    ones_f = alloc("ones_f", [1, 128], F32)
    qcol = alloc("qcol", [128, 1], BF16)
    onesr = alloc("onesr", [1, 64], BF16)
    epsc = alloc("epsc", [128, 1], F32)
    rrow = alloc("rrow", [1, 512], F32)
    rrow_bf = alloc("rrow_bf", [1, 512], BF16)
    outsl = alloc("outsl", [128, 512], F32)
    pb = [alloc("pb%d" % i, [128, TT], F32, psum=True) for i in range(7)]
    ptr = alloc("ptr", [128, 1024], BF16, psum=True)
    banks = Banks(pb)
    sb = dict(ones_bf=ones_bf, ones_f=ones_f)
    sV, sS, sT, sX = (p.sem(n) for n in ("sV", "sS", "sT", "sX"))
    xi_r = xin.rearrange("(c p) t -> p c t", p=128)
    xo_r = xout.rearrange("(c p) t -> p c t", p=128)

    ph = Ph(p, "init")
    ph.G.memset(ones_bf[:, :], 1.0)
    ph.G.memset(ones_f[:, :], 1.0)
    ph.G.memset(qcol[:, :], 0.25)
    ph.G.memset(onesr[:, :], 1.0)
    ph.G.memset(epsc[:, :], 1e-12)
    ph.G.memset(kst[64:65, :], 1.0)
    c1 = iodma(ph, p, ident[:, :], identd)
    ph.G.wait_ge(sX, p.cnt["sX"])
    ph.emit()

    ph = Ph(p, "ln")
    emit_ln_stream(p, ph, banks, sb, xi_r, rows, [xta, xta], cbuf, sq4, ln,
                   msS[0:1, 0:TT])
    ph.emit()
    banks.reset()

    ph = Ph(p, "heads")
    prev_bcast = [0]
    def wh_dma(h, slot):
        wdma(ph, p, whbuf[:, slot * 1536:(slot + 1) * 1536], wqkv[h],
             sem="sW%d" % slot)
        return wdma(ph, p, colbuf[0:64, slot * 4:(slot + 1) * 4], bqkv[h],
                    sem="sW%d" % slot)
    whct = {0: wh_dma(0, 0), 1: wh_dma(1, 1)}
    ptr_cv = 0
    for h in range(16):
        slot = h % 2
        w3 = whbuf[:, slot * 1536:(slot + 1) * 1536].rearrange("p (k n) -> p k n", n=192)
        bq_c = colbuf[0:64, slot * 4 + 0: slot * 4 + 1]
        bk_c = colbuf[0:64, slot * 4 + 1: slot * 4 + 2]
        bv_c = colbuf[0:64, slot * 4 + 2: slot * 4 + 3]
        ph.T.wait_ge(p.sem("sW%d" % slot), whct[h])
        for w in range(3):
            for tt in range(NTT):
                bi, bk_ = banks.take(ph, p)
                for kc in range(NCH):
                    op = ph.T.matmul(bk_[0:64, 0:TT], w3[:, kc, w * 64:(w + 1) * 64],
                                     ln[:, kc * T + tt * TT: kc * T + (tt + 1) * TT],
                                     start=(kc == 0), stop=(kc == NCH - 1))
                ct = p.bump("sT")
                op.then_inc(sT, 1)
                sl = slice(tt * TT, (tt + 1) * TT)
                ph.V.wait_ge(sT, ct)
                cv = p.bump("sV")
                if w == 0:
                    ph.V.tensor_scalar_add(qst[0:64, sl], bk_[0:64, 0:TT],
                                           bq_c).then_inc(sV, 1)
                elif w == 1:
                    ph.V.tensor_scalar(kst[0:64, sl], bk_[0:64, 0:TT], bk_c, -2.0,
                                       ALU.add, ALU.mult).then_inc(sV, 1)
                else:
                    ph.V.tensor_scalar_add(sq[0:64, sl], bk_[0:64, 0:TT],
                                           bv_c).then_inc(sV, 1)
                banks.release(bi, "sV", cv)
        qkv_done = p.cnt["sV"]
        mm_done = p.cnt["sT"]
        if h + 2 < 16:
            ph.G.wait_ge(sT, mm_done)
            ph.G.wait_ge(sV, qkv_done)
            whct[h + 2] = wh_dma(h + 2, slot)
        for grp in range(2):
            if ptr_cv:
                ph.T.wait_ge(sV, ptr_cv)
            ph.T.wait_ge(sV, qkv_done)
            for t8 in range(8):
                ti = grp * 8 + t8
                op = ph.T.transpose(ptr[:, t8 * 128:t8 * 128 + 64],
                                    sq[0:64, ti * 128:(ti + 1) * 128],
                                    ident[0:64, 0:64])
            ct = p.bump("sT")
            op.then_inc(sT, 1)
            ph.V.wait_ge(sT, ct)
            for t8 in range(8):
                ti = grp * 8 + t8
                ph.V.tensor_copy(vtok[:, ti * 64:(ti + 1) * 64],
                                 ptr[:, t8 * 128:t8 * 128 + 64])
            cv = p.bump("sV")
            ph.V.ops[-1].then_inc(sV, 1)
            ptr_cv = cv
        ph.S.wait_ge(sV, qkv_done)
        ph.S.wait_ge(sT, p.cnt["sT"])
        cs = p.bump("sS")
        ph.S.activation(sq[0:64, :], qst[0:64, :], AF.Square).then_inc(sS, 1)
        ph.T.wait_ge(sS, cs)
        for tt in range(NTT):
            bi, bk_ = banks.take(ph, p)
            ct = p.bump("sT")
            ph.T.matmul(bk_[0:1, 0:TT], ones_bf[0:64, :],
                        sq[0:64, tt * TT:(tt + 1) * TT],
                        start=True, stop=True).then_inc(sT, 1)
            ph.V.wait_ge(sT, ct)
            cv = p.bump("sV")
            ph.V.tensor_copy(qst[64:65, tt * TT:(tt + 1) * TT],
                             bk_[0:1, 0:TT]).then_inc(sV, 1)
            banks.release(bi, "sV", cv)
        q2_done = p.cnt["sV"]
        cs = p.bump("sS")
        ph.S.wait_ge(sV, q2_done)
        ph.S.activation(sq[0:64, :], kst[0:64, :], AF.Square).then_inc(sS, 1)
        ph.T.wait_ge(sS, cs)
        for tt in range(NTT):
            bi, bk_ = banks.take(ph, p)
            ct = p.bump("sT")
            ph.T.matmul(bk_[0:1, 0:TT], qcol[0:64, 0:1],
                        sq[0:64, tt * TT:(tt + 1) * TT],
                        start=True, stop=True).then_inc(sT, 1)
            cv = p.bump("sV")
            ph.V.wait_ge(sT, ct)
            ph.V.tensor_copy(sq[64:65, tt * TT:(tt + 1) * TT],
                             bk_[0:1, 0:TT]).then_inc(sV, 1)
            banks.release(bi, "sV", cv)
        cv = p.cnt["sV"]
        ph.T.wait_ge(sV, cv)
        bi, bk_ = banks.take(ph, p)
        for jt in range(16):
            op = ph.T.matmul(bk_[:, jt:jt + 1], sq[64:65, jt * 128:(jt + 1) * 128],
                             kst[64:65, 0:1], start=True, stop=True)
        ct = p.bump("sT")
        op.then_inc(sT, 1)
        ph.V.wait_ge(sT, ct)
        cv = p.bump("sV")
        ph.V.tensor_copy(colbuf[:, 8:24], bk_[:, 0:16]).then_inc(sV, 1)
        banks.release(bi, "sV", cv)
        k2c_done = cv
        ph.S.wait_ge(sV, k2c_done)
        for jt in range(16):
            for it in range(NTT):
                bi, bk_ = banks.take(ph, p)
                ct = p.bump("sT")
                ph.T.matmul(bk_[:, 0:TT], kst[0:65, jt * 128:(jt + 1) * 128],
                            qst[0:65, it * TT:(it + 1) * TT],
                            start=True, stop=True).then_inc(sT, 1)
                ph.S.wait_ge(sT, ct)
                ph.S.activation(msS[:, 0:TT], bk_[:, 0:TT], AF.Relu,
                                bias=colbuf[:, 8 + jt:9 + jt])
                cs = p.bump("sS")
                ph.S.activation(PS[:, jt * T + it * TT: jt * T + (it + 1) * TT],
                                msS[:, 0:TT], AF.Sqrt, bias=epsc[:, 0:1]).then_inc(sS, 1)
                banks.release(bi, "sS", cs)
        PSb = PS[:, :].bitcast(BF16)
        for jt in range(16):
            ph.S.activation(dist32[:, :], PS[:, jt * T:(jt + 1) * T],
                            AF.Ln, bias=1.0, scale=1.0 / 64.0)
            cs = p.bump("sS")
            ph.S.activation(PSb[:, jt * T:(jt + 1) * T], dist32[:, :],
                            AF.Exp, scale=-65.5,
                            accum_out=colbuf[:, 24 + jt % 2:25 + jt % 2]).then_inc(sS, 1)
            ph.V.wait_ge(sS, cs)
            cvr = p.bump("sV")
            ph.V.reciprocal(colbuf[:, 26:27], colbuf[:, 24 + jt % 2:25 + jt % 2]).then_inc(sV, 1)
            ph.V.wait_ge(sV, cvr)
            cs2 = p.bump("sS")
            ph.S.wait_ge(sV, cvr)
            ph.S.activation(colbuf[:, 27:28], colbuf[:, 26:27], AF.Sqrt).then_inc(sS, 1)
            ph.V.wait_ge(sS, cs2)
            cv = p.bump("sV")
            ph.V.tensor_scalar_mul(vp65[:, jt * 72: jt * 72 + 64],
                                   vtok[:, jt * 64:(jt + 1) * 64],
                                   colbuf[:, 27:28]).then_inc(sV, 1)
            cv = p.bump("sV")
            ph.V.tensor_copy(vp65[:, jt * 72 + 64: jt * 72 + 65],
                             colbuf[:, 27:28]).then_inc(sV, 1)
        vp_done = p.cnt["sV"]
        if dbg == "h0b" and h == 0:
            ph.G.wait_ge(sV, vp_done)
            wdma(ph, p, dbgd[0][:, 0:1152], vp65[:, :])
            wdma(ph, p, dbgd[1][:, 0:1024], vtok[:, :])
        if dbg == "h0" and h == 0:
            ph.G.wait_ge(sV, vp_done)
            ph.G.wait_ge(p.sem("sS"), p.cnt["sS"])
            wdma(ph, p, dbgd[0], PSb[:, 0:T])
            wdma(ph, p, dbgd[1], PSb[:, 8 * T:9 * T])
            wdma(ph, p, dbgd[2], qst[:, :])
            wdma(ph, p, dbgd[3], kst[:, :])
        ph.T.wait_ge(sV, vp_done)
        for it in range(NTT):
            bi, bk_ = banks.take(ph, p)
            for jt in range(16):
                op = ph.T.matmul(bk_[0:65, 0:TT], vp65[:, jt * 72: jt * 72 + 65],
                                 PSb[:, jt * T + it * TT: jt * T + (it + 1) * TT],
                                 start=(jt == 0), stop=(jt == 15))
            ct = p.bump("sT")
            op.then_inc(sT, 1)
            ph.V.wait_ge(sT, ct)
            if prev_bcast[0]:
                ph.V.wait_ge(sT, prev_bcast[0])
            cvr = p.bump("sV")
            ph.V.reciprocal(rrow[0:1, 0:TT], bk_[64:65, 0:TT]).then_inc(sV, 1)
            ph.V.wait_ge(sV, cvr)
            cv = p.bump("sV")
            ph.V.tensor_copy(rrow_bf[0:1, 0:TT], rrow[0:1, 0:TT]).then_inc(sV, 1)
            ph.T.wait_ge(sV, cv)
            bi2, bk2 = banks.take(ph, p)
            ct2 = p.bump("sT")
            ph.T.matmul(bk2[0:64, 0:TT], onesr[0:1, :], rrow_bf[0:1, 0:TT],
                        start=True, stop=True).then_inc(sT, 1)
            prev_bcast[0] = ct2
            ph.V.wait_ge(sT, ct2)
            cv = p.bump("sV")
            ph.V.tensor_copy(sq[0:64, it * TT:(it + 1) * TT],
                             bk_[0:64, 0:TT]).then_inc(sV, 1)
            banks.release(bi, "sV", cv)
            ph.V.wait_ge(sV, cv)
            hs = slice((h % 2) * 64, (h % 2) * 64 + 64)
            cv = p.bump("sV")
            ph.V.tensor_mul(ao[hs, (h // 2) * T + it * TT: (h // 2) * T + (it + 1) * TT],
                            sq[0:64, it * TT:(it + 1) * TT],
                            bk2[0:64, 0:TT]).then_inc(sV, 1)
            banks.release(bi2, "sV", cv)
    if dbg in ("h0", "h0b"):
        ph.G.wait_ge(sV, p.cnt["sV"])
        wdma(ph, p, dbgd[2], ao[:, 0:T])
        wdma(ph, p, dbgd[3], ao[:, T:2 * T])
    ph.emit()
    banks.reset()

    ph = Ph(p, "wo")
    cbo = wdma(ph, p, colbuf[:, 0:8], bod, sem="sWc")
    def wo_dma(nn, slot):
        return wdma(ph, p, wobuf[:, slot * 1024:(slot + 1) * 1024], wod[nn],
                    sem="sW%d" % slot)
    wct = {0: wo_dma(0, 0), 1: wo_dma(1, 1)}
    ph.V.wait_ge(p.sem("sWc"), p.cnt["sWc"])
    st_cnt = {}
    ld_cnt = {}
    last_stt_cv = [0]
    k = 0
    for nn in range(NCH):
        slot = nn % 2
        w3 = wobuf[:, slot * 1024:(slot + 1) * 1024].rearrange("p (k n) -> p k n", n=128)
        ph.T.wait_ge(p.sem("sW%d" % slot), wct[nn])
        for tt in range(NTT):
            xslot = 0
            xt = xta
            if k >= 1:
                ph.G.wait_ge(sV, last_stt_cv[0])
            ld_cnt[k] = iodma(ph, p, xt[:, 0:TT], xi_r[:, nn, tt * TT:(tt + 1) * TT],
                              sem="sXi%d" % xslot)
            bi, bk_ = banks.take(ph, p)
            for kc in range(NCH):
                op = ph.T.matmul(bk_[:, 0:TT], w3[:, kc, :],
                                 ao[:, kc * T + tt * TT: kc * T + (tt + 1) * TT],
                                 start=(kc == 0), stop=(kc == NCH - 1))
            ct = p.bump("sT")
            op.then_inc(sT, 1)
            ph.V.wait_ge(sT, ct)
            ph.V.wait_ge(p.sem("sXi%d" % xslot), ld_cnt[k])
            if k >= 1:
                ph.V.wait_ge(p.sem("sXo"), st_cnt[k - 1])
            osl = outsl[:, 0:TT]
            cv = p.bump("sV")
            ph.V.scalar_tensor_tensor(osl, bk_[:, 0:TT], colbuf[:, nn:nn + 1],
                                      xt[:, 0:TT], ALU.add, ALU.add).then_inc(sV, 1)
            last_stt_cv[0] = cv
            banks.release(bi, "sV", cv)
            ph.G.wait_ge(sV, cv)
            st_cnt[k] = iodma(ph, p, xo_r[:, nn, tt * TT:(tt + 1) * TT], osl,
                              sem="sXo")
            k += 1
        if nn + 2 < NCH:
            ph.G.wait_ge(sT, p.cnt["sT"])
            wct[nn + 2] = wo_dma(nn + 2, slot)
    ph.G.wait_ge(p.sem("sXo"), p.cnt["sXo"])
    ph.emit()
    for h in reversed(ctx):
        h.__exit__(None, None, None)
    if finalize:
        nc.finalize()
    return nc


def prep_l0attn_weights(ins):
    f32 = np.float32
    WQ = np.asarray(ins["L0_attn_WQ"], f32)
    WK = np.asarray(ins["L0_attn_WK"], f32)
    WV = np.asarray(ins["L0_attn_WV"], f32)
    WO = np.asarray(ins["L0_attn_WO"], f32)
    bQ = np.asarray(ins["L0_attn_bQ"], f32)
    bK = np.asarray(ins["L0_attn_bK"], f32)
    bV = np.asarray(ins["L0_attn_bV"], f32)
    bO = np.asarray(ins["L0_attn_bO"], f32)
    wq_t = np.empty((16, 128, 8, 3, 64), f32)
    for h in range(16):
        for wi, W in enumerate((WQ, WK, WV)):
            wq_t[h, :, :, wi, :] = W[:, h * 64:(h + 1) * 64].reshape(8, 128, 64).transpose(1, 0, 2)
    bq_t = np.stack([np.stack([bQ.reshape(16, 64)[h], bK.reshape(16, 64)[h],
                               bV.reshape(16, 64)[h], np.zeros(64, f32)], axis=1)
                     for h in range(16)])
    wo_t = np.ascontiguousarray(
        WO.reshape(8, 128, 8, 128).transpose(2, 1, 0, 3)).reshape(8, 128, 1024)
    con = host_constants()
    bf = ml_dtypes.bfloat16
    return dict(
        wqkv=np.ascontiguousarray(wq_t).reshape(16, 128, 1536).astype(bf),
        bqkv=np.ascontiguousarray(bq_t),
        wod=wo_t.astype(bf),
        bod=np.ascontiguousarray(bO.reshape(8, 128).T),
        c_ident=con["c_ident"],
    )


def build_l1attn(dbg=None,
                 nc=None, p=None, seg="", xin_ap=None, xout_ap=None, finalize=True):
    if nc is None:
        nc = bass.Bass("TRN2", target_bir_lowering=False)
        p = Prog(nc)
    p.blkseg = seg
    xin = xin_ap if xin_ap is not None else nc.dram_tensor(
        "xin" + seg, [D, T], F32, kind="ExternalInput").ap()
    xout = xout_ap if xout_ap is not None else nc.dram_tensor(
        "xout" + seg, [D, T], F32, kind="ExternalOutput").ap()
    wdq = nc.dram_tensor("wdq" + seg, [2, 128, 1024], BF16, kind="ExternalInput").ap()
    wdkv = nc.dram_tensor("wdkv" + seg, [2, 128, 1024], BF16, kind="ExternalInput").ap()
    wuq = nc.dram_tensor("wuq" + seg, [128, 2048], BF16, kind="ExternalInput").ap()
    wuk = nc.dram_tensor("wuk" + seg, [128, 2048], BF16, kind="ExternalInput").ap()
    wuv = nc.dram_tensor("wuv" + seg, [128, 2048], BF16, kind="ExternalInput").ap()
    wod = nc.dram_tensor("wod" + seg, [8, 128, 1024], BF16, kind="ExternalInput").ap()
    identd = nc.dram_tensor("c_ident" + seg, [128, 128], BF16, kind="ExternalInput").ap()
    maskcd = nc.dram_tensor("c_mask_causal" + seg, [128, 2048], F32, kind="ExternalInput").ap()
    maskcTd = nc.dram_tensor("c_mask_causal_T" + seg, [128, 2048], F32, kind="ExternalInput").ap()
    dbgd = [nc.dram_tensor("dbg%d" % i + seg, [128, T], F32, kind="ExternalOutput").ap()
            for i in range(4)] if dbg else None

    ctx = []
    def alloc(name, shape, dt, psum=False):
        name = name + seg
        h = nc.psum_tensor(name, shape, dt) if psum else nc.sbuf_tensor(name, shape, dt)
        t = h.__enter__()
        ctx.append(h)
        return t

    ln = alloc("ln", [128, NCH * T], BF16)
    cq = alloc("cq", [128, 2 * T], BF16)
    ckv = alloc("ckv", [128, 2 * T], BF16)
    PS = alloc("PS", [128, 16 * TT], BF16)
    qst = alloc("qst", [128, T], BF16)
    kst = alloc("kst", [128, T], BF16)
    vtok = alloc("vtok", [128, 16 * 72], BF16)
    sq = alloc("sq", [128, T], BF16)
    ao = alloc("ao", [128, NCH * T], BF16)
    wbuf = alloc("wbuf", [128, 2048 * 3], BF16)
    wobuf = alloc("wobuf", [128, 2048], BF16)
    wdbuf = alloc("wdbuf", [128, 2048], BF16)
    xta = alloc("xta", [128, NCH * TT], F32)
    cbuf = alloc("cbuf", [128, NCH * TT], BF16)
    sq4 = alloc("sq4", [128, NCH * TT], BF16)
    rows = alloc("rows", [128, 1024], F32)
    msW = alloc("msW", [128, T], F32)
    maskc = alloc("maskc", [128, 2048], F32)
    maskcT = alloc("maskcT", [128, 2048], F32)
    colbuf = alloc("colbuf", [128, 32], F32)
    ident = alloc("ident", [128, 128], BF16)
    identf = alloc("identf", [128, 128], F32)
    ones_bf = alloc("ones_bf", [128, 1], BF16)
    pidb = None
    ones_f = alloc("ones_f", [1, 128], F32)
    rrow = alloc("rrow", [1, 512], F32)
    rrow_bf = alloc("rrow_bf", [1, 512], BF16)
    onesr = alloc("onesr", [1, 64], BF16)
    outsl = alloc("outsl", [128, 512], F32)
    pb = [alloc("pb%d" % i, [128, TT], F32, psum=True) for i in range(7)]
    ptr = alloc("ptr", [128, 1024], BF16, psum=True)
    banks = Banks(pb)
    sb = dict(ones_bf=ones_bf, ones_f=ones_f)
    sV, sS, sT, sX = (p.sem(n) for n in ("sV", "sS", "sT", "sX"))
    xi_r = xin.rearrange("(c p) t -> p c t", p=128)
    xo_r = xout.rearrange("(c p) t -> p c t", p=128)

    ph = Ph(p, "init")
    ph.G.memset(ones_bf[:, :], 1.0)
    ph.G.memset(ones_f[:, :], 1.0)
    ph.G.memset(onesr[:, :], 1.0)
    ph.G.memset(kst[64:65, :], 1.0)
    iodma(ph, p, ident[:, :], identd)
    iodma(ph, p, maskc[:, :], maskcd)
    iodma(ph, p, maskcT[:, :], maskcTd)
    c1 = wdma(ph, p, identf[:, :], identd)
    ph.G.wait_ge(sX, p.cnt["sX"])
    ph.G.wait_ge(p.sem("sW"), p.cnt["sW"])
    ph.emit()

    ph = Ph(p, "ln")
    emit_ln_stream(p, ph, banks, sb, xi_r, rows, [xta, xta], cbuf, sq4, ln,
                   msW[0:1, 0:TT])
    ph.emit()
    banks.reset()

    ph = Ph(p, "cqkv")
    cw = wdma(ph, p, wdbuf[:, 0:1024], wdq[0], sem="sW0")
    cw2 = wdma(ph, p, wdbuf[:, 1024:2048], wdq[1], sem="sW0")
    for half, (dst, wsrc) in enumerate(((cq, wdq), (ckv, wdkv))):
        for nn in range(2):
            w3 = wdbuf[:, nn * 1024:(nn + 1) * 1024].rearrange("p (k n) -> p k n", n=128)
            ph.T.wait_ge(p.sem("sW0"), p.cnt["sW0"] - (16 if (half == 0 and nn == 0) else 0))
            for tt in range(NTT):
                bi, bk_ = banks.take(ph, p)
                for kc in range(NCH):
                    op = ph.T.matmul(bk_[:, 0:TT], w3[:, kc, :],
                                     ln[:, kc * T + tt * TT: kc * T + (tt + 1) * TT],
                                     start=(kc == 0), stop=(kc == NCH - 1))
                ct = p.bump("sT")
                op.then_inc(sT, 1)
                ph.V.wait_ge(sT, ct)
                cv = p.bump("sV")
                ph.V.tensor_copy(dst[:, nn * T + tt * TT: nn * T + (tt + 1) * TT],
                                 bk_[:, 0:TT]).then_inc(sV, 1)
                banks.release(bi, "sV", cv)
        if half == 0:
            ph.G.wait_ge(sT, p.cnt["sT"])
            wdma(ph, p, wdbuf[:, 0:1024], wdkv[0], sem="sW0")
            wdma(ph, p, wdbuf[:, 1024:2048], wdkv[1], sem="sW0")
    wu_ct = []
    for i, wsrc in enumerate((wuq, wuk, wuv)):
        wu_ct.append(wdma(ph, p, wbuf[:, i * 2048:(i + 1) * 2048], wsrc, sem="sWc"))
    ph.emit()
    banks.reset()

    ph = Ph(p, "heads")
    ph.T.wait_ge(p.sem("sWc"), p.cnt["sWc"])
    ptr_cv = 0
    diag_cs = [0]
    prev_bcast = [0]
    for h in range(16):
        wq3 = wbuf[:, 0:2048].rearrange("p (k n) -> p k n", n=1024)
        wk3 = wbuf[:, 2048:4096].rearrange("p (k n) -> p k n", n=1024)
        wv3 = wbuf[:, 4096:6144].rearrange("p (k n) -> p k n", n=1024)
        for w in range(3):
            src = (cq, ckv, ckv)[w]
            w3 = (wq3, wk3, wv3)[w]
            for tt in range(NTT):
                bi, bk_ = banks.take(ph, p)
                for kc in range(2):
                    op = ph.T.matmul(bk_[0:64, 0:TT],
                                     w3[:, kc, h * 64:(h + 1) * 64],
                                     src[:, kc * T + tt * TT: kc * T + (tt + 1) * TT],
                                     start=(kc == 0), stop=(kc == 1))
                ct = p.bump("sT")
                op.then_inc(sT, 1)
                sl = slice(tt * TT, (tt + 1) * TT)
                ph.V.wait_ge(sT, ct)
                cv = p.bump("sV")
                if w == 0:
                    ph.V.tensor_scalar_mul(qst[0:64, sl], bk_[0:64, 0:TT],
                                           0.125).then_inc(sV, 1)
                elif w == 1:
                    ph.V.tensor_copy(kst[0:64, sl], bk_[0:64, 0:TT]).then_inc(sV, 1)
                else:
                    ph.V.tensor_copy(sq[0:64, sl], bk_[0:64, 0:TT]).then_inc(sV, 1)
                banks.release(bi, "sV", cv)
        qkv_done = p.cnt["sV"]
        for grp in range(2):
            if ptr_cv:
                ph.T.wait_ge(sV, ptr_cv)
            ph.T.wait_ge(sV, qkv_done)
            for t8 in range(8):
                ti = grp * 8 + t8
                op = ph.T.transpose(ptr[:, t8 * 128:t8 * 128 + 64],
                                    sq[0:64, ti * 128:(ti + 1) * 128],
                                    ident[0:64, 0:64])
            ct = p.bump("sT")
            op.then_inc(sT, 1)
            ph.V.wait_ge(sT, ct)
            for t8 in range(8):
                ti = grp * 8 + t8
                ph.V.tensor_copy(vtok[:, ti * 72:ti * 72 + 64],
                                 ptr[:, t8 * 128:t8 * 128 + 64])
                ph.V.memset(vtok[:, ti * 72 + 64:ti * 72 + 65], 1.0)
            cv = p.bump("sV")
            ph.V.ops[-1].then_inc(sV, 1)
            ptr_cv = cv
        vtok_done = ptr_cv
        for it in range(NTT):
            for qt in range(it * 4, it * 4 + 4):
                nk = qt // 4 + 1
                for kt in range(nk):
                    bi, bk_ = banks.take(ph, p)
                    ct = p.bump("sT")
                    ph.T.matmul(bk_[:, 0:TT], qst[0:64, qt * 128:(qt + 1) * 128],
                                kst[0:64, kt * TT:(kt + 1) * TT],
                                start=True, stop=True).then_inc(sT, 1)
                    ph.V.wait_ge(sT, ct)
                    cv = p.bump("sV")
                    if kt == qt // 4:
                        ph.V.tensor_add(msW[:, kt * TT:(kt + 1) * TT], bk_[:, 0:TT],
                                        maskc[:, (qt % 4) * TT:(qt % 4 + 1) * TT]
                                        ).then_inc(sV, 1)
                    else:
                        ph.V.tensor_copy(msW[:, kt * TT:(kt + 1) * TT],
                                         bk_[:, 0:TT]).then_inc(sV, 1)
                    banks.release(bi, "sV", cv)
                cvm = p.bump("sV")
                ph.V.reduce_max(colbuf[:, 8:9], msW[:, 0:nk * TT], AX.X,
                                negate=True).then_inc(sV, 1)
                ph.V.wait_ge(sV, cvm)
                bi, bk_ = banks.take(ph, p)
                ct = p.bump("sT")
                ph.T.wait_ge(sV, cvm)
                ph.T.transpose(bk_[0:1, 0:128], colbuf[:, 8:9],
                               identf[:, :]).then_inc(sT, 1)
                ph.V.wait_ge(sT, ct)
                cv = p.bump("sV")
                ph.V.tensor_copy(qst[64:65, qt * 128:(qt + 1) * 128],
                                 bk_[0:1, 0:128]).then_inc(sV, 1)
                banks.release(bi, "sV", cv)
            negm_done = p.cnt["sV"]
            ph.T.wait_ge(sV, negm_done)
            njc = 4 * (it + 1)
            for jc in range(njc):
                bi, bk_ = banks.take(ph, p)
                ct = p.bump("sT")
                ph.T.matmul(bk_[:, 0:TT], kst[0:65, jc * 128:(jc + 1) * 128],
                            qst[0:65, it * TT:(it + 1) * TT],
                            start=True, stop=True).then_inc(sT, 1)
                if jc // 4 == it:
                    ph.V.wait_ge(sT, ct)
                    if diag_cs[0]:
                        ph.V.wait_ge(sS, diag_cs[0])
                    cv = p.bump("sV")
                    ph.V.tensor_add(msW[:, 0:TT], bk_[:, 0:TT],
                                    maskcT[:, (jc % 4) * TT:((jc % 4) + 1) * TT]
                                    ).then_inc(sV, 1)
                    ph.S.wait_ge(sV, cv)
                    cs = p.bump("sS")
                    ph.S.activation(PS[:, jc * TT:(jc + 1) * TT], msW[:, 0:TT],
                                    AF.Exp).then_inc(sS, 1)
                    diag_cs[0] = cs
                    banks.release(bi, "sV", cv)
                else:
                    ph.S.wait_ge(sT, ct)
                    cs = p.bump("sS")
                    ph.S.activation(PS[:, jc * TT:(jc + 1) * TT], bk_[:, 0:TT],
                                    AF.Exp).then_inc(sS, 1)
                    banks.release(bi, "sS", cs)
            exp_done = p.cnt["sS"]
            ph.T.wait_ge(sS, exp_done)
            bi, bk_ = banks.take(ph, p)
            for jc in range(njc):
                op = ph.T.matmul(bk_[0:65, 0:TT], vtok[:, jc * 72:jc * 72 + 65],
                                 PS[:, jc * TT:(jc + 1) * TT],
                                 start=(jc == 0), stop=(jc == njc - 1))
            ct = p.bump("sT")
            op.then_inc(sT, 1)
            ph.V.wait_ge(sT, ct)
            if prev_bcast[0]:
                ph.V.wait_ge(sT, prev_bcast[0])
            cvr = p.bump("sV")
            ph.V.reciprocal(rrow[0:1, 0:TT], bk_[64:65, 0:TT]).then_inc(sV, 1)
            ph.V.wait_ge(sV, cvr)
            cv = p.bump("sV")
            ph.V.tensor_copy(rrow_bf[0:1, 0:TT], rrow[0:1, 0:TT]).then_inc(sV, 1)
            ph.T.wait_ge(sV, cv)
            bi2, bk2 = banks.take(ph, p)
            ct2 = p.bump("sT")
            ph.T.matmul(bk2[0:64, 0:TT], onesr[0:1, :], rrow_bf[0:1, 0:TT],
                        start=True, stop=True).then_inc(sT, 1)
            prev_bcast[0] = ct2
            ph.V.wait_ge(sT, ct2)
            cv = p.bump("sV")
            ph.V.tensor_copy(msW[:, 0:TT], bk_[:, 0:TT]).then_inc(sV, 1)
            banks.release(bi, "sV", cv)
            ph.V.wait_ge(sV, cv)
            hs = slice((h % 2) * 64, (h % 2) * 64 + 64)
            cv = p.bump("sV")
            ph.V.tensor_mul(ao[hs, (h // 2) * T + it * TT: (h // 2) * T + (it + 1) * TT],
                            msW[0:64, 0:TT], bk2[0:64, 0:TT]).then_inc(sV, 1)
            banks.release(bi2, "sV", cv)
    ph.emit()
    banks.reset()

    ph = Ph(p, "wo")
    def wo_dma(nn, slot):
        return wdma(ph, p, wobuf[:, slot * 1024:(slot + 1) * 1024], wod[nn],
                    sem="sW%d" % slot)
    wct = {0: wo_dma(0, 0), 1: wo_dma(1, 1)}
    st_cnt = {}
    last_stt = [0]
    k = 0
    for nn in range(NCH):
        slot = nn % 2
        w3 = wobuf[:, slot * 1024:(slot + 1) * 1024].rearrange("p (k n) -> p k n", n=128)
        ph.T.wait_ge(p.sem("sW%d" % slot), wct[nn])
        for tt in range(NTT):
            if k >= 1:
                ph.G.wait_ge(sV, last_stt[0])
            ldc = iodma(ph, p, xta[:, 0:TT], xi_r[:, nn, tt * TT:(tt + 1) * TT],
                        sem="sXi0")
            bi, bk_ = banks.take(ph, p)
            for kc in range(NCH):
                op = ph.T.matmul(bk_[:, 0:TT], w3[:, kc, :],
                                 ao[:, kc * T + tt * TT: kc * T + (tt + 1) * TT],
                                 start=(kc == 0), stop=(kc == NCH - 1))
            ct = p.bump("sT")
            op.then_inc(sT, 1)
            ph.V.wait_ge(sT, ct)
            ph.V.wait_ge(p.sem("sXi0"), ldc)
            if k >= 1:
                ph.V.wait_ge(p.sem("sXo"), st_cnt[k - 1])
            cv = p.bump("sV")
            ph.V.tensor_add(outsl[:, 0:TT], bk_[:, 0:TT], xta[:, 0:TT]).then_inc(sV, 1)
            last_stt[0] = cv
            banks.release(bi, "sV", cv)
            ph.G.wait_ge(sV, cv)
            st_cnt[k] = iodma(ph, p, xo_r[:, nn, tt * TT:(tt + 1) * TT],
                              outsl[:, 0:TT], sem="sXo")
            k += 1
        if nn + 2 < NCH:
            ph.G.wait_ge(sT, p.cnt["sT"])
            wct[nn + 2] = wo_dma(nn + 2, slot)
    ph.G.wait_ge(p.sem("sXo"), p.cnt["sXo"])
    ph.emit()
    for h in reversed(ctx):
        h.__exit__(None, None, None)
    if finalize:
        nc.finalize()
    return nc


def prep_l1attn_weights(ins):
    f32 = np.float32
    def two(W):
        return np.ascontiguousarray(
            np.asarray(W, f32).reshape(2, 128, 1024).transpose(0, 1, 2)
            if W.shape == (1024, 256) else W)
    W_dq = np.asarray(ins["L1_attn_W_dq"], f32)
    W_dkv = np.asarray(ins["L1_attn_W_dkv"], f32)
    def down_t(W):
        return np.ascontiguousarray(
            W.reshape(8, 128, 2, 128).transpose(2, 1, 0, 3)).reshape(2, 128, 1024)
    def up_t(W):
        return np.ascontiguousarray(
            np.asarray(W, f32).reshape(2, 128, 1024).transpose(1, 0, 2)).reshape(128, 2048)
    WO = np.asarray(ins["L1_attn_W_o"], f32)
    wo_t = np.ascontiguousarray(
        WO.reshape(8, 128, 8, 128).transpose(2, 1, 0, 3)).reshape(8, 128, 1024)
    con = host_constants()
    bf = ml_dtypes.bfloat16
    return dict(
        wdq=down_t(W_dq).astype(bf), wdkv=down_t(W_dkv).astype(bf),
        wuq=up_t(ins["L1_attn_W_uq"]).astype(bf), wuk=up_t(ins["L1_attn_W_uk"]).astype(bf),
        wuv=up_t(ins["L1_attn_W_uv"]).astype(bf),
        wod=wo_t.astype(bf),
        c_ident=con["c_ident"],
        c_mask_causal=np.ascontiguousarray(
            con["c_mask_causal"].reshape(4, 128, 512).transpose(1, 0, 2).reshape(128, 2048)),
        c_mask_causal_T=np.ascontiguousarray(
            con["c_mask_causal_T"].reshape(4, 128, 512).transpose(1, 0, 2).reshape(128, 2048)),
    )


CAPACITY = 5120.0

def build_moe(ncores=8, dbg=None, counts_mode=None,
              nc=None, p=None, seg="", xin_ap=None, xout_ap=None, finalize=True):
    if nc is None:
        nc = bass.Bass("TRN2", target_bir_lowering=False)
        p = Prog(nc)
    E = 8
    p.blkseg = seg
    xin = xin_ap if xin_ap is not None else nc.dram_tensor(
        "xin" + seg, [D, T], F32, kind="ExternalInput").ap()
    xout = None
    if counts_mode != "emit":
        xout = xout_ap if xout_ap is not None else nc.dram_tensor(
            "xout" + seg, [D, T], F32, kind="ExternalOutput").ap()
    wgt = nc.dram_tensor("wgt" + seg, [8, 128, 8], F32, kind="ExternalInput").ap()
    cfc = nc.dram_tensor("cfc" + seg, [8, 32, 128, 1024], BF16, kind="ExternalInput").ap()
    cpj = nc.dram_tensor("cpj" + seg, [8, 8, 128, 4096], BF16, kind="ExternalInput").ap()
    coreid = nc.dram_tensor("coreid" + seg, [1, 128], F32, kind="ExternalInput").ap()
    identd = nc.dram_tensor("c_ident" + seg, [128, 128], BF16, kind="ExternalInput").ap()
    io8d = nc.dram_tensor("c_iota8" + seg, [128, 8], F32, kind="ExternalInput").ap()
    io8m9d = nc.dram_tensor("c_io8m9" + seg, [128, 8], F32, kind="ExternalInput").ap()
    io8rd = nc.dram_tensor("c_iota8r" + seg, [64, 8], F32, kind="ExternalInput").ap()
    if counts_mode == "emit":
        cntout = nc.dram_tensor("cntout" + seg, [1, 64], F32, kind="ExternalOutput").ap()
    elif counts_mode == "input":
        g8in = nc.dram_tensor("g8in" + seg, [8, 64], F32, kind="ExternalInput").ap()
    dbgd = [nc.dram_tensor("dbg%d" % i + seg, [128, T], F32, kind="ExternalOutput").ap()
            for i in range(4)] if dbg else None

    ctx = []
    def alloc(name, shape, dt, psum=False):
        name = name + seg
        h = nc.psum_tensor(name, shape, dt) if psum else nc.sbuf_tensor(name, shape, dt)
        t = h.__enter__()
        ctx.append(h)
        return t

    ln = alloc("ln", [128, NCH * T], BF16)
    xta = alloc("xta", [128, NCH * TT], F32)
    cbuf = alloc("cbuf", [128, NCH * TT], BF16)
    sq4f = alloc("sq4f", [128, TT], F32)
    rows = alloc("rows", [128, 1024], F32)
    lgt = alloc("lgt", [128, T], F32)
    LG = alloc("LG", [128, 128], F32)
    eqf = alloc("eqf", [128, 8], F32)
    LG2 = None
    tmp8 = alloc("tmp8", [128, 8], F32)
    OH0tm = alloc("OH0tm", [128, 128], BF16)
    OH1tm = alloc("OH1tm", [128, 128], BF16)
    OH0e = alloc("OH0e", [128, T], F32)
    OH1e = alloc("OH1e", [128, T], F32)
    SC = alloc("SC", [128, T], F32)
    v0e = alloc("v0e", [128, T], BF16)
    v1e = alloc("v1e", [128, T], BF16)
    m0b = alloc("m0b", [128, T], BF16)
    vrow0 = alloc("vrow0", [1, T], BF16)
    vrow1 = alloc("vrow1", [1, T], BF16)
    wgtb = alloc("wgtb", [128, 64], F32)
    cb2 = alloc("cb2", [128, 160], F32)
    G8 = alloc("G8", [128, 64], F32)
    G8T = alloc("G8T", [128, 8], F32)
    maskT = alloc("maskT", [128, 8], F32)
    io8r = alloc("io8r", [64, 8], F32)
    io8c = alloc("io8c", [128, 8], F32)
    io8m9 = alloc("io8m9", [128, 8], F32)
    Wtm = alloc("Wtm", [128, 128], F32)
    wrow_bf = alloc("wrow_bf", [1, 2048], BF16)
    wbs = alloc("wbs", [128, 512], BF16)
    hbuf = alloc("hbuf", [128, 32 * 512], BF16)
    acc = alloc("acc", [128, NCH * 512], F32)
    cfcb = alloc("cfcb", [128, 2048], BF16)
    cpjb = alloc("cpjb", [128, 8192], BF16)
    msW = alloc("msW", [128, 512], F32)
    LG2 = msW[:, 0:8]
    t1s = alloc("t1s", [128, 512], F32)
    outsl = t1s
    _crow_holder = None
    crow = None
    ident = alloc("ident", [128, 128], BF16)
    identf = alloc("identf", [128, 128], F32)
    ones_bf = alloc("ones_bf", [128, 1], BF16)
    pidb = None
    ones_f = alloc("ones_f", [1, 128], F32)
    onesc_f = alloc("onesc_f", [128, 1], F32)
    onesr = alloc("onesr", [1, 128], BF16)
    pb = [alloc("pb%d" % i, [128, TT], F32, psum=True) for i in range(7)]
    ptr = alloc("ptr", [128, 1024], BF16, psum=True)
    banks = Banks(pb)
    crow = G8[64:65, 0:64]
    pidb = wrow_bf[0:1, 1024:1025]
    sb = dict(ones_bf=ones_bf, ones_f=ones_f)
    sV, sS, sT, sX = (p.sem(n) for n in ("sV", "sS", "sT", "sX"))
    sG = p.sem("sG")
    xi_r = xin.rearrange("(c p) t -> p c t", p=128)
    xo_r = xout.rearrange("(c p) t -> p c t", p=128) if xout is not None else None

    ph = Ph(p, "init")
    ph.G.memset(ones_bf[:, :], 1.0)
    ph.G.memset(ones_f[:, :], 1.0)
    ph.G.memset(onesc_f[:, :], 1.0)
    ph.G.memset(onesr[:, :], 1.0)
    iodma(ph, p, ident[:, :], identd)
    wdma(ph, p, identf[:, :], identd)
    iodma(ph, p, io8c[:, :], io8d)
    iodma(ph, p, io8m9[:, :], io8m9d)
    iodma(ph, p, io8r[:, :], io8rd)
    iodma(ph, p, wgtb[:, :].rearrange("p (k e) -> p k e", e=8),
          wgt.rearrange("k p e -> p k e"))
    iodma(ph, p, xta[0:1, 0:128], coreid)
    ph.G.wait_ge(sX, p.cnt["sX"])
    ph.G.wait_ge(p.sem("sW"), p.cnt["sW"])
    ph.emit()

    ph = Ph(p, "pid")
    cv = p.bump("sV")
    ph.V.tensor_copy(pidb[0:1, 0:1], xta[0:1, 0:1]).then_inc(sV, 1)
    ph.T.wait_ge(sV, cv)
    bi, bk_ = banks.take(ph, p)
    ct = p.bump("sT")
    ph.T.matmul(bk_[0:64, 0:1], onesr[0:1, 0:64], pidb[0:1, 0:1],
                start=True, stop=True).then_inc(sT, 1)
    ph.V.wait_ge(sT, ct)
    cv = p.bump("sV")
    ph.V.tensor_copy(cb2[0:64, 112:113], bk_[0:64, 0:1]).then_inc(sV, 1)
    banks.release(bi, "sV", cv)
    w3 = wgtb[:, :].rearrange("p (k e) -> p k e", e=8)
    bi, bk_ = banks.take(ph, p)
    for kc in range(NCH):
        op = ph.T.matmul(bk_[0:8, 0:1], w3[:, kc, :], onesc_f[:, 0:1],
                         start=(kc == 0), stop=(kc == NCH - 1))
    ct = p.bump("sT")
    op.then_inc(sT, 1)
    ph.V.wait_ge(sT, ct)
    cv = p.bump("sV")
    ph.V.tensor_copy(cb2[0:8, 104:105], bk_[0:8, 0:1]).then_inc(sV, 1)
    banks.release(bi, "sV", cv)
    ph.emit()
    banks.reset()

    ph = Ph(p, "ln")
    def logits_extra(tt, xt, mrow, vrow):
        bi, bk_ = banks.take(ph, p)
        for kc in range(NCH):
            op = ph.T.matmul(bk_[0:8, 0:TT], w3[:, kc, :],
                             xt[:, kc * TT:(kc + 1) * TT],
                             start=(kc == 0), stop=(kc == NCH - 1))
        ct = p.bump("sT")
        op.then_inc(sT, 1)
        bm_i, bm_ = banks.take(ph, p)
        ct2 = p.bump("sT")
        ph.T.matmul(bm_[0:8, 0:TT], ones_f[0:1, 0:8], mrow,
                    start=True, stop=True)
        ph.T.matmul(bm_[0:8, TT if False else 0:TT][0:8, 0:TT] if False else bm_[0:8, 0:TT],
                    ones_f[0:1, 0:8], mrow, start=True, stop=True).then_inc(sT, 1)
        br_i, br_ = banks.take(ph, p)
        ct3 = p.bump("sT")
        ph.T.matmul(br_[0:8, 0:TT], ones_f[0:1, 0:8], vrow,
                    start=True, stop=True).then_inc(sT, 1)
        ph.V.wait_ge(sT, ct3)
        sl = slice(tt * TT, (tt + 1) * TT)
        ph.V.tensor_scalar_mul(t1s[0:8, 0:TT], bm_[0:8, 0:TT], cb2[0:8, 104:105])
        ph.V.tensor_sub(t1s[0:8, 0:TT], bk_[0:8, 0:TT], t1s[0:8, 0:TT])
        cv = p.bump("sV")
        ph.V.tensor_mul(lgt[0:8, sl], t1s[0:8, 0:TT], br_[0:8, 0:TT]).then_inc(sV, 1)
        banks.release(bi, "sV", cv)
        banks.release(bm_i, "sV", cv)
        banks.release(br_i, "sV", cv)
    emit_ln_stream(p, ph, banks, sb, xi_r, rows, [xta, xta], cbuf,
                   hbuf[:, 0:NCH * TT], ln,
                   msW[0:1, 0:TT], extra_fn=logits_extra, fp32_stats=True,
                   sq4f=sq4f, onesc_f=onesc_f)
    ph.emit()
    banks.reset()

    ph = Ph(p, "route")
    lgt_done = p.cnt["sV"]
    ph.T.wait_ge(sV, lgt_done)
    for g in range(16):
        bi, bk_ = banks.take(ph, p)
        ct = p.bump("sT")
        ph.T.transpose(bk_[0:128, 0:8], lgt[0:8, g * 128:(g + 1) * 128],
                       identf[0:8, 0:8]).then_inc(sT, 1)
        ph.V.wait_ge(sT, ct)
        cv = p.bump("sV")
        ph.V.tensor_copy(LG[:, g * 8:(g + 1) * 8], bk_[0:128, 0:8]).then_inc(sV, 1)
        banks.release(bi, "sV", cv)
    def vstep(op):
        cvm = p.bump("sV")
        op.then_inc(sV, 1)
        ph.V.wait_ge(sV, cvm)
    for g in range(16):
        Lg = LG[:, g * 8:(g + 1) * 8]
        vstep(ph.V.reduce_max(cb2[:, 96:97], Lg, AX.X))
        vstep(ph.V.tensor_single_scalar(eqf[:, :], Lg, cb2[:, 96:97], ALU.is_equal))
        vstep(ph.V.tensor_mul(tmp8[:, :], eqf[:, :], io8m9[:, :]))
        vstep(ph.V.tensor_scalar_add(tmp8[:, :], tmp8[:, :], 9.0))
        vstep(ph.V.tensor_reduce(cb2[:, 97:98], tmp8[:, :], AX.X, ALU.min))
        vstep(ph.V.tensor_single_scalar(eqf[:, :], io8c[:, :], cb2[:, 97:98], ALU.is_equal))
        vstep(ph.V.tensor_copy(OH0tm[:, g * 8:(g + 1) * 8], eqf[:, :]))
        vstep(ph.V.tensor_copy(cb2[:, g:g + 1], cb2[:, 96:97]))
        vstep(ph.V.scalar_tensor_tensor(LG2[:, :], eqf[:, :], NEG, Lg,
                                        ALU.mult, ALU.add))
        vstep(ph.V.reduce_max(cb2[:, 96:97], LG2[:, :], AX.X))
        vstep(ph.V.tensor_single_scalar(eqf[:, :], LG2[:, :], cb2[:, 96:97], ALU.is_equal))
        vstep(ph.V.tensor_mul(tmp8[:, :], eqf[:, :], io8m9[:, :]))
        vstep(ph.V.tensor_scalar_add(tmp8[:, :], tmp8[:, :], 9.0))
        vstep(ph.V.tensor_reduce(cb2[:, 98:99], tmp8[:, :], AX.X, ALU.min))
        vstep(ph.V.tensor_single_scalar(eqf[:, :], io8c[:, :], cb2[:, 98:99], ALU.is_equal))
        vstep(ph.V.tensor_copy(OH1tm[:, g * 8:(g + 1) * 8], eqf[:, :]))
        vstep(ph.V.tensor_copy(cb2[:, 16 + g:17 + g], cb2[:, 96:97]))
    cv = p.bump("sV")
    ph.V.tensor_sub(cb2[:, 96:112], cb2[:, 0:16], cb2[:, 16:32]).then_inc(sV, 1)
    ph.S.wait_ge(sV, cv)
    cs = p.bump("sS")
    ph.S.activation(cb2[:, 32:48], cb2[:, 96:112], AF.Sigmoid).then_inc(sS, 1)
    ph.V.wait_ge(sS, cs)
    cv = p.bump("sV")
    ph.V.tensor_scalar(cb2[:, 48:64], cb2[:, 32:48], -1.0, 1.0,
                       ALU.mult, ALU.add).then_inc(sV, 1)
    oh_done = p.cnt["sV"]
    ph.T.wait_ge(sV, oh_done)
    for k, (tm, em) in enumerate(((OH0tm, OH0e), (OH1tm, OH1e))):
        for g in range(16):
            bi, bk_ = banks.take(ph, p)
            ct = p.bump("sT")
            ph.T.transpose(ptr[0:8, 0:128], tm[:, g * 8:(g + 1) * 8],
                           ident[:, :]).then_inc(sT, 1)
            ph.V.wait_ge(sT, ct)
            cv = p.bump("sV")
            ph.V.tensor_copy(em[0:8, g * 128:(g + 1) * 128],
                             ptr[0:8, 0:128]).then_inc(sV, 1)
            banks.release(bi, "sV", cv)
            ph.T.wait_ge(sV, cv)
    ohe_done = p.cnt["sV"]
    cv0 = p.bump("sV")
    ph.V.tensor_tensor_scan(SC[0:8, :], OH0e[0:8, :], OH0e[0:8, :], 0.0,
                            ALU.add, ALU.bypass).then_inc(sV, 1)
    ph.V.wait_ge(sV, cv0)
    bi, bk_ = banks.take(ph, p)
    ct = p.bump("sT")
    ph.T.wait_ge(sV, cv0)
    ph.T.transpose(bk_[0:1, 0:8], SC[0:8, T - 1:T], identf[0:8, 0:8]).then_inc(sT, 1)
    ph.V.wait_ge(sT, ct)
    cv = p.bump("sV")
    ph.V.tensor_copy(crow[0:1, 0:8], bk_[0:1, 0:8]).then_inc(sV, 1)
    banks.release(bi, "sV", cv)
    cv = p.bump("sV")
    ph.V.tensor_sub(SC[0:8, :], SC[0:8, :], OH0e[0:8, :]).then_inc(sV, 1)
    excl0_done = cv
    ph.emit()
    banks.reset()

    ph = Ph(p, "cc")
    cv1 = p.bump("sV")
    ph.V.tensor_tensor_scan(lgt[0:8, :] if False else OH0e[64:72, 0:T] if False else SC[32:40, 0:T],
                            OH1e[0:8, :], OH1e[0:8, :], 0.0,
                            ALU.add, ALU.bypass).then_inc(sV, 1)
    ph.V.wait_ge(sV, cv1)
    ph.V.wait_ge(sV, cv1)
    cvc = p.bump("sV")
    ph.V.tensor_copy(eqf[0:8, 0:1], SC[32:40, T - 1:T]).then_inc(sV, 1)
    bi, bk_ = banks.take(ph, p)
    ct = p.bump("sT")
    ph.T.wait_ge(sV, cvc)
    ph.T.transpose(bk_[0:1, 0:8], eqf[0:8, 0:1], identf[0:8, 0:8]).then_inc(sT, 1)
    ph.V.wait_ge(sT, ct)
    ph.V.memset(crow[0:1, 8:32], 0.0)
    ph.V.memset(crow[0:1, 40:64], 0.0)
    cv = p.bump("sV")
    ph.V.tensor_copy(crow[0:1, 32:40], bk_[0:1, 0:8]).then_inc(sV, 1)
    banks.release(bi, "sV", cv)
    ph.G.wait_ge(sV, cv)
    if counts_mode == "emit":
        cx1 = iodma(ph, p, cntout, crow[0:1, :], sem="sCC")
        ph.G.wait_ge(p.sem("sCC"), cx1)
        ph.emit()
        for h in reversed(ctx):
            h.__exit__(None, None, None)
        if finalize:
            nc.finalize()
        return nc
    if counts_mode == "input":
        cx1 = iodma(ph, p, G8[0:8, :], g8in, sem="sCC")
        ph.G.wait_ge(p.sem("sCC"), cx1)
    else:
        cv = p.bump("sV")
        ph.V.tensor_copy(G8[0:1, :], crow[0:1, :]).then_inc(sV, 1)
        ph.G.wait_ge(sV, cv)
    ph.emit()
    banks.reset()

    ph = Ph(p, "rank")
    bi, bk_ = banks.take(ph, p)
    ct = p.bump("sT")
    ph.T.transpose(bk_[0:64, 0:ncores], G8[0:ncores, 0:64],
                   identf[0:ncores, 0:ncores]).then_inc(sT, 1)
    ph.V.wait_ge(sT, ct)
    cv = p.bump("sV")
    ph.V.tensor_copy(G8T[0:64, 0:ncores], bk_[0:64, 0:ncores]).then_inc(sV, 1)
    banks.release(bi, "sV", cv)
    ph.V.tensor_single_scalar(maskT[0:64, 0:ncores], io8r[0:64, 0:ncores],
                              cb2[0:64, 112:113], ALU.is_lt)
    cv = p.bump("sV")
    ph.V.tensor_mul(maskT[0:64, 0:ncores], maskT[0:64, 0:ncores],
                    G8T[0:64, 0:ncores]).then_inc(sV, 1)
    ph.V.wait_ge(sV, cv)
    cv = p.bump("sV")
    ph.V.reduce_sum(cb2[0:64, 113:114], maskT[0:64, 0:ncores], AX.X).then_inc(sV, 1)
    ph.V.wait_ge(sV, cv)
    cv = p.bump("sV")
    ph.V.reduce_sum(cb2[0:64, 114:115], G8T[0:64, 0:ncores], AX.X).then_inc(sV, 1)
    ph.V.wait_ge(sV, cv)
    ph.V.tensor_scalar_add(SC[0:8, :], SC[0:8, :], cb2[0:8, 113:114])
    cv = p.bump("sV")
    ph.V.tensor_single_scalar(v0e[0:8, :], SC[0:8, :], CAPACITY,
                              ALU.is_lt).then_inc(sV, 1)
    ph.V.wait_ge(sV, cv)
    ph.V.tensor_copy(SC[0:8, :], SC[32:40, 0:T])
    ph.V.tensor_copy(cb2[0:8, 115:116], cb2[32:40, 113:114])
    ph.V.tensor_sub(SC[0:8, :], SC[0:8, :], OH1e[0:8, :])
    ph.V.tensor_scalar_add(SC[0:8, :], SC[0:8, :], cb2[0:8, 115:116])
    cv = p.bump("sV")
    ph.V.tensor_scalar_add(SC[0:8, :], SC[0:8, :],
                           cb2[0:8, 114:115]).then_inc(sV, 1)
    ph.V.wait_ge(sV, cv)
    cv = p.bump("sV")
    ph.V.tensor_single_scalar(v1e[0:8, :], SC[0:8, :], CAPACITY,
                              ALU.is_lt).then_inc(sV, 1)
    valid_done = cv
    for k, (oe, ve, vr) in enumerate(((OH0e, v0e, vrow0), (OH1e, v1e, vrow1))):
        cv = p.bump("sV")
        ph.V.tensor_mul(m0b[0:8, :], oe[0:8, :], ve[0:8, :]).then_inc(sV, 1)
        ph.T.wait_ge(sV, cv)
        for tt in range(NTT):
            bi, bk_ = banks.take(ph, p)
            ct = p.bump("sT")
            ph.T.matmul(bk_[0:1, 0:TT], ones_bf[0:8, :],
                        m0b[0:8, tt * TT:(tt + 1) * TT],
                        start=True, stop=True).then_inc(sT, 1)
            ph.V.wait_ge(sT, ct)
            cv = p.bump("sV")
            ph.V.tensor_copy(vr[0:1, tt * TT:(tt + 1) * TT],
                             bk_[0:1, 0:TT]).then_inc(sV, 1)
            banks.release(bi, "sV", cv)
    ph.T.wait_ge(sV, p.cnt["sV"])
    for k, (vr, col) in enumerate(((vrow0, 64), (vrow1, 80))):
        for g in range(16):
            bi, bk_ = banks.take(ph, p)
            ct = p.bump("sT")
            ph.T.matmul(bk_[0:128, 0:1], vr[0:1, g * 128:(g + 1) * 128],
                        ones_bf[0:1, :], start=True, stop=True).then_inc(sT, 1)
            ph.V.wait_ge(sT, ct)
            cv = p.bump("sV")
            ph.V.tensor_copy(cb2[:, col + g:col + g + 1],
                             bk_[0:128, 0:1]).then_inc(sV, 1)
            banks.release(bi, "sV", cv)
    ph.V.tensor_mul(cb2[:, 32:48], cb2[:, 32:48], cb2[:, 64:80])
    cv = p.bump("sV")
    ph.V.tensor_mul(cb2[:, 48:64], cb2[:, 48:64], cb2[:, 80:96]).then_inc(sV, 1)
    ph.V.wait_ge(sV, cv)
    for g in range(16):
        ph.V.tensor_scalar_mul(Wtm[:, g * 8:(g + 1) * 8],
                               OH0tm[:, g * 8:(g + 1) * 8], cb2[:, 32 + g:33 + g])
        cv = p.bump("sV")
        ph.V.scalar_tensor_tensor(Wtm[:, g * 8:(g + 1) * 8],
                                  OH1tm[:, g * 8:(g + 1) * 8],
                                  cb2[:, 48 + g:49 + g], Wtm[:, g * 8:(g + 1) * 8],
                                  ALU.mult, ALU.add).then_inc(sV, 1)
    W_done = p.cnt["sV"]
    if dbg == "route":
        ph.G.wait_ge(sV, W_done)
        wdma(ph, p, dbgd[0][:, 0:128], Wtm[:, :])
        wdma(ph, p, dbgd[1][:, 0:T], lgt[:, 0:T])
        wdma(ph, p, dbgd[2][:, 0:T], SC[:, 0:T])
        wdma(ph, p, dbgd[3][:, 0:128], LG[:, :])
    ph.emit()
    banks.reset()

    for tq in range(NTT):
        ph = Ph(p, "exp%d" % tq)
        ph.G.memset(acc[:, :], 0.0)
        cg = p.bump("sG")
        ph.G.ops[-1].then_inc(sG, 1)
        ph.V.wait_ge(sG, cg)
        cfc_ct = {}
        cpj_ct = {}
        for e in range(E):
            ph.T.wait_ge(sV, p.cnt["sV"])
            for g in range(4 * tq, 4 * tq + 4):
                bi, bk_ = banks.take(ph, p)
                ct = p.bump("sT")
                ph.T.transpose(bk_[0:1, 0:128], Wtm[:, g * 8 + e:g * 8 + e + 1],
                               identf[:, :]).then_inc(sT, 1)
                ph.V.wait_ge(sT, ct)
                cv = p.bump("sV")
                ph.V.tensor_copy(wrow_bf[0:1, (g - 4 * tq) * 128:(g - 4 * tq + 1) * 128],
                                 bk_[0:1, 0:128]).then_inc(sV, 1)
                banks.release(bi, "sV", cv)
            cv = p.cnt["sV"]
            ph.T.wait_ge(sV, cv)
            bi, bk_ = banks.take(ph, p)
            ct = p.bump("sT")
            ph.T.matmul(bk_[:, 0:TT], onesr[0:1, :], wrow_bf[0:1, 0:TT],
                        start=True, stop=True).then_inc(sT, 1)
            ph.V.wait_ge(sT, ct)
            cv = p.bump("sV")
            ph.V.tensor_copy(wbs[:, 0:TT], bk_[:, 0:TT]).then_inc(sV, 1)
            banks.release(bi, "sV", cv)
            wbs_done = cv
            slot = e % 2
            if e == 0 and tq == 0:
                cfc_ct[0] = wdma(ph, p, cfcb[:, 0:1024], cfc[0, 0], sem="sWa0")
            for j in range(32):
                jslot = j % 2
                if j + 1 < 32 or e + 1 < E:
                    nj = (j + 1) % 32
                    ne = e + 1 if j + 1 == 32 else e
                    if (j, e) == (0, 0) and tq == 0:
                        pass
                key = (e, j)
                if key not in cfc_ct:
                    cfc_ct[key] = wdma(ph, p, cfcb[:, jslot * 1024:(jslot + 1) * 1024],
                                       cfc[e, j], sem="sWa%d" % jslot)
                nkey = (e, j + 1) if j + 1 < 32 else ((e + 1, 0) if e + 1 < E else None)
                wf3 = cfcb[:, jslot * 1024:(jslot + 1) * 1024].rearrange(
                    "p (k n) -> p k n", n=128)
                ph.T.wait_ge(p.sem("sWa%d" % jslot), cfc_ct[key])
                bi, bk_ = banks.take(ph, p)
                for kc in range(NCH):
                    op = ph.T.matmul(bk_[:, 0:TT], wf3[:, kc, :],
                                     ln[:, kc * T + tq * TT: kc * T + (tq + 1) * TT],
                                     start=(kc == 0), stop=(kc == NCH - 1))
                ct = p.bump("sT")
                op.then_inc(sT, 1)
                if nkey is not None:
                    ph.G.wait_ge(sT, ct)
                    cfc_ct[nkey] = wdma(
                        ph, p, cfcb[:, (1 - jslot) * 1024:(2 - jslot) * 1024],
                        cfc[nkey[0], nkey[1]], sem="sWa%d" % (1 - jslot))
                ph.S.wait_ge(sT, ct)
                cs = p.bump("sS")
                ph.S.activation(msW[:, 0:TT], bk_[:, 0:TT], AF.Relu).then_inc(sS, 1)
                banks.release(bi, "sS", cs)
                ph.V.wait_ge(sS, cs)
                cv = p.bump("sV")
                ph.V.tensor_mul(hbuf[:, j * TT:(j + 1) * TT], msW[:, 0:TT],
                                msW[:, 0:TT]).then_inc(sV, 1)
            h_done = p.cnt["sV"]
            ph.T.wait_ge(sV, h_done)
            for nn in range(NCH):
                pslot = nn % 2
                key = (e, nn)
                if key not in cpj_ct:
                    cpj_ct[key] = wdma(ph, p, cpjb[:, pslot * 4096:(pslot + 1) * 4096],
                                       cpj[e, nn], sem="sWb%d" % pslot)
                wp3 = cpjb[:, pslot * 4096:(pslot + 1) * 4096].rearrange(
                    "p (k n) -> p k n", n=128)
                ph.T.wait_ge(p.sem("sWb%d" % pslot), cpj_ct[key])
                bi, bk_ = banks.take(ph, p)
                for kc in range(32):
                    op = ph.T.matmul(bk_[:, 0:TT], wp3[:, kc, :],
                                     hbuf[:, kc * TT:(kc + 1) * TT],
                                     start=(kc == 0), stop=(kc == 31))
                ct = p.bump("sT")
                op.then_inc(sT, 1)
                nkey = (e, nn + 1) if nn + 1 < NCH else ((e + 1, 0) if e + 1 < E else None)
                if nkey is not None:
                    ph.G.wait_ge(sT, ct)
                    cpj_ct[nkey] = wdma(
                        ph, p, cpjb[:, (1 - pslot) * 4096:(2 - pslot) * 4096],
                        cpj[nkey[0], nkey[1]], sem="sWb%d" % (1 - pslot))
                ph.V.wait_ge(sT, ct)
                ph.V.wait_ge(sV, wbs_done)
                ph.V.tensor_mul(t1s[:, 0:TT], bk_[:, 0:TT], wbs[:, 0:TT])
                cv = p.bump("sV")
                ph.V.tensor_add(acc[:, nn * TT:(nn + 1) * TT],
                                acc[:, nn * TT:(nn + 1) * TT],
                                t1s[:, 0:TT]).then_inc(sV, 1)
                banks.release(bi, "sV", cv)
        st = [0]
        for nn in range(NCH):
            if st[0]:
                ph.G.wait_ge(sV, st[0])
            ldc = iodma(ph, p, xta[:, 0:TT], xi_r[:, nn, tq * TT:(tq + 1) * TT],
                        sem="sXi0")
            ph.V.wait_ge(p.sem("sXi0"), ldc)
            if nn >= 1:
                ph.V.wait_ge(p.sem("sXo"), p.cnt["sXo"])
            cv = p.bump("sV")
            ph.V.scalar_tensor_tensor(outsl[:, 0:TT], acc[:, nn * TT:(nn + 1) * TT],
                                      0.5, xta[:, 0:TT],
                                      ALU.mult, ALU.add).then_inc(sV, 1)
            st[0] = cv
            ph.G.wait_ge(sV, cv)
            iodma(ph, p, xo_r[:, nn, tq * TT:(tq + 1) * TT], outsl[:, 0:TT],
                  sem="sXo")
        ph.G.wait_ge(p.sem("sXo"), p.cnt["sXo"])
        ph.emit()
        banks.reset()
    for h in reversed(ctx):
        h.__exit__(None, None, None)
    if finalize:
        nc.finalize()
    return nc


def prep_moe_weights(ins, core_id=0):
    f32 = np.float32
    w_g = np.asarray(ins["L1_ffn_w_g"], f32)
    c_fc = np.asarray(ins["L1_ffn_c_fc"], f32)
    c_proj = np.asarray(ins["L1_ffn_c_proj"], f32)
    wgt = np.ascontiguousarray(w_g.T.reshape(8, 128, 8))
    cfc_t = np.ascontiguousarray(
        c_fc.reshape(8, 8, 128, 32, 128).transpose(0, 3, 2, 1, 4)).reshape(8, 32, 128, 1024)
    cpj_t = np.ascontiguousarray(
        c_proj.reshape(8, 32, 128, 8, 128).transpose(0, 3, 2, 1, 4)).reshape(8, 8, 128, 4096)
    con = host_constants()
    io8m9 = np.broadcast_to(np.arange(8, dtype=f32) - 9.0, (128, 8)).copy()
    io8r = np.broadcast_to(np.arange(8, dtype=f32)[None, :], (64, 8)).copy()
    bf = ml_dtypes.bfloat16
    return dict(
        wgt=wgt, cfc=cfc_t.astype(bf), cpj=cpj_t.astype(bf),
        coreid=np.full((1, 128), float(core_id), f32),
        c_ident=con["c_ident"], c_iota8=con["c_iota8"],
        c_io8m9=io8m9, c_iota8r=io8r,
    )



_PROG_CACHE = {}

def _progs():
    if "ffn" not in _PROG_CACHE:
        _PROG_CACHE["ffn"] = build_l0ffn()
        _PROG_CACHE["attn0"] = build_l0attn()
        _PROG_CACHE["moe_emit"] = build_moe(ncores=8, counts_mode="emit")
        _PROG_CACHE["moe_in"] = build_moe(ncores=8, counts_mode="input")
        _PROG_CACHE["attn1"] = build_l1attn()
    return _PROG_CACHE


def kernel_unfused(**inputs):
    ins = {k: np.asarray(v) for k, v in inputs.items()}
    x = np.asarray(ins["L0_attn_x"], np.float32)
    nco = 8
    progs = _progs()
    w_ffn = prep_l0ffn_weights(ins)
    w_a0 = prep_l0attn_weights(ins)
    w_a1 = prep_l1attn_weights(ins)
    w_moe = prep_moe_weights(ins, core_id=0)
    moe_core = [dict(w_moe, coreid=np.full((1, 128), float(i), np.float32))
                for i in range(nco)]

    xt = [np.ascontiguousarray(x[i].T) for i in range(nco)]

    def launch(nc, w_percore):
        in_maps = [dict(w_percore[i], xin=xt[i]) for i in range(nco)]
        res = run_bass_kernel_spmd(nc, in_maps, core_ids=list(range(nco)))
        for i in range(nco):
            xt[i] = np.ascontiguousarray(np.asarray(res.results[i]["xout"]))

    def launch_moe():
        in_maps = [dict(moe_core[i], xin=xt[i]) for i in range(nco)]
        res = run_bass_kernel_spmd(progs["moe_emit"], in_maps,
                                   core_ids=list(range(nco)))
        g8 = np.ascontiguousarray(np.concatenate(
            [np.asarray(res.results[i]["cntout"]) for i in range(nco)], axis=0))
        in_maps = [dict(moe_core[i], xin=xt[i], g8in=g8) for i in range(nco)]
        res = run_bass_kernel_spmd(progs["moe_in"], in_maps,
                                   core_ids=list(range(nco)))
        for i in range(nco):
            xt[i] = np.ascontiguousarray(np.asarray(res.results[i]["xout"]))

    shared = lambda w: [w] * nco
    launch(progs["ffn"], shared(w_ffn))
    launch(progs["attn0"], shared(w_a0))
    launch(progs["ffn"], shared(w_ffn))
    launch_moe()
    launch(progs["attn1"], shared(w_a1))
    launch_moe()

    out = np.stack([xt[i].T for i in range(nco)], axis=0).astype(np.float32)
    return out



def build_chain1():
    nc = bass.Bass("TRN2", target_bir_lowering=False, dynamic_dma_scratch_size=16384)
    p = Prog(nc)
    xin = nc.dram_tensor("xin", [D, T], F32, kind="ExternalInput").ap()
    xb1 = nc.dram_tensor("xb1", [D, T], F32, kind="Internal").ap()
    xb2 = nc.dram_tensor("xb2", [D, T], F32, kind="Internal").ap()
    xmid = nc.dram_tensor("xmid1", [D, T], F32, kind="ExternalOutput").ap()
    build_l0ffn(nc=nc, p=p, seg="_a", xin_ap=xin, xout_ap=xb1, finalize=False)
    build_l0attn(nc=nc, p=p, seg="_b", xin_ap=xb1, xout_ap=xb2, finalize=False)
    build_l0ffn(nc=nc, p=p, seg="_c", xin_ap=xb2, xout_ap=xmid, finalize=False)
    build_moe(ncores=8, counts_mode="emit", nc=nc, p=p, seg="_d",
              xin_ap=xmid, finalize=False)
    nc.finalize()
    return nc


def build_chain2():
    nc = bass.Bass("TRN2", target_bir_lowering=False, dynamic_dma_scratch_size=16384)
    p = Prog(nc)
    xin = nc.dram_tensor("xin", [D, T], F32, kind="ExternalInput").ap()
    xb1 = nc.dram_tensor("xb1", [D, T], F32, kind="Internal").ap()
    xmid = nc.dram_tensor("xmid2", [D, T], F32, kind="ExternalOutput").ap()
    build_moe(ncores=8, counts_mode="input", nc=nc, p=p, seg="_e",
              xin_ap=xin, xout_ap=xb1, finalize=False)
    build_l1attn(nc=nc, p=p, seg="_f", xin_ap=xb1, xout_ap=xmid, finalize=False)
    build_moe(ncores=8, counts_mode="emit", nc=nc, p=p, seg="_g",
              xin_ap=xmid, finalize=False)
    nc.finalize()
    return nc


def build_chain3():
    nc = bass.Bass("TRN2", target_bir_lowering=False, dynamic_dma_scratch_size=16384)
    p = Prog(nc)
    xin = nc.dram_tensor("xin", [D, T], F32, kind="ExternalInput").ap()
    xout = nc.dram_tensor("xout", [D, T], F32, kind="ExternalOutput").ap()
    build_moe(ncores=8, counts_mode="input", nc=nc, p=p, seg="_h",
              xin_ap=xin, xout_ap=xout, finalize=False)
    nc.finalize()
    return nc


def _seg(w, seg):
    return {k + seg: v for k, v in w.items()}


def kernel_chained(**inputs):
    ins = {k: np.asarray(v) for k, v in inputs.items()}
    x = np.asarray(ins["L0_attn_x"], np.float32)
    nco = 8
    if "c1" not in _PROG_CACHE:
        _PROG_CACHE["c1"] = build_chain1()
        _PROG_CACHE["c2"] = build_chain2()
        _PROG_CACHE["c3"] = build_chain3()
    w_ffn = prep_l0ffn_weights(ins)
    w_a0 = prep_l0attn_weights(ins)
    w_a1 = prep_l1attn_weights(ins)
    w_moe = prep_moe_weights(ins, core_id=0)
    moe_core = [dict(w_moe, coreid=np.full((1, 128), float(i), np.float32))
                for i in range(nco)]
    xt = [np.ascontiguousarray(x[i].T) for i in range(nco)]

    c1_maps = [{**_seg(w_ffn, "_a"), **_seg(w_a0, "_b"), **_seg(w_ffn, "_c"),
                **_seg(moe_core[i], "_d"), "xin": xt[i]} for i in range(nco)]
    res = run_bass_kernel_spmd(_PROG_CACHE["c1"], c1_maps, core_ids=list(range(nco)))
    g8 = np.ascontiguousarray(np.concatenate(
        [np.asarray(res.results[i]["cntout_d"]) for i in range(nco)], axis=0))
    xm = [np.ascontiguousarray(np.asarray(res.results[i]["xmid1"])) for i in range(nco)]

    c2_maps = [{**_seg(dict(moe_core[i], g8in=g8), "_e"), **_seg(w_a1, "_f"),
                **_seg(moe_core[i], "_g"), "xin": xm[i]} for i in range(nco)]
    res = run_bass_kernel_spmd(_PROG_CACHE["c2"], c2_maps, core_ids=list(range(nco)))
    g82 = np.ascontiguousarray(np.concatenate(
        [np.asarray(res.results[i]["cntout_g"]) for i in range(nco)], axis=0))
    xm = [np.ascontiguousarray(np.asarray(res.results[i]["xmid2"])) for i in range(nco)]

    c3_maps = [{**_seg(dict(moe_core[i], g8in=g82), "_h"), "xin": xm[i]}
               for i in range(nco)]
    res = run_bass_kernel_spmd(_PROG_CACHE["c3"], c3_maps, core_ids=list(range(nco)))
    out = np.stack([np.asarray(res.results[i]["xout"]).T for i in range(nco)],
                   axis=0).astype(np.float32)
    return out


kernel = kernel_chained


# version 07 (final)
# speedup vs baseline: 1.1684x; 1.1684x over previous
import numpy as np
import ml_dtypes
import concourse.bass as bass
import concourse.mybir as mybir
from concourse.bass_utils import run_bass_kernel_spmd

F32 = mybir.dt.float32
BF16 = mybir.dt.bfloat16
FP16 = mybir.dt.float16
AF = mybir.ActivationFunctionType
ALU = mybir.AluOpType
AX = mybir.AxisListType

T = 2048
D = 1024
NCH = 8
TT = 512
NTT = T // TT
NEG = -3.0e38
EPS_LN = 1e-5


class _Call:
    __slots__ = ("name", "a", "k", "_then")
    def __init__(self, name, a, k):
        self.name = name; self.a = a; self.k = k; self._then = None
    def then_inc(self, sem, n=1):
        if self._then is not None:
            raise RuntimeError("then_inc already set on this op")
        self._then = (sem, n)
        return self
    def play(self, eng):
        try:
            r = getattr(eng, self.name)(*self.a, **self.k)
        except Exception as e:
            shapes = [getattr(a, "shape", None) for a in self.a]
            raise RuntimeError("op %s shapes=%s: %s" % (self.name, shapes, e)) from e
        if self._then is not None:
            r.then_inc(*self._then)


class _Rec:
    def __init__(self):
        self.ops = []
    def __getattr__(self, name):
        def call(*a, **k):
            c = _Call(name, a, k)
            self.ops.append(c)
            return c
        return call


class Prog:
    def __init__(self, nc):
        self.nc = nc
        self._sems = {}
        self.cnt = {}
        self.blkseg = ""

    def sem(self, name):
        if name not in self._sems:
            self._sems[name] = self.nc.semaphore(name).__enter__()
            self.cnt[name] = 0
        return self._sems[name]

    def bump(self, name, k=1):
        self.sem(name)
        self.cnt[name] += k
        return self.cnt[name]


class Ph:
    def __init__(self, prog, name):
        self.prog = prog
        self.name = name + prog.blkseg
        self.T = _Rec(); self.V = _Rec(); self.S = _Rec(); self.G = _Rec(); self.Y = _Rec()

    def emit(self):
        nc = self.prog.nc
        with nc.Block(self.name) as blk:
            if self.G.ops:
                @blk.gpsimd
                def _(e, ops=self.G.ops):
                    for c in ops: c.play(e)
            if self.T.ops:
                @blk.tensor
                def _(e, ops=self.T.ops):
                    for c in ops: c.play(e)
            if self.V.ops:
                @blk.vector
                def _(e, ops=self.V.ops):
                    for c in ops: c.play(e)
            if self.S.ops:
                @blk.scalar
                def _(e, ops=self.S.ops):
                    for c in ops: c.play(e)
            if self.Y.ops:
                @blk.sync
                def _(e, ops=self.Y.ops):
                    for c in ops: c.play(e)


class Banks:
    def __init__(self, tensors):
        self.b = [[t, None] for t in tensors]
        self.i = 0
    def take(self, ph, p):
        t, need = self.b[self.i]
        idx = self.i
        self.i = (self.i + 1) % len(self.b)
        if need is not None and need != "taken":
            ph.T.wait_ge(p.sem(need[0]), need[1])
        self.b[idx][1] = "taken"
        return idx, t
    def release(self, idx, sem_name, count):
        self.b[idx][1] = (sem_name, count)
    def reset(self):
        for e in self.b:
            e[1] = None
        self.i = 0


def wdma(ph, p, dst, src, sem="sW"):
    c = p.bump(sem, 16)
    ph.G.dma_start(dst, src).then_inc(p.sem(sem), 16)
    return c


def iodma(ph, p, dst, src, sem="sX"):
    c = p.bump(sem, 16)
    ph.G.dma_start(dst, src).then_inc(p.sem(sem), 16)
    return c



def host_constants():
    bf = ml_dtypes.bfloat16
    con = {}
    con["c_ident"] = np.eye(128, dtype=np.float32).astype(bf)
    m = np.zeros((2, 128, 512), np.float32)
    for r in range(2):
        il = r * 128 + np.arange(128)[:, None]
        jl = np.arange(512)[None, :]
        bad = (jl < il) | (jl > il + 256)
        m[r][bad] = NEG
    con["c_mask_w"] = m.reshape(256, 512)
    m0 = np.zeros((256, 256), np.float32)
    i0 = np.arange(256)[:, None]; j0 = np.arange(256)[None, :]
    m0[(j0 > i0) | (j0 < i0 - 256)] = NEG
    con["c_mask_w0"] = m0
    HD = 64; WINDOW = 256
    half = np.arange(0, HD, 2, dtype=np.float32)
    inv_freq = 1.0 / 10000.0 ** (half / HD)
    t = np.arange(T, dtype=np.float32)
    freqs = t[:, None] * inv_freq[None, :]
    freqs = np.concatenate([freqs, freqs], -1)
    base_scale = (half + 0.4 * HD) / (1.4 * HD)
    power = (t - T // 2) / (WINDOW // 2)
    scale = base_scale[None, :] ** power[:, None]
    scale = np.concatenate([scale, scale], -1)
    cos = np.cos(freqs); sin = np.sin(freqs)
    sign = np.ones(64, np.float32); sign[:32] = -1.0
    con["c_rot_qc"] = (cos * scale).T.astype(np.float32).copy()
    con["c_rot_qs"] = (sin * scale * sign[None, :]).T.astype(np.float32).copy()
    con["c_rot_kc"] = (cos / scale).T.astype(np.float32).copy()
    con["c_rot_ks"] = (sin / scale * sign[None, :]).T.astype(np.float32).copy()
    mc = np.zeros((4, 128, 512), np.float32)
    for r in range(4):
        qa = r * 128 + np.arange(128)[:, None]
        ka = np.arange(512)[None, :]
        mc[r][ka > qa] = NEG
    con["c_mask_causal"] = mc.reshape(512, 512)
    mcT = np.zeros((4, 128, 512), np.float32)
    for r in range(4):
        ja = r * 128 + np.arange(128)[:, None]
        ia = np.arange(512)[None, :]
        mcT[r][ja > ia] = NEG
    con["c_mask_causal_T"] = mcT.reshape(512, 512)
    con["c_iota8"] = np.broadcast_to(np.arange(8, dtype=np.float32), (128, 8)).copy()
    b8 = np.zeros((2, 128), np.float32); b8[0, 0:64] = 8.0; b8[1, 64:128] = 8.0
    b1 = np.zeros((2, 128), np.float32); b1[0, 0:64] = 1.0; b1[1, 64:128] = 1.0
    con["c_bsel8"] = b8.astype(ml_dtypes.bfloat16)
    con["c_bsel1"] = b1.astype(ml_dtypes.bfloat16)
    return con



def setup_consts(p, ph, sb):
    nc = p.nc
    ph.G.memset(sb["ones_bf"][:, :], 1.0)
    ph.G.memset(sb["ones_f"][:, :], 1.0)


def bcast_row(ph, p, banks, row_ap, sb, width=TT):
    idx, b = banks.take(ph, p)
    ct = p.bump("sT")
    ph.T.matmul(b[:, 0:width], sb["ones_f"][0:1, :], row_ap,
                start=True, stop=True).then_inc(p.sem("sT"), 1)
    return idx, b, ct




def emit_ln(p, ph, banks, sb, src_bf, sq, rows, apply_fn, m2row=None):
    sV, sS, sT = p.sem("sV"), p.sem("sS"), p.sem("sT")
    for c in range(NCH):
        cs = p.bump("sS")
        op = ph.S.activation(sq[:, c * T:(c + 1) * T], src_bf(c), AF.Square)
        op.then_inc(sS, 1)
    ph.T.wait_ge(sS, p.cnt["sS"])
    for tt in range(NTT):
        iS, bS = banks.take(ph, p)
        iQ, bQ = banks.take(ph, p)
        for c in range(NCH):
            ph.T.matmul(bS[0:1, 0:TT], sb["ones_bf"][:, 0:1],
                        src_bf(c)[:, tt * TT:(tt + 1) * TT],
                        start=(c == 0), stop=(c == NCH - 1))
        for c in range(NCH):
            op = ph.T.matmul(bQ[0:1, 0:TT], sb["ones_bf"][:, 0:1],
                             sq[:, c * T + tt * TT: c * T + (tt + 1) * TT],
                             start=(c == 0), stop=(c == NCH - 1))
        ct = p.bump("sT")
        op.then_inc(sT, 1)
        ph.V.wait_ge(sT, ct)
        mrow = rows[0:1, 0:TT]
        vrow = rows[0:1, TT:2 * TT]
        ph.V.tensor_scalar_mul(mrow, bS[0:1, 0:TT], 1.0 / D)
        cv = p.bump("sV")
        ph.V.tensor_scalar_mul(vrow, bQ[0:1, 0:TT], 1.0 / D).then_inc(sV, 1)
        ph.V.wait_ge(sV, cv)
        banks.release(iS, "sV", cv)
        banks.release(iQ, "sV", cv)
        ph.V.tensor_mul(m2row, mrow, mrow)
        cv = p.bump("sV")
        ph.V.scalar_tensor_tensor(vrow, vrow, EPS_LN, m2row,
                                  ALU.add, ALU.subtract).then_inc(sV, 1)
        ph.S.wait_ge(sV, cv)
        cs = p.bump("sS")
        ph.S.activation(vrow, vrow, AF.Sqrt).then_inc(sS, 1)
        ph.V.wait_ge(sS, cs)
        cv = p.bump("sV")
        ph.V.reciprocal(vrow, vrow).then_inc(sV, 1)
        ph.V.wait_ge(sV, cv)
        ph.T.wait_ge(sV, cv)
        im, bm, ctm = bcast_row(ph, p, banks, mrow, sb)
        ir, br, ctr = bcast_row(ph, p, banks, vrow, sb)
        ph.V.wait_ge(sT, ctr)
        for c in range(NCH):
            apply_fn(c, tt, bm[:, 0:TT], br[:, 0:TT])
        cv = p.bump("sV")
        ph.V.ops[-1].then_inc(sV, 1)
        banks.release(im, "sV", cv)
        banks.release(ir, "sV", cv)


def ao_slice(attn_out, hp, qt, h):
    return attn_out[h * 64:(h + 1) * 64, hp * T + qt * 128: hp * T + (qt + 1) * 128]


def emit_shift_cast(p, ph, sb, src, dst):
    for c in range(NCH):
        a = src[:, c * T:(c + 1) * T]
        d = dst[:, c * T:(c + 1) * T]
        if c < 4:
            ph.V.tensor_copy(d, a)
        else:
            ph.V.memset(d[:, 0:1], 0.0)
            ph.V.tensor_copy(d[:, 1:T], a[:, 0:T - 1])
    cv = p.bump("sV")
    ph.V.ops[-1].then_inc(p.sem("sV"), 1)
    return cv


def build_l0ffn(nl=6, stop_after=None, pair_level=9, dbg=None, npairs=8,
                nc=None, p=None, seg="", xin_ap=None, xout_ap=None, finalize=True):
    if nc is None:
        nc = bass.Bass("TRN2", target_bir_lowering=False,
                       dynamic_dma_scratch_size=16384)
        p = Prog(nc)
    p.blkseg = seg
    xin = xin_ap if xin_ap is not None else nc.dram_tensor(
        "xin" + seg, [D, T], F32, kind="ExternalInput").ap()
    xout = xout_ap if xout_ap is not None else nc.dram_tensor(
        "xout" + seg, [D, T], F32, kind="ExternalOutput").ap()
    wqkv = nc.dram_tensor("wqkv" + seg, [6, 8, 128, 3072], BF16, kind="ExternalInput").ap()
    wo = nc.dram_tensor("wo" + seg, [6, 8, 128, 1024], BF16, kind="ExternalInput").ap()
    w1m = nc.dram_tensor("w1m" + seg, [6, 22, 128, 2048], BF16, kind="ExternalInput").ap()
    w2 = nc.dram_tensor("w2" + seg, [6, 8, 128, 2816], BF16, kind="ExternalInput").ap()
    b1a = nc.dram_tensor("b1a" + seg, [6, 128, 22], F32, kind="ExternalInput").ap()
    b1g = nc.dram_tensor("b1g" + seg, [6, 128, 22], F32, kind="ExternalInput").ap()
    lng = nc.dram_tensor("lng" + seg, [6, 128, 8], F32, kind="ExternalInput").ap()
    lnb = nc.dram_tensor("lnb" + seg, [6, 128, 8], F32, kind="ExternalInput").ap()
    b2d = nc.dram_tensor("b2d" + seg, [6, 128, 8], F32, kind="ExternalInput").ap()
    qsd = nc.dram_tensor("qsd" + seg, [6, 64, 1], F32, kind="ExternalInput").ap()
    ksd = nc.dram_tensor("ksd" + seg, [6, 64, 1], F32, kind="ExternalInput").ap()
    identd = nc.dram_tensor("c_ident" + seg, [128, 128], BF16, kind="ExternalInput").ap()
    maskwd = nc.dram_tensor("c_mask_w" + seg, [128, 1024], F32, kind="ExternalInput").ap()
    bsel8d = nc.dram_tensor("c_bsel8" + seg, [2, 128], BF16, kind="ExternalInput").ap()
    bsel1d = nc.dram_tensor("c_bsel1" + seg, [2, 128], BF16, kind="ExternalInput").ap()
    mask0d = nc.dram_tensor("c_mask_w0" + seg, [128, 512], F32, kind="ExternalInput").ap()
    rotd = {}
    for nm in ("qc", "qs", "kc", "ks"):
        rotd[nm] = nc.dram_tensor("c_rot_" + nm + seg, [64, T], F32, kind="ExternalInput").ap()
    dbgd = [nc.dram_tensor("dbg%d" % i + seg, [128, T], F32, kind="ExternalOutput").ap()
            for i in range(4)] if dbg else None

    ctx = []
    def alloc(name, shape, dt, psum=False):
        name = name + seg
        h = nc.psum_tensor(name, shape, dt) if psum else nc.sbuf_tensor(name, shape, dt)
        t = h.__enter__()
        ctx.append(h)
        return t

    y = alloc("y", [128, NCH * T], F32)
    xs = alloc("xs", [128, NCH * T], BF16)
    arena = alloc("arena", [128, 45056], BF16)
    wbuf = alloc("wbuf", [128, 6144], BF16)
    rows = alloc("rows", [128, 2048], F32)
    colbuf = alloc("colbuf", [128, 80], F32)
    gtmp = alloc("gtmp", [128, 512], BF16)
    ident = alloc("ident", [128, 128], BF16)
    ones_bf = alloc("ones_bf", [128, 1], BF16)
    ones_f = alloc("ones_f", [1, 128], F32)
    sel2 = alloc("sel2", [128, 2], BF16)
    bsel8 = alloc("bsel8", [2, 128], BF16)
    bsel1 = alloc("bsel1", [2, 128], BF16)
    pb = [alloc("pb%d" % i, [128, TT], F32, psum=True) for i in range(7)]
    ptr = alloc("ptr", [128, 1024], BF16, psum=True)
    sb = dict(ones_bf=ones_bf, ones_f=ones_f)
    banks = Banks(pb)

    A = lambda off, ln: arena[:, off:off + ln]
    af32 = arena[:, :].bitcast(F32)
    qAB, kAB = A(0, T), A(T, T)
    qr, kr = A(2 * T, T), A(3 * T, T)
    swp = sqp = A(4 * T, T)
    vTf, vtok = A(5 * T, T), A(6 * T, T)
    P = A(7 * T, 512)
    PT = A(7 * T + 512, 512)
    msS = af32[:, 7 * T // 2 + 512: 7 * T // 2 + 1024]
    maskw = af32[:, 8192:9216]
    mask0 = af32[:, 9216:9728]
    attn_out = A(19456, NCH * T)
    qct, qst = A(36864, T), A(38912, T)
    kct, kst = A(40960, T), A(43008, T)


    sW, sX, sT, sV, sS = (p.sem(n) for n in ("sW", "sX", "sT", "sV", "sS"))
    state = dict(ptr_cv=0, P_free=0, ms_free=0, col_free=0)

    ph = Ph(p, "init")
    ph.G.memset(ones_bf[:, :], 1.0)
    ph.G.memset(ones_f[:, :], 1.0)
    ph.G.memset(sel2[:, :], 0.0)
    ph.G.memset(sel2[0:64, 0:1], 1.0)
    ph.G.memset(sel2[64:128, 1:2], 1.0)
    c1 = iodma(ph, p, ident[:, :], identd)
    iodma(ph, p, bsel8[:, :], bsel8d)
    iodma(ph, p, bsel1[:, :], bsel1d)
    c2 = iodma(ph, p, y[:, :].rearrange("p (c t) -> p c t", t=T),
               xin.rearrange("(c p) t -> p c t", p=128))
    ph.G.wait_ge(sX, p.cnt["sX"])
    ph.emit()

    ph = Ph(p, "outerln")
    cast = A(0, NCH * T)
    for c in range(NCH):
        ph.V.tensor_copy(cast[:, c * T:(c + 1) * T], y[:, c * T:(c + 1) * T])
    cv = p.bump("sV")
    ph.V.ops[-1].then_inc(sV, 1)
    ph.S.wait_ge(sV, cv)
    ph.T.wait_ge(sV, cv)

    def oln_apply(c, tt, mb, rb):
        sl = slice(c * T + tt * TT, c * T + (tt + 1) * TT)
        ph.V.tensor_sub(y[:, sl], y[:, sl], mb)
        ph.V.tensor_mul(y[:, sl], y[:, sl], rb)
    emit_ln(p, ph, banks, sb, lambda c: cast[:, c * T:(c + 1) * T],
            A(16384, NCH * T), rows, oln_apply, m2row=af32[0:1, 16384:16896])
    ph.emit()
    banks.reset()

    stops = {"ln": 0, "pre": 1, "pairs": 2, "wo": 3, "fln": 4, "fj": 5, "fw2": 6}
    stop = stops.get(stop_after, 99)
    for li in range(nl):
        if stop < 1:
            break
        ph = Ph(p, "a%d_pre" % li)
        cq = wdma(ph, p, colbuf[0:64, 0:1], qsd[li], sem="sWc")
        ck = wdma(ph, p, colbuf[0:64, 2:3], ksd[li], sem="sWc")
        crot = [wdma(ph, p, tbl[0:64, :], rotd[nm], sem="sWc")
                for tbl, nm in ((qct, "qc"), (qst, "qs"), (kct, "kc"), (kst, "ks"))]
        cm1 = iodma(ph, p, maskw, maskwd)
        cm2 = iodma(ph, p, mask0, mask0d)
        ph.G.wait_ge(sX, p.cnt["sX"])
        ph.V.wait_ge(p.sem("sWc"), p.cnt["sWc"])
        ph.V.tensor_copy(colbuf[0:32, 1:2], colbuf[32:64, 0:1])
        ph.V.tensor_copy(colbuf[32:64, 1:2], colbuf[0:32, 0:1])
        ph.V.tensor_scalar_mul(qct[0:64, :], qct[0:64, :], colbuf[0:64, 0:1])
        ph.V.tensor_scalar_mul(qst[0:64, :], qst[0:64, :], colbuf[0:64, 1:2])
        for tbl in (qct, qst, kct, kst):
            ph.V.tensor_copy(tbl[64:128, :], tbl[0:64, :])
        ph.V.tensor_copy(colbuf[0:32, 3:4], colbuf[32:64, 2:3])
        ph.V.tensor_copy(colbuf[32:64, 3:4], colbuf[0:32, 2:3])
        ph.V.tensor_scalar_mul(kct[0:64, :], kct[0:64, :], colbuf[0:64, 2:3])
        ph.V.tensor_scalar_mul(kst[0:64, :], kst[0:64, :], colbuf[0:64, 3:4])
        ph.V.tensor_copy(kct[64:128, :], kct[0:64, :])
        ph.V.tensor_copy(kst[64:128, :], kst[0:64, :])
        emit_shift_cast(p, ph, sb, y[:, :], xs[:, :])
        ph.emit()
        banks.reset()
        if stop < 2:
            break

        ph = Ph(p, "a%d_pairs" % li)
        state["ptr_cv"] = 0; state["P_free"] = 0; state["ms_free"] = 0; state["PT_free"] = 0

        def pair_dma(hp, slot):
            return wdma(ph, p, wbuf[:, slot * 3072:(slot + 1) * 3072],
                        wqkv[li, hp], sem="sW%d" % slot)

        wq_ct = {0: pair_dma(0, 0), 1: pair_dma(1, 1)}
        for hp in range(npairs):
            slot = hp % 2
            w4 = wbuf[:, slot * 3072:(slot + 1) * 3072].rearrange(
                "p (k n) -> p k n", n=384)
            ph.T.wait_ge(p.sem("sW%d" % slot), wq_ct[hp])
            evac_jobs = []
            for half in range(3):
                for tt in range(NTT):
                    bi, bk = banks.take(ph, p)
                    for kc in range(NCH):
                        lt = w4[:, kc, half * 128:(half + 1) * 128]
                        op = ph.T.matmul(bk[:, 0:TT], lt,
                                         xs[:, kc * T + tt * TT: kc * T + (tt + 1) * TT],
                                         start=(kc == 0), stop=(kc == NCH - 1))
                    ct = p.bump("sT")
                    op.then_inc(sT, 1)
                    sl = slice(tt * TT, (tt + 1) * TT)
                    ph.V.wait_ge(sT, ct)
                    if half == 0:
                        ph.V.tensor_copy(qAB[0:64, sl], bk[0:64, 0:TT])
                        ph.V.tensor_copy(kAB[0:64, sl], bk[64:128, 0:TT])
                    elif half == 1:
                        ph.V.tensor_copy(qAB[64:128, sl], bk[0:64, 0:TT])
                        ph.V.tensor_copy(kAB[64:128, sl], bk[64:128, 0:TT])
                    else:
                        ph.V.tensor_copy(vTf[:, sl], bk[:, 0:TT])
                    cv = p.bump("sV")
                    ph.V.ops[-1].then_inc(sV, 1)
                    banks.release(bi, "sV", cv)
            mm_done = p.cnt["sT"]
            qkv_evac = p.cnt["sV"]
            if dbg == "qkv" and hp == 0:
                ph.G.wait_ge(sV, qkv_evac)
                for i, src in enumerate((qAB, kAB, vTf)):
                    wdma(ph, p, dbgd[i], src)
            if hp + 2 < 8:
                ph.G.wait_ge(sT, mm_done)
                wq_ct[hp + 2] = pair_dma(hp + 2, slot)
            if pair_level < 2:
                continue

            for grp in range(2):
                if state["ptr_cv"]:
                    ph.T.wait_ge(sV, state["ptr_cv"])
                ph.T.wait_ge(sV, qkv_evac)
                for t8 in range(8):
                    ti = grp * 8 + t8
                    op = ph.T.transpose(ptr[:, t8 * 128:(t8 + 1) * 128],
                                        vTf[:, ti * 128:(ti + 1) * 128], ident[:, :])
                ct = p.bump("sT")
                op.then_inc(sT, 1)
                ph.V.wait_ge(sT, ct)
                for t8 in range(8):
                    ti = grp * 8 + t8
                    ph.V.tensor_copy(vtok[:, ti * 128:(ti + 1) * 128],
                                     ptr[:, t8 * 128:(t8 + 1) * 128])
                cv = p.bump("sV")
                ph.V.ops[-1].then_inc(sV, 1)
                state["ptr_cv"] = cv

            if pair_level < 3:
                continue
            for (src, dst, ctab, stab, bsel) in (
                    (qAB, qr, qct, qst, bsel8),
                    (kAB, kr, kct, kst, bsel1)):
                ph.S.wait_ge(sV, max(qkv_evac, p.cnt["sV"]))
                cs = p.bump("sS")
                ph.S.activation(sqp[:, :], src[:, :], AF.Square).then_inc(sS, 1)
                ph.T.wait_ge(sS, cs)
                nb = []
                for tt in range(NTT):
                    bi, bk = banks.take(ph, p)
                    op = ph.T.matmul(bk[0:2, 0:TT], sel2[:, :],
                                     sqp[:, tt * TT:(tt + 1) * TT],
                                     start=True, stop=True)
                    nb.append((bi, bk))
                ct_nm = p.bump("sT")
                op.then_inc(sT, 1)
                ph.V.wait_ge(sT, ct_nm)
                for tt in range(NTT):
                    bi, bk = nb[tt]
                    cv = p.bump("sV")
                    ph.V.tensor_copy(rows[0:2, tt * TT:(tt + 1) * TT],
                                     bk[0:2, 0:TT]).then_inc(sV, 1)
                    banks.release(bi, "sV", cv)
                ph.S.wait_ge(sV, p.cnt["sV"])
                cs = p.bump("sS")
                ph.S.activation(rows[0:2, 0:T], rows[0:2, 0:T], AF.Sqrt).then_inc(sS, 1)
                ph.V.wait_ge(sS, cs)
                cvm = p.bump("sV")
                ph.V.tensor_scalar_max(rows[0:2, 0:T], rows[0:2, 0:T],
                                       1e-12).then_inc(sV, 1)
                ph.V.wait_ge(sV, cvm)
                cv = p.bump("sV")
                ph.V.reciprocal(rows[0:2, 0:T], rows[0:2, 0:T]).then_inc(sV, 1)
                ph.V.wait_ge(sV, cv)
                rn_done = cv
                ph.G.wait_ge(sT, ct_nm)
                ph.G.wait_ge(sV, p.cnt["sV"])
                for (a, b) in ((0, 32), (32, 0), (64, 96), (96, 64)):
                    ph.G.tensor_copy(swp[a:a + 32, :], src[b:b + 32, :])
                cg = p.bump("sG")
                ph.G.ops[-1].then_inc(p.sem("sG"), 1)
                ph.V.wait_ge(p.sem("sG"), cg)
                ph.V.tensor_mul(dst[:, :], src[:, :], ctab[:, :])
                ph.V.tensor_mul(swp[:, :], swp[:, :], stab[:, :])
                cvd = p.bump("sV")
                ph.V.tensor_add(swp[:, :], dst[:, :], swp[:, :]).then_inc(sV, 1)
                ph.V.wait_ge(sV, cvd)
                ph.T.wait_ge(sV, rn_done)
                for tt in range(NTT):
                    bi, bk = banks.take(ph, p)
                    sl = slice(tt * TT, (tt + 1) * TT)
                    gslot = gtmp[0:2, 0:TT]
                    if tt > 0:
                        ph.V.wait_ge(sT, p.cnt["sT"])
                    cv0 = p.bump("sV")
                    ph.V.tensor_copy(gslot, rows[0:2, sl]).then_inc(sV, 1)
                    ph.T.wait_ge(sV, cv0)
                    ct = p.bump("sT")
                    ph.T.matmul(bk[:, 0:TT], bsel[:, :], gslot,
                                start=True, stop=True).then_inc(sT, 1)
                    ph.V.wait_ge(sT, ct)
                    cv = p.bump("sV")
                    ph.V.tensor_mul(dst[:, sl], swp[:, sl], bk[:, 0:TT]).then_inc(sV, 1)
                    banks.release(bi, "sV", cv)
            stage_done = p.cnt["sV"]
            if dbg == "stage" and hp == 0:
                ph.G.wait_ge(sV, stage_done)
                wdma(ph, p, dbgd[0], qr)
                wdma(ph, p, dbgd[1], kr)
                wdma(ph, p, dbgd[2], vtok)
            if pair_level < 5:
                continue
            ph.T.wait_ge(sV, stage_done)
            P_s = (P, A(4 * T, 512))
            PT_s = (PT, A(4 * T + 512, 512))
            ms_s = (msS, af32[:, 2 * T + 512: 2 * T + 1024])
            units = [(qt, hh) for qt in ([0] if dbg == "sm" else range(16))
                     for hh in ([0] if dbg == "sm" else range(2))]
            st1 = {}
            P_free = [0, 0]
            PDEPTH = int((dbg or "x")[2:]) if (dbg or "").startswith("pd") else 1
            for i in range(len(units) + PDEPTH):
                if i < len(units):
                    qt, h = units[i]
                    w = qt // 2
                    r = qt % 2
                    if w == 0:
                        span, koff = 256, 0
                        mview = mask0[:, r * 256:(r + 1) * 256]
                    else:
                        span, koff = 512, (w - 1) * 256
                        mview = maskw[:, r * 512:(r + 1) * 512]
                    sl1 = i % 2
                    hs = slice(h * 64, (h + 1) * 64)
                    bi, bk_ = banks.take(ph, p)
                    ct1 = p.bump("sT")
                    ph.T.matmul(bk_[:, 0:span], qr[hs, qt * 128:(qt + 1) * 128],
                                kr[hs, koff:koff + span],
                                start=True, stop=True).then_inc(sT, 1)
                    ph.V.wait_ge(sT, ct1)
                    if st1.get("ms_busy%d" % sl1):
                        ph.V.wait_ge(p.sem("sS"), st1["ms_busy%d" % sl1])
                    ph.V.tensor_add(ms_s[sl1][:, 0:span], bk_[:, 0:span], mview)
                    cv1 = p.bump("sV")
                    ph.V.reduce_max(colbuf[:, 9 + 4 * sl1:10 + 4 * sl1],
                                    ms_s[sl1][:, 0:span], AX.X,
                                    negate=True).then_inc(sV, 1)
                    ph.V.wait_ge(sV, cv1)
                    banks.release(bi, "sV", cv1)
                    st1[i] = (cv1, span, koff)
                if i >= PDEPTH:
                    j = i - PDEPTH
                    qt, h = units[j]
                    sl = j % 2
                    cv1, span, koff = st1[j]
                    hs = slice(h * 64, (h + 1) * 64)
                    ph.S.wait_ge(sV, cv1)
                    if P_free[sl]:
                        ph.S.wait_ge(sT, P_free[sl])
                    cs1 = p.bump("sS")
                    ph.S.activation(P_s[sl][:, 0:span], ms_s[sl][:, 0:span], AF.Exp,
                                    bias=colbuf[:, 9 + 4 * sl:10 + 4 * sl],
                                    accum_out=colbuf[:, 10 + 4 * sl:11 + 4 * sl]
                                    ).then_inc(sS, 1)
                    st1["ms_busy%d" % sl] = cs1
                    ph.V.wait_ge(sS, cs1)
                    cvr = p.bump("sV")
                    ph.V.reciprocal(colbuf[:, 11 + 4 * sl:12 + 4 * sl],
                                    colbuf[:, 10 + 4 * sl:11 + 4 * sl]).then_inc(sV, 1)
                    ph.V.wait_ge(sV, cvr)
                    cv2 = p.bump("sV")
                    ph.V.tensor_scalar_mul(P_s[sl][:, 0:span], P_s[sl][:, 0:span],
                                           colbuf[:, 11 + 4 * sl:12 + 4 * sl]
                                           ).then_inc(sV, 1)
                    ph.T.wait_ge(sV, cv2)
                    if state["ptr_cv"]:
                        ph.T.wait_ge(sV, state["ptr_cv"])
                    nk = span // 128
                    for kc in range(nk):
                        op = ph.T.transpose(ptr[:, kc * 128:(kc + 1) * 128],
                                            P_s[sl][:, kc * 128:(kc + 1) * 128],
                                            ident[:, :])
                    ct2 = p.bump("sT")
                    op.then_inc(sT, 1)
                    P_free[sl] = ct2
                    ph.V.wait_ge(sT, ct2)
                    if st1.get("PT_free%d" % sl):
                        ph.V.wait_ge(sT, st1["PT_free%d" % sl])
                    for kc in range(nk):
                        ph.V.tensor_copy(PT_s[sl][:, kc * 128:(kc + 1) * 128],
                                         ptr[:, kc * 128:(kc + 1) * 128])
                    cv3 = p.bump("sV")
                    ph.V.ops[-1].then_inc(sV, 1)
                    state["ptr_cv"] = cv3
                    ph.T.wait_ge(sV, cv3)
                    bo_i, bo_bk = banks.take(ph, p)
                    for kc in range(nk):
                        kb = koff // 128 + kc
                        op = ph.T.matmul(bo_bk[0:64, 0:128],
                                         vtok[:, kb * 128 + h * 64: kb * 128 + h * 64 + 64],
                                         PT_s[sl][:, kc * 128:(kc + 1) * 128],
                                         start=(kc == 0), stop=(kc == nk - 1))
                    ct3 = p.bump("sT")
                    op.then_inc(sT, 1)
                    st1["PT_free%d" % sl] = ct3
                    ph.V.wait_ge(sT, ct3)
                    cv4 = p.bump("sV")
                    ph.V.tensor_copy(ao_slice(attn_out, hp, qt, h),
                                     bo_bk[0:64, 0:128]).then_inc(sV, 1)
                    banks.release(bo_i, "sV", cv4)

        if dbg == "ypre":
            ph.G.wait_ge(sV, p.cnt["sV"])
            wdma(ph, p, dbgd[0], y[:, 0:T])
            wdma(ph, p, dbgd[1], y[:, T:2 * T])
            wdma(ph, p, dbgd[2], y[:, 2 * T:3 * T])
        if dbg in ("attn", "attn1"):
            off = T if dbg == "attn1" else 0
            ph.G.wait_ge(sV, p.cnt["sV"])
            for i in range(4):
                wdma(ph, p, dbgd[i],
                     arena[:, 19456 + i * 2 * T + off: 19456 + i * 2 * T + off + T])
        ph.emit()
        banks.reset()
        if stop < 3:
            break

        ph = Ph(p, "a%d_wo" % li)
        def wo_dma(nn, slot):
            return wdma(ph, p, wbuf[:, slot * 1024:(slot + 1) * 1024],
                        wo[li, nn], sem="sW%d" % slot)
        wct = {0: wo_dma(0, 0), 1: wo_dma(1, 1)}
        if dbg == "wo":
            ph.G.wait_ge(p.sem("sW0"), wct[0])
            wdma(ph, p, dbgd[0][:, 0:1024], wbuf[:, 0:1024])
            ph.G.wait_ge(p.sem("sW1"), wct[1])
            wdma(ph, p, dbgd[1][:, 0:1024], wbuf[:, 1024:2048])
        for nn in range(NCH):
            slot = nn % 2
            w3 = wbuf[:, slot * 1024:(slot + 1) * 1024].rearrange("p (k n) -> p k n", n=128)
            ph.T.wait_ge(p.sem("sW%d" % slot), wct[nn])
            for tt in range(NTT):
                bi, bk = banks.take(ph, p)
                for kc in range(NCH):
                    op = ph.T.matmul(bk[:, 0:TT], w3[:, kc, :],
                                     attn_out[:, kc * T + tt * TT: kc * T + (tt + 1) * TT],
                                     start=(kc == 0), stop=(kc == NCH - 1))
                ct = p.bump("sT")
                op.then_inc(sT, 1)
                sl = slice(nn * T + tt * TT, nn * T + (tt + 1) * TT)
                ph.V.wait_ge(sT, ct)
                cv = p.bump("sV")
                ph.V.tensor_add(y[:, sl], y[:, sl], bk[:, 0:TT]).then_inc(sV, 1)
                banks.release(bi, "sV", cv)
            if nn + 2 < NCH:
                ph.G.wait_ge(sT, p.cnt["sT"])
                wct[nn + 2] = wo_dma(nn + 2, slot)
        ph.emit()
        banks.reset()
        if stop < 4:
            break

        ph = Ph(p, "f%d_ln" % li)
        cb1 = wdma(ph, p, colbuf[:, 32:54], b1a[li], sem="sWc")
        cb2 = wdma(ph, p, colbuf[:, 54:76], b1g[li], sem="sWc")
        cb3 = wdma(ph, p, colbuf[:, 8:16], lng[li], sem="sWc")
        cb4 = wdma(ph, p, colbuf[:, 16:24], lnb[li], sem="sWc")
        cb5 = wdma(ph, p, colbuf[:, 24:32], b2d[li], sem="sWc")
        cshift = emit_shift_cast(p, ph, sb, y[:, :], xs[:, :])
        ph.V.wait_ge(p.sem("sWc"), p.cnt["sWc"])
        ph.S.wait_ge(sV, cshift)
        ph.T.wait_ge(sV, cshift)

        def fln_apply(c, tt, mb, rb):
            sl = slice(c * T + tt * TT, c * T + (tt + 1) * TT)
            ph.V.tensor_sub(msS[:, 0:TT], xs[:, sl], mb)
            ph.V.tensor_mul(msS[:, 0:TT], msS[:, 0:TT], rb)
            ph.V.tensor_scalar(xs[:, sl], msS[:, 0:TT], colbuf[:, 8 + c:9 + c],
                               colbuf[:, 16 + c:17 + c], ALU.mult, ALU.add)
        emit_ln(p, ph, banks, sb, lambda c: xs[:, c * T:(c + 1) * T],
                A(16384, NCH * T), rows, fln_apply, m2row=af32[0:1, 16384:16896])
        if dbg == "fln":
            ph.G.wait_ge(sV, p.cnt["sV"])
            for i in range(4):
                wdma(ph, p, dbgd[i], xs[:, i * T:(i + 1) * T])
        if dbg == "ytrace":
            ph.G.wait_ge(sV, p.cnt["sV"])
            wdma(ph, p, dbgd[1], y[:, 0:T])
        ph.emit()
        banks.reset()
        if stop < 5:
            break

        ph = Ph(p, "f%d_j" % li)
        NJ = 22

        def w1_dma(j, slot):
            return wdma(ph, p, wbuf[:, slot * 2048:(slot + 1) * 2048],
                        w1m[li, j], sem="sW%d" % slot)

        w1ct = {0: w1_dma(0, 0), 1: w1_dma(1, 1)}
        gslot_cv = [0, 0]
        fjl = 9
        if dbg and dbg.startswith("fj") and len(dbg) == 3:
            fjl = int(dbg[2])
        for j in range(NJ):
            slot = j % 2
            wa = wbuf[:, slot * 2048:slot * 2048 + 1024].rearrange("p (k n) -> p k n", n=128)
            wg = wbuf[:, slot * 2048 + 1024:(slot + 1) * 2048].rearrange("p (k n) -> p k n", n=128)
            ph.T.wait_ge(p.sem("sW%d" % slot), w1ct[j])
            for tt in range(NTT):
                if fjl < 2:
                    continue
                bia, bka = banks.take(ph, p)
                for kc in range(NCH):
                    ph.T.matmul(bka[:, 0:TT], wa[:, kc, :],
                                xs[:, kc * T + tt * TT: kc * T + (tt + 1) * TT],
                                start=(kc == 0), stop=(kc == NCH - 1))
                big, bkg = banks.take(ph, p)
                for kc in range(NCH):
                    op = ph.T.matmul(bkg[:, 0:TT], wg[:, kc, :],
                                     xs[:, kc * T + tt * TT: kc * T + (tt + 1) * TT],
                                     start=(kc == 0), stop=(kc == NCH - 1))
                ct = p.bump("sT")
                op.then_inc(sT, 1)
                gs = 0
                if fjl < 3:
                    cv = p.bump("sV")
                    ph.V.wait_ge(sT, ct)
                    ph.V.tensor_copy(arena[:, j * T + tt * TT: j * T + (tt + 1) * TT],
                                     bka[:, 0:TT]).then_inc(sV, 1)
                    banks.release(bia, "sV", cv)
                    banks.release(big, "sV", cv)
                    continue
                ph.S.wait_ge(sT, ct)
                if gslot_cv[gs]:
                    ph.S.wait_ge(sV, gslot_cv[gs])
                cs = p.bump("sS")
                ph.S.activation(gtmp[:, gs * TT:(gs + 1) * TT], bkg[:, 0:TT], AF.Gelu,
                                bias=colbuf[:, 54 + j:55 + j]).then_inc(sS, 1)
                banks.release(big, "sS", cs)
                ph.V.wait_ge(sS, cs)
                cv = p.bump("sV")
                if fjl < 4:
                    ph.V.tensor_copy(arena[:, j * T + tt * TT: j * T + (tt + 1) * TT],
                                     gtmp[:, gs * TT:(gs + 1) * TT]).then_inc(sV, 1)
                else:
                    ph.V.scalar_tensor_tensor(
                        arena[:, j * T + tt * TT: j * T + (tt + 1) * TT],
                        bka[:, 0:TT], colbuf[:, 32 + j:33 + j],
                        gtmp[:, gs * TT:(gs + 1) * TT],
                        ALU.add, ALU.mult).then_inc(sV, 1)
                banks.release(bia, "sV", cv)
                gslot_cv[gs] = cv
            if j + 2 < NJ:
                ph.G.wait_ge(sT, p.cnt["sT"])
                w1ct[j + 2] = w1_dma(j + 2, slot)
        if dbg == "fj":
            ph.G.wait_ge(sV, p.cnt["sV"])
            for i in range(4):
                wdma(ph, p, dbgd[i], arena[:, i * T:(i + 1) * T])
        if dbg == "ytrace" or (dbg and dbg.startswith("fj") and len(dbg) == 3):
            ph.G.wait_ge(sV, p.cnt["sV"])
            ph.G.wait_ge(sT, p.cnt["sT"])
            wdma(ph, p, dbgd[2], y[:, 0:T])
        ph.emit()
        banks.reset()
        if stop < 6:
            break

        ph = Ph(p, "f%d_w2" % li)

        def w2_dma(nn, slot):
            return wdma(ph, p, wbuf[:, slot * 2816:(slot + 1) * 2816],
                        w2[li, nn], sem="sW%d" % slot)

        w2ct = {0: w2_dma(0, 0), 1: w2_dma(1, 1)}
        for nn in range(NCH):
            slot = nn % 2
            wv = wbuf[:, slot * 2816:(slot + 1) * 2816].rearrange("p (k n) -> p k n", n=128)
            ph.T.wait_ge(p.sem("sW%d" % slot), w2ct[nn])
            for tt in range(NTT):
                bi, bk = banks.take(ph, p)
                for kc in range(NJ):
                    op = ph.T.matmul(bk[:, 0:TT], wv[:, kc, :],
                                     arena[:, kc * T + tt * TT: kc * T + (tt + 1) * TT],
                                     start=(kc == 0), stop=(kc == NJ - 1))
                ct = p.bump("sT")
                op.then_inc(sT, 1)
                sl = slice(nn * T + tt * TT, nn * T + (tt + 1) * TT)
                ph.V.wait_ge(sT, ct)
                cv = p.bump("sV")
                ph.V.scalar_tensor_tensor(y[:, sl], bk[:, 0:TT],
                                          colbuf[:, 24 + nn:25 + nn], y[:, sl],
                                          ALU.add, ALU.add).then_inc(sV, 1)
                banks.release(bi, "sV", cv)
            if nn + 2 < NCH:
                ph.G.wait_ge(sT, p.cnt["sT"])
                w2ct[nn + 2] = w2_dma(nn + 2, slot)
        ph.emit()
        banks.reset()

    ph = Ph(p, "final")
    xi = xin.rearrange("(c p) t -> p c t", p=128)
    xo = xout.rearrange("(c p) t -> p c t", p=128)
    xtmp = af32[:, 0:2 * T]
    out_c = af32[:, 2 * T:4 * T]
    st_cnt = {}
    for c in range(NCH):
        s2 = (c % 2) * T
        cx = iodma(ph, p, xtmp[:, s2:s2 + T], xi[:, c, :], sem="sXin")
        ph.V.wait_ge(p.sem("sXin"), cx)
        if c >= 2:
            ph.V.wait_ge(p.sem("sXout"), st_cnt[c - 1])
        cv = p.bump("sV")
        ph.V.scalar_tensor_tensor(out_c[:, s2:s2 + T], y[:, c * T:(c + 1) * T], 0.5,
                                  xtmp[:, s2:s2 + T], ALU.mult, ALU.add).then_inc(sV, 1)
        ph.G.wait_ge(sV, cv)
        if dbg == "fin" and c == 0:
            wdma(ph, p, dbgd[0], xtmp[:, 0:T])
            wdma(ph, p, dbgd[1], y[:, 0:T])
            wdma(ph, p, dbgd[2], out_c[:, 0:T])
        st_cnt[c] = iodma(ph, p, xo[:, c, :], out_c[:, s2:s2 + T], sem="sXout")
    ph.G.wait_ge(p.sem("sXout"), p.cnt["sXout"])
    ph.emit()

    for h in reversed(ctx):
        h.__exit__(None, None, None)
    if finalize:
        nc.finalize()
    return nc



def _col6(a, nj):
    a = np.asarray(a, np.float32)
    return np.ascontiguousarray(a.reshape(6, nj, 128).transpose(0, 2, 1))


def prep_l0ffn_weights(ins):
    f32 = np.float32
    W1 = np.asarray(ins["L0_ffn_W1"], f32)
    b1 = np.asarray(ins["L0_ffn_b1"], f32)
    W2 = np.asarray(ins["L0_ffn_W2"], f32)
    w1a = np.zeros((6, D, 2816), f32); w1a[:, :, :2730] = W1[:, :, :2730]
    w1g = np.zeros((6, D, 2816), f32); w1g[:, :, :2730] = W1[:, :, 2730:]
    b1a = np.zeros((6, 2816), f32); b1a[:, :2730] = b1[:, :2730]
    b1g = np.zeros((6, 2816), f32); b1g[:, :2730] = b1[:, 2730:]
    w2p = np.zeros((6, 2816, D), f32); w2p[:, :2730] = W2
    con = host_constants()
    Wqkv = np.asarray(ins["L0_ffn_Wqkv"], f32)
    q_ = Wqkv[:, :, 0:1024].reshape(6, 8, 128, 16, 64)
    k_ = Wqkv[:, :, 1024:2048].reshape(6, 8, 128, 16, 64)
    v_ = Wqkv[:, :, 2048:3072].reshape(6, 8, 128, 16, 64)
    wq_t = np.empty((6, 8, 128, 8, 384), f32)
    for hp in range(8):
        wq_t[:, hp, :, :, 0:64] = q_[:, :, :, 2 * hp].transpose(0, 2, 1, 3)
        wq_t[:, hp, :, :, 64:128] = k_[:, :, :, 2 * hp].transpose(0, 2, 1, 3)
        wq_t[:, hp, :, :, 128:192] = q_[:, :, :, 2 * hp + 1].transpose(0, 2, 1, 3)
        wq_t[:, hp, :, :, 192:256] = k_[:, :, :, 2 * hp + 1].transpose(0, 2, 1, 3)
        wq_t[:, hp, :, :, 256:320] = v_[:, :, :, 2 * hp].transpose(0, 2, 1, 3)
        wq_t[:, hp, :, :, 320:384] = v_[:, :, :, 2 * hp + 1].transpose(0, 2, 1, 3)
    Wo_ = np.asarray(ins["L0_ffn_Wo"], f32)
    wo_t = np.ascontiguousarray(
        Wo_.reshape(6, 8, 128, 8, 128).transpose(0, 3, 2, 1, 4)).reshape(6, 8, 128, 1024)
    w1a_t = np.ascontiguousarray(
        w1a.reshape(6, 8, 128, 22, 128).transpose(0, 3, 2, 1, 4)).reshape(6, 22, 128, 1024)
    w1g_t = np.ascontiguousarray(
        w1g.reshape(6, 8, 128, 22, 128).transpose(0, 3, 2, 1, 4)).reshape(6, 22, 128, 1024)
    w2_t = np.ascontiguousarray(
        w2p.reshape(6, 22, 128, 8, 128).transpose(0, 3, 2, 1, 4)).reshape(6, 8, 128, 2816)
    bf = ml_dtypes.bfloat16
    d = dict(
        wqkv=np.ascontiguousarray(wq_t).reshape(6, 8, 128, 3072).astype(bf),
        wo=wo_t.astype(bf),
        w1m=np.concatenate([w1a_t, w1g_t], axis=-1).astype(bf), w2=w2_t.astype(bf),
        b1a=_col6(b1a, 22), b1g=_col6(b1g, 22),
        lng=_col6(ins["L0_ffn_ln_g"], 8), lnb=_col6(ins["L0_ffn_ln_b"], 8),
        b2d=_col6(ins["L0_ffn_b2"], 8),
        qsd=np.ascontiguousarray(np.asarray(ins["L0_ffn_q_scale"], f32).reshape(6, 64, 1)),
        ksd=np.ascontiguousarray(np.asarray(ins["L0_ffn_k_scale"], f32).reshape(6, 64, 1)),
        c_ident=con["c_ident"],
        c_mask_w=np.ascontiguousarray(
            con["c_mask_w"].reshape(2, 128, 512).transpose(1, 0, 2).reshape(128, 1024)),
        c_mask_w0=np.ascontiguousarray(
            con["c_mask_w0"].reshape(2, 128, 256).transpose(1, 0, 2).reshape(128, 512)),
        c_rot_qc=con["c_rot_qc"], c_rot_qs=con["c_rot_qs"],
        c_rot_kc=con["c_rot_kc"], c_rot_ks=con["c_rot_ks"],
        c_bsel8=con["c_bsel8"], c_bsel1=con["c_bsel1"],
    )
    return d


def emit_ln_stream(p, ph, banks, sb, xi_r, rows, xt_slots, cbuf, sq4, ln_out,
                   m2row, extra_fn=None, ln_sem="sXin", fp32_stats=False,
                   sq4f=None, onesc_f=None):
    sV, sS, sT = p.sem("sV"), p.sem("sS"), p.sem("sT")
    single = xt_slots[0] is xt_slots[1]
    prev_cv = 0
    for tt in range(NTT):
        slot = 0 if single else tt % 2
        xt = xt_slots[slot]
        if single and prev_cv:
            ph.G.wait_ge(sV, prev_cv)
        cx = iodma(ph, p, xt[:, :].rearrange("p (c t) -> p c t", t=TT),
                   xi_r[:, :, tt * TT:(tt + 1) * TT], sem=ln_sem + str(slot))
        ph.V.wait_ge(p.sem(ln_sem + str(slot)), cx)
        cv = p.bump("sV")
        ph.V.tensor_copy(cbuf[:, :], xt[:, :]).then_inc(sV, 1)
        ph.S.wait_ge(sV, cv)
        iS, bS = banks.take(ph, p)
        iQ, bQ = banks.take(ph, p)
        if fp32_stats:
            for c in range(NCH):
                if c > 0:
                    ph.S.wait_ge(sT, p.cnt["sT"])
                cs = p.bump("sS")
                ph.S.activation(sq4f[:, 0:TT], xt[:, c * TT:(c + 1) * TT],
                                AF.Square).then_inc(sS, 1)
                ph.T.wait_ge(sS, cs)
                ph.T.matmul(bS[0:1, 0:TT], onesc_f[:, 0:1],
                            xt[:, c * TT:(c + 1) * TT],
                            start=(c == 0), stop=(c == NCH - 1))
                ctq = p.bump("sT")
                ph.T.matmul(bQ[0:1, 0:TT], onesc_f[:, 0:1],
                            sq4f[:, 0:TT],
                            start=(c == 0), stop=(c == NCH - 1)).then_inc(sT, 1)
            op = ph.T.nop()
            ct = ctq
        else:
            cs = p.bump("sS")
            ph.S.activation(sq4[:, :], cbuf[:, :], AF.Square).then_inc(sS, 1)
            ph.T.wait_ge(sS, cs)
            for c in range(NCH):
                ph.T.matmul(bS[0:1, 0:TT], sb["ones_bf"][:, 0:1],
                            cbuf[:, c * TT:(c + 1) * TT],
                            start=(c == 0), stop=(c == NCH - 1))
            for c in range(NCH):
                op = ph.T.matmul(bQ[0:1, 0:TT], sb["ones_bf"][:, 0:1],
                                 sq4[:, c * TT:(c + 1) * TT],
                                 start=(c == 0), stop=(c == NCH - 1))
            ct = p.bump("sT")
            op.then_inc(sT, 1)
        ph.V.wait_ge(sT, ct)
        mrow = rows[0:1, 0:TT]
        vrow = rows[0:1, TT:2 * TT]
        ph.V.tensor_scalar_mul(mrow, bS[0:1, 0:TT], 1.0 / D)
        cv = p.bump("sV")
        ph.V.tensor_scalar_mul(vrow, bQ[0:1, 0:TT], 1.0 / D).then_inc(sV, 1)
        ph.V.wait_ge(sV, cv)
        banks.release(iS, "sV", cv)
        banks.release(iQ, "sV", cv)
        ph.V.tensor_mul(m2row, mrow, mrow)
        cv = p.bump("sV")
        ph.V.scalar_tensor_tensor(vrow, vrow, EPS_LN, m2row,
                                  ALU.add, ALU.subtract).then_inc(sV, 1)
        ph.S.wait_ge(sV, cv)
        cs = p.bump("sS")
        ph.S.activation(vrow, vrow, AF.Sqrt).then_inc(sS, 1)
        ph.V.wait_ge(sS, cs)
        cv = p.bump("sV")
        ph.V.reciprocal(vrow, vrow).then_inc(sV, 1)
        ph.V.wait_ge(sV, cv)
        ph.T.wait_ge(sV, cv)
        im, bm, ctm = bcast_row(ph, p, banks, mrow, sb)
        ir, br, ctr = bcast_row(ph, p, banks, vrow, sb)
        ph.V.wait_ge(sT, ctr)
        for c in range(NCH):
            sl = slice(c * T + tt * TT, c * T + (tt + 1) * TT)
            ph.V.tensor_sub(sq4[:, c * TT:(c + 1) * TT], xt[:, c * TT:(c + 1) * TT],
                            bm[:, 0:TT])
            ph.V.tensor_mul(ln_out[:, sl], sq4[:, c * TT:(c + 1) * TT], br[:, 0:TT])
        cv = p.bump("sV")
        ph.V.ops[-1].then_inc(sV, 1)
        banks.release(im, "sV", cv)
        banks.release(ir, "sV", cv)
        if extra_fn is not None:
            extra_fn(tt, xt, mrow, vrow)
        prev_cv = p.cnt["sV"]


def build_l0attn(dbg=None,
                 nc=None, p=None, seg="", xin_ap=None, xout_ap=None, finalize=True):
    if nc is None:
        nc = bass.Bass("TRN2", target_bir_lowering=False)
        p = Prog(nc)
    HD = 64
    p.blkseg = seg
    xin = xin_ap if xin_ap is not None else nc.dram_tensor(
        "xin" + seg, [D, T], F32, kind="ExternalInput").ap()
    xout = xout_ap if xout_ap is not None else nc.dram_tensor(
        "xout" + seg, [D, T], F32, kind="ExternalOutput").ap()
    wqkv = nc.dram_tensor("wqkv" + seg, [16, 128, 1536], BF16, kind="ExternalInput").ap()
    bqkv = nc.dram_tensor("bqkv" + seg, [16, 64, 4], F32, kind="ExternalInput").ap()
    wod = nc.dram_tensor("wod" + seg, [8, 128, 1024], BF16, kind="ExternalInput").ap()
    bod = nc.dram_tensor("bod" + seg, [128, 8], F32, kind="ExternalInput").ap()
    identd = nc.dram_tensor("c_ident" + seg, [128, 128], BF16, kind="ExternalInput").ap()
    dbgd = [nc.dram_tensor("dbg%d" % i + seg, [128, T], F32, kind="ExternalOutput").ap()
            for i in range(4)] if dbg else None

    ctx = []
    def alloc(name, shape, dt, psum=False):
        name = name + seg
        h = nc.psum_tensor(name, shape, dt) if psum else nc.sbuf_tensor(name, shape, dt)
        t = h.__enter__()
        ctx.append(h)
        return t

    ln = alloc("ln", [128, NCH * T], BF16)
    PS = alloc("PS", [128, 16 * T], FP16)
    dist32 = alloc("dist32", [128, T], F32)
    qst = alloc("qst", [128, T], BF16)
    kst = alloc("kst", [128, T], BF16)
    sq = alloc("sq", [128, T], BF16)
    vtok = alloc("vtok", [128, 1024], BF16)
    vp65 = alloc("vp65", [128, 16 * 72], BF16)
    ao = alloc("ao", [128, NCH * T], BF16)
    whbuf = alloc("whbuf", [128, 3072], BF16)
    wobuf = alloc("wobuf", [128, 2048], BF16)
    xta = alloc("xta", [128, NCH * TT], F32)
    cbuf = alloc("cbuf", [128, NCH * TT], BF16)
    sq4 = alloc("sq4", [128, NCH * TT], BF16)
    rows = alloc("rows", [128, 1024], F32)
    msS = alloc("msS", [128, 512], F32)
    colbuf = alloc("colbuf", [128, 32], F32)
    ident = alloc("ident", [128, 128], BF16)
    ones_bf = alloc("ones_bf", [128, 1], BF16)
    ones_f = alloc("ones_f", [1, 128], F32)
    qcol = alloc("qcol", [128, 1], BF16)
    onesr = alloc("onesr", [1, 64], BF16)
    epsc = alloc("epsc", [128, 1], F32)
    rrow = alloc("rrow", [1, 512], F32)
    rrow_bf = alloc("rrow_bf", [1, 512], BF16)
    outsl = alloc("outsl", [128, 512], F32)
    pb = [alloc("pb%d" % i, [128, TT], F32, psum=True) for i in range(7)]
    ptr = alloc("ptr", [128, 1024], BF16, psum=True)
    banks = Banks(pb)
    sb = dict(ones_bf=ones_bf, ones_f=ones_f)
    sV, sS, sT, sX = (p.sem(n) for n in ("sV", "sS", "sT", "sX"))
    xi_r = xin.rearrange("(c p) t -> p c t", p=128)
    xo_r = xout.rearrange("(c p) t -> p c t", p=128)

    ph = Ph(p, "init")
    ph.G.memset(ones_bf[:, :], 1.0)
    ph.G.memset(ones_f[:, :], 1.0)
    ph.G.memset(qcol[:, :], 0.25)
    ph.G.memset(onesr[:, :], 1.0)
    ph.G.memset(epsc[:, :], 1e-12)
    ph.G.memset(kst[64:65, :], 1.0)
    c1 = iodma(ph, p, ident[:, :], identd)
    ph.G.wait_ge(sX, p.cnt["sX"])
    ph.emit()

    ph = Ph(p, "ln")
    emit_ln_stream(p, ph, banks, sb, xi_r, rows, [xta, xta], cbuf, sq4, ln,
                   msS[0:1, 0:TT])
    ph.emit()
    banks.reset()

    ph = Ph(p, "heads")
    prev_bcast = [0]
    def wh_dma(h, slot):
        wdma(ph, p, whbuf[:, slot * 1536:(slot + 1) * 1536], wqkv[h],
             sem="sW%d" % slot)
        return wdma(ph, p, colbuf[0:64, slot * 4:(slot + 1) * 4], bqkv[h],
                    sem="sW%d" % slot)
    whct = {0: wh_dma(0, 0), 1: wh_dma(1, 1)}
    ptr_cv = 0
    for h in range(16):
        slot = h % 2
        w3 = whbuf[:, slot * 1536:(slot + 1) * 1536].rearrange("p (k n) -> p k n", n=192)
        bq_c = colbuf[0:64, slot * 4 + 0: slot * 4 + 1]
        bk_c = colbuf[0:64, slot * 4 + 1: slot * 4 + 2]
        bv_c = colbuf[0:64, slot * 4 + 2: slot * 4 + 3]
        ph.T.wait_ge(p.sem("sW%d" % slot), whct[h])
        for w in range(3):
            for tt in range(NTT):
                bi, bk_ = banks.take(ph, p)
                for kc in range(NCH):
                    op = ph.T.matmul(bk_[0:64, 0:TT], w3[:, kc, w * 64:(w + 1) * 64],
                                     ln[:, kc * T + tt * TT: kc * T + (tt + 1) * TT],
                                     start=(kc == 0), stop=(kc == NCH - 1))
                ct = p.bump("sT")
                op.then_inc(sT, 1)
                sl = slice(tt * TT, (tt + 1) * TT)
                ph.V.wait_ge(sT, ct)
                cv = p.bump("sV")
                if w == 0:
                    ph.V.tensor_scalar_add(qst[0:64, sl], bk_[0:64, 0:TT],
                                           bq_c).then_inc(sV, 1)
                elif w == 1:
                    ph.V.tensor_scalar(kst[0:64, sl], bk_[0:64, 0:TT], bk_c, -2.0,
                                       ALU.add, ALU.mult).then_inc(sV, 1)
                else:
                    ph.V.tensor_scalar_add(sq[0:64, sl], bk_[0:64, 0:TT],
                                           bv_c).then_inc(sV, 1)
                banks.release(bi, "sV", cv)
        qkv_done = p.cnt["sV"]
        mm_done = p.cnt["sT"]
        if h + 2 < 16:
            ph.G.wait_ge(sT, mm_done)
            ph.G.wait_ge(sV, qkv_done)
            whct[h + 2] = wh_dma(h + 2, slot)
        for grp in range(2):
            if ptr_cv:
                ph.T.wait_ge(sV, ptr_cv)
            ph.T.wait_ge(sV, qkv_done)
            for t8 in range(8):
                ti = grp * 8 + t8
                op = ph.T.transpose(ptr[:, t8 * 128:t8 * 128 + 64],
                                    sq[0:64, ti * 128:(ti + 1) * 128],
                                    ident[0:64, 0:64])
            ct = p.bump("sT")
            op.then_inc(sT, 1)
            ph.V.wait_ge(sT, ct)
            for t8 in range(8):
                ti = grp * 8 + t8
                ph.V.tensor_copy(vtok[:, ti * 64:(ti + 1) * 64],
                                 ptr[:, t8 * 128:t8 * 128 + 64])
            cv = p.bump("sV")
            ph.V.ops[-1].then_inc(sV, 1)
            ptr_cv = cv
        ph.S.wait_ge(sV, qkv_done)
        ph.S.wait_ge(sT, p.cnt["sT"])
        cs = p.bump("sS")
        ph.S.activation(sq[0:64, :], qst[0:64, :], AF.Square).then_inc(sS, 1)
        ph.T.wait_ge(sS, cs)
        for tt in range(NTT):
            bi, bk_ = banks.take(ph, p)
            ct = p.bump("sT")
            ph.T.matmul(bk_[0:1, 0:TT], ones_bf[0:64, :],
                        sq[0:64, tt * TT:(tt + 1) * TT],
                        start=True, stop=True).then_inc(sT, 1)
            ph.V.wait_ge(sT, ct)
            cv = p.bump("sV")
            ph.V.tensor_copy(qst[64:65, tt * TT:(tt + 1) * TT],
                             bk_[0:1, 0:TT]).then_inc(sV, 1)
            banks.release(bi, "sV", cv)
        q2_done = p.cnt["sV"]
        cs = p.bump("sS")
        ph.S.wait_ge(sV, q2_done)
        ph.S.activation(sq[0:64, :], kst[0:64, :], AF.Square).then_inc(sS, 1)
        ph.T.wait_ge(sS, cs)
        for tt in range(NTT):
            bi, bk_ = banks.take(ph, p)
            ct = p.bump("sT")
            ph.T.matmul(bk_[0:1, 0:TT], qcol[0:64, 0:1],
                        sq[0:64, tt * TT:(tt + 1) * TT],
                        start=True, stop=True).then_inc(sT, 1)
            cv = p.bump("sV")
            ph.V.wait_ge(sT, ct)
            ph.V.tensor_copy(sq[64:65, tt * TT:(tt + 1) * TT],
                             bk_[0:1, 0:TT]).then_inc(sV, 1)
            banks.release(bi, "sV", cv)
        cv = p.cnt["sV"]
        ph.T.wait_ge(sV, cv)
        bi, bk_ = banks.take(ph, p)
        for jt in range(16):
            op = ph.T.matmul(bk_[:, jt:jt + 1], sq[64:65, jt * 128:(jt + 1) * 128],
                             kst[64:65, 0:1], start=True, stop=True)
        ct = p.bump("sT")
        op.then_inc(sT, 1)
        ph.V.wait_ge(sT, ct)
        cv = p.bump("sV")
        ph.V.tensor_copy(colbuf[:, 8:24], bk_[:, 0:16]).then_inc(sV, 1)
        banks.release(bi, "sV", cv)
        k2c_done = cv
        ph.S.wait_ge(sV, k2c_done)
        for jt in range(16):
            for it in range(NTT):
                bi, bk_ = banks.take(ph, p)
                ct = p.bump("sT")
                ph.T.matmul(bk_[:, 0:TT], kst[0:65, jt * 128:(jt + 1) * 128],
                            qst[0:65, it * TT:(it + 1) * TT],
                            start=True, stop=True).then_inc(sT, 1)
                ph.S.wait_ge(sT, ct)
                ph.S.activation(msS[:, 0:TT], bk_[:, 0:TT], AF.Relu,
                                bias=colbuf[:, 8 + jt:9 + jt])
                cs = p.bump("sS")
                ph.S.activation(PS[:, jt * T + it * TT: jt * T + (it + 1) * TT],
                                msS[:, 0:TT], AF.Sqrt, bias=epsc[:, 0:1]).then_inc(sS, 1)
                banks.release(bi, "sS", cs)
        PSb = PS[:, :].bitcast(BF16)
        for jt in range(16):
            ph.S.activation(dist32[:, :], PS[:, jt * T:(jt + 1) * T],
                            AF.Ln, bias=1.0, scale=1.0 / 64.0)
            cs = p.bump("sS")
            ph.S.activation(PSb[:, jt * T:(jt + 1) * T], dist32[:, :],
                            AF.Exp, scale=-65.5,
                            accum_out=colbuf[:, 24 + jt % 2:25 + jt % 2]).then_inc(sS, 1)
            ph.V.wait_ge(sS, cs)
            cvr = p.bump("sV")
            ph.V.reciprocal(colbuf[:, 26:27], colbuf[:, 24 + jt % 2:25 + jt % 2]).then_inc(sV, 1)
            ph.V.wait_ge(sV, cvr)
            cs2 = p.bump("sS")
            ph.S.wait_ge(sV, cvr)
            ph.S.activation(colbuf[:, 27:28], colbuf[:, 26:27], AF.Sqrt).then_inc(sS, 1)
            ph.V.wait_ge(sS, cs2)
            cv = p.bump("sV")
            ph.V.tensor_scalar_mul(vp65[:, jt * 72: jt * 72 + 64],
                                   vtok[:, jt * 64:(jt + 1) * 64],
                                   colbuf[:, 27:28]).then_inc(sV, 1)
            cv = p.bump("sV")
            ph.V.tensor_copy(vp65[:, jt * 72 + 64: jt * 72 + 65],
                             colbuf[:, 27:28]).then_inc(sV, 1)
        vp_done = p.cnt["sV"]
        if dbg == "h0b" and h == 0:
            ph.G.wait_ge(sV, vp_done)
            wdma(ph, p, dbgd[0][:, 0:1152], vp65[:, :])
            wdma(ph, p, dbgd[1][:, 0:1024], vtok[:, :])
        if dbg == "h0" and h == 0:
            ph.G.wait_ge(sV, vp_done)
            ph.G.wait_ge(p.sem("sS"), p.cnt["sS"])
            wdma(ph, p, dbgd[0], PSb[:, 0:T])
            wdma(ph, p, dbgd[1], PSb[:, 8 * T:9 * T])
            wdma(ph, p, dbgd[2], qst[:, :])
            wdma(ph, p, dbgd[3], kst[:, :])
        ph.T.wait_ge(sV, vp_done)
        for it in range(NTT):
            bi, bk_ = banks.take(ph, p)
            for jt in range(16):
                op = ph.T.matmul(bk_[0:65, 0:TT], vp65[:, jt * 72: jt * 72 + 65],
                                 PSb[:, jt * T + it * TT: jt * T + (it + 1) * TT],
                                 start=(jt == 0), stop=(jt == 15))
            ct = p.bump("sT")
            op.then_inc(sT, 1)
            ph.V.wait_ge(sT, ct)
            if prev_bcast[0]:
                ph.V.wait_ge(sT, prev_bcast[0])
            cvr = p.bump("sV")
            ph.V.reciprocal(rrow[0:1, 0:TT], bk_[64:65, 0:TT]).then_inc(sV, 1)
            ph.V.wait_ge(sV, cvr)
            cv = p.bump("sV")
            ph.V.tensor_copy(rrow_bf[0:1, 0:TT], rrow[0:1, 0:TT]).then_inc(sV, 1)
            ph.T.wait_ge(sV, cv)
            bi2, bk2 = banks.take(ph, p)
            ct2 = p.bump("sT")
            ph.T.matmul(bk2[0:64, 0:TT], onesr[0:1, :], rrow_bf[0:1, 0:TT],
                        start=True, stop=True).then_inc(sT, 1)
            prev_bcast[0] = ct2
            ph.V.wait_ge(sT, ct2)
            cv = p.bump("sV")
            ph.V.tensor_copy(sq[0:64, it * TT:(it + 1) * TT],
                             bk_[0:64, 0:TT]).then_inc(sV, 1)
            banks.release(bi, "sV", cv)
            ph.V.wait_ge(sV, cv)
            hs = slice((h % 2) * 64, (h % 2) * 64 + 64)
            cv = p.bump("sV")
            ph.V.tensor_mul(ao[hs, (h // 2) * T + it * TT: (h // 2) * T + (it + 1) * TT],
                            sq[0:64, it * TT:(it + 1) * TT],
                            bk2[0:64, 0:TT]).then_inc(sV, 1)
            banks.release(bi2, "sV", cv)
    if dbg in ("h0", "h0b"):
        ph.G.wait_ge(sV, p.cnt["sV"])
        wdma(ph, p, dbgd[2], ao[:, 0:T])
        wdma(ph, p, dbgd[3], ao[:, T:2 * T])
    ph.emit()
    banks.reset()

    ph = Ph(p, "wo")
    cbo = wdma(ph, p, colbuf[:, 0:8], bod, sem="sWc")
    def wo_dma(nn, slot):
        return wdma(ph, p, wobuf[:, slot * 1024:(slot + 1) * 1024], wod[nn],
                    sem="sW%d" % slot)
    wct = {0: wo_dma(0, 0), 1: wo_dma(1, 1)}
    ph.V.wait_ge(p.sem("sWc"), p.cnt["sWc"])
    st_cnt = {}
    ld_cnt = {}
    last_stt_cv = [0]
    k = 0
    for nn in range(NCH):
        slot = nn % 2
        w3 = wobuf[:, slot * 1024:(slot + 1) * 1024].rearrange("p (k n) -> p k n", n=128)
        ph.T.wait_ge(p.sem("sW%d" % slot), wct[nn])
        for tt in range(NTT):
            xslot = 0
            xt = xta
            if k >= 1:
                ph.G.wait_ge(sV, last_stt_cv[0])
            ld_cnt[k] = iodma(ph, p, xt[:, 0:TT], xi_r[:, nn, tt * TT:(tt + 1) * TT],
                              sem="sXi%d" % xslot)
            bi, bk_ = banks.take(ph, p)
            for kc in range(NCH):
                op = ph.T.matmul(bk_[:, 0:TT], w3[:, kc, :],
                                 ao[:, kc * T + tt * TT: kc * T + (tt + 1) * TT],
                                 start=(kc == 0), stop=(kc == NCH - 1))
            ct = p.bump("sT")
            op.then_inc(sT, 1)
            ph.V.wait_ge(sT, ct)
            ph.V.wait_ge(p.sem("sXi%d" % xslot), ld_cnt[k])
            if k >= 1:
                ph.V.wait_ge(p.sem("sXo"), st_cnt[k - 1])
            osl = outsl[:, 0:TT]
            cv = p.bump("sV")
            ph.V.scalar_tensor_tensor(osl, bk_[:, 0:TT], colbuf[:, nn:nn + 1],
                                      xt[:, 0:TT], ALU.add, ALU.add).then_inc(sV, 1)
            last_stt_cv[0] = cv
            banks.release(bi, "sV", cv)
            ph.G.wait_ge(sV, cv)
            st_cnt[k] = iodma(ph, p, xo_r[:, nn, tt * TT:(tt + 1) * TT], osl,
                              sem="sXo")
            k += 1
        if nn + 2 < NCH:
            ph.G.wait_ge(sT, p.cnt["sT"])
            wct[nn + 2] = wo_dma(nn + 2, slot)
    ph.G.wait_ge(p.sem("sXo"), p.cnt["sXo"])
    ph.emit()
    for h in reversed(ctx):
        h.__exit__(None, None, None)
    if finalize:
        nc.finalize()
    return nc


def prep_l0attn_weights(ins):
    f32 = np.float32
    WQ = np.asarray(ins["L0_attn_WQ"], f32)
    WK = np.asarray(ins["L0_attn_WK"], f32)
    WV = np.asarray(ins["L0_attn_WV"], f32)
    WO = np.asarray(ins["L0_attn_WO"], f32)
    bQ = np.asarray(ins["L0_attn_bQ"], f32)
    bK = np.asarray(ins["L0_attn_bK"], f32)
    bV = np.asarray(ins["L0_attn_bV"], f32)
    bO = np.asarray(ins["L0_attn_bO"], f32)
    wq_t = np.empty((16, 128, 8, 3, 64), f32)
    for h in range(16):
        for wi, W in enumerate((WQ, WK, WV)):
            wq_t[h, :, :, wi, :] = W[:, h * 64:(h + 1) * 64].reshape(8, 128, 64).transpose(1, 0, 2)
    bq_t = np.stack([np.stack([bQ.reshape(16, 64)[h], bK.reshape(16, 64)[h],
                               bV.reshape(16, 64)[h], np.zeros(64, f32)], axis=1)
                     for h in range(16)])
    wo_t = np.ascontiguousarray(
        WO.reshape(8, 128, 8, 128).transpose(2, 1, 0, 3)).reshape(8, 128, 1024)
    con = host_constants()
    bf = ml_dtypes.bfloat16
    return dict(
        wqkv=np.ascontiguousarray(wq_t).reshape(16, 128, 1536).astype(bf),
        bqkv=np.ascontiguousarray(bq_t),
        wod=wo_t.astype(bf),
        bod=np.ascontiguousarray(bO.reshape(8, 128).T),
        c_ident=con["c_ident"],
    )


def build_l1attn(dbg=None,
                 nc=None, p=None, seg="", xin_ap=None, xout_ap=None, finalize=True):
    if nc is None:
        nc = bass.Bass("TRN2", target_bir_lowering=False)
        p = Prog(nc)
    p.blkseg = seg
    xin = xin_ap if xin_ap is not None else nc.dram_tensor(
        "xin" + seg, [D, T], F32, kind="ExternalInput").ap()
    xout = xout_ap if xout_ap is not None else nc.dram_tensor(
        "xout" + seg, [D, T], F32, kind="ExternalOutput").ap()
    wdq = nc.dram_tensor("wdq" + seg, [2, 128, 1024], BF16, kind="ExternalInput").ap()
    wdkv = nc.dram_tensor("wdkv" + seg, [2, 128, 1024], BF16, kind="ExternalInput").ap()
    wuq = nc.dram_tensor("wuq" + seg, [128, 2048], BF16, kind="ExternalInput").ap()
    wuk = nc.dram_tensor("wuk" + seg, [128, 2048], BF16, kind="ExternalInput").ap()
    wuv = nc.dram_tensor("wuv" + seg, [128, 2048], BF16, kind="ExternalInput").ap()
    wod = nc.dram_tensor("wod" + seg, [8, 128, 1024], BF16, kind="ExternalInput").ap()
    identd = nc.dram_tensor("c_ident" + seg, [128, 128], BF16, kind="ExternalInput").ap()
    maskcd = nc.dram_tensor("c_mask_causal" + seg, [128, 2048], F32, kind="ExternalInput").ap()
    maskcTd = nc.dram_tensor("c_mask_causal_T" + seg, [128, 2048], F32, kind="ExternalInput").ap()
    dbgd = [nc.dram_tensor("dbg%d" % i + seg, [128, T], F32, kind="ExternalOutput").ap()
            for i in range(4)] if dbg else None

    ctx = []
    def alloc(name, shape, dt, psum=False):
        name = name + seg
        h = nc.psum_tensor(name, shape, dt) if psum else nc.sbuf_tensor(name, shape, dt)
        t = h.__enter__()
        ctx.append(h)
        return t

    ln = alloc("ln", [128, NCH * T], BF16)
    cq = alloc("cq", [128, 2 * T], BF16)
    ckv = alloc("ckv", [128, 2 * T], BF16)
    PS = alloc("PS", [128, 16 * TT], BF16)
    qst = alloc("qst", [128, T], BF16)
    kst = alloc("kst", [128, T], BF16)
    vtok = alloc("vtok", [128, 16 * 72], BF16)
    sq = alloc("sq", [128, T], BF16)
    ao = alloc("ao", [128, NCH * T], BF16)
    wbuf = alloc("wbuf", [128, 2048 * 3], BF16)
    wobuf = alloc("wobuf", [128, 2048], BF16)
    wdbuf = alloc("wdbuf", [128, 2048], BF16)
    xta = alloc("xta", [128, NCH * TT], F32)
    cbuf = alloc("cbuf", [128, NCH * TT], BF16)
    sq4 = alloc("sq4", [128, NCH * TT], BF16)
    rows = alloc("rows", [128, 1024], F32)
    msW = alloc("msW", [128, T], F32)
    maskc = alloc("maskc", [128, 2048], F32)
    maskcT = alloc("maskcT", [128, 2048], F32)
    colbuf = alloc("colbuf", [128, 32], F32)
    ident = alloc("ident", [128, 128], BF16)
    identf = alloc("identf", [128, 128], F32)
    ones_bf = alloc("ones_bf", [128, 1], BF16)
    pidb = None
    ones_f = alloc("ones_f", [1, 128], F32)
    rrow = alloc("rrow", [1, 512], F32)
    rrow_bf = alloc("rrow_bf", [1, 512], BF16)
    onesr = alloc("onesr", [1, 64], BF16)
    outsl = alloc("outsl", [128, 512], F32)
    pb = [alloc("pb%d" % i, [128, TT], F32, psum=True) for i in range(7)]
    ptr = alloc("ptr", [128, 1024], BF16, psum=True)
    banks = Banks(pb)
    sb = dict(ones_bf=ones_bf, ones_f=ones_f)
    sV, sS, sT, sX = (p.sem(n) for n in ("sV", "sS", "sT", "sX"))
    xi_r = xin.rearrange("(c p) t -> p c t", p=128)
    xo_r = xout.rearrange("(c p) t -> p c t", p=128)

    ph = Ph(p, "init")
    ph.G.memset(ones_bf[:, :], 1.0)
    ph.G.memset(ones_f[:, :], 1.0)
    ph.G.memset(onesr[:, :], 1.0)
    ph.G.memset(kst[64:65, :], 1.0)
    iodma(ph, p, ident[:, :], identd)
    iodma(ph, p, maskc[:, :], maskcd)
    iodma(ph, p, maskcT[:, :], maskcTd)
    c1 = wdma(ph, p, identf[:, :], identd)
    ph.G.wait_ge(sX, p.cnt["sX"])
    ph.G.wait_ge(p.sem("sW"), p.cnt["sW"])
    ph.emit()

    ph = Ph(p, "ln")
    emit_ln_stream(p, ph, banks, sb, xi_r, rows, [xta, xta], cbuf, sq4, ln,
                   msW[0:1, 0:TT])
    ph.emit()
    banks.reset()

    ph = Ph(p, "cqkv")
    cw = wdma(ph, p, wdbuf[:, 0:1024], wdq[0], sem="sW0")
    cw2 = wdma(ph, p, wdbuf[:, 1024:2048], wdq[1], sem="sW0")
    for half, (dst, wsrc) in enumerate(((cq, wdq), (ckv, wdkv))):
        for nn in range(2):
            w3 = wdbuf[:, nn * 1024:(nn + 1) * 1024].rearrange("p (k n) -> p k n", n=128)
            ph.T.wait_ge(p.sem("sW0"), p.cnt["sW0"] - (16 if (half == 0 and nn == 0) else 0))
            for tt in range(NTT):
                bi, bk_ = banks.take(ph, p)
                for kc in range(NCH):
                    op = ph.T.matmul(bk_[:, 0:TT], w3[:, kc, :],
                                     ln[:, kc * T + tt * TT: kc * T + (tt + 1) * TT],
                                     start=(kc == 0), stop=(kc == NCH - 1))
                ct = p.bump("sT")
                op.then_inc(sT, 1)
                ph.V.wait_ge(sT, ct)
                cv = p.bump("sV")
                ph.V.tensor_copy(dst[:, nn * T + tt * TT: nn * T + (tt + 1) * TT],
                                 bk_[:, 0:TT]).then_inc(sV, 1)
                banks.release(bi, "sV", cv)
        if half == 0:
            ph.G.wait_ge(sT, p.cnt["sT"])
            wdma(ph, p, wdbuf[:, 0:1024], wdkv[0], sem="sW0")
            wdma(ph, p, wdbuf[:, 1024:2048], wdkv[1], sem="sW0")
    wu_ct = []
    for i, wsrc in enumerate((wuq, wuk, wuv)):
        wu_ct.append(wdma(ph, p, wbuf[:, i * 2048:(i + 1) * 2048], wsrc, sem="sWc"))
    ph.emit()
    banks.reset()

    ph = Ph(p, "heads")
    ph.T.wait_ge(p.sem("sWc"), p.cnt["sWc"])
    ptr_cv = 0
    diag_cs = [0]
    prev_bcast = [0]
    for h in range(16):
        wq3 = wbuf[:, 0:2048].rearrange("p (k n) -> p k n", n=1024)
        wk3 = wbuf[:, 2048:4096].rearrange("p (k n) -> p k n", n=1024)
        wv3 = wbuf[:, 4096:6144].rearrange("p (k n) -> p k n", n=1024)
        for w in range(3):
            src = (cq, ckv, ckv)[w]
            w3 = (wq3, wk3, wv3)[w]
            for tt in range(NTT):
                bi, bk_ = banks.take(ph, p)
                for kc in range(2):
                    op = ph.T.matmul(bk_[0:64, 0:TT],
                                     w3[:, kc, h * 64:(h + 1) * 64],
                                     src[:, kc * T + tt * TT: kc * T + (tt + 1) * TT],
                                     start=(kc == 0), stop=(kc == 1))
                ct = p.bump("sT")
                op.then_inc(sT, 1)
                sl = slice(tt * TT, (tt + 1) * TT)
                ph.V.wait_ge(sT, ct)
                cv = p.bump("sV")
                if w == 0:
                    ph.V.tensor_scalar_mul(qst[0:64, sl], bk_[0:64, 0:TT],
                                           0.125).then_inc(sV, 1)
                elif w == 1:
                    ph.V.tensor_copy(kst[0:64, sl], bk_[0:64, 0:TT]).then_inc(sV, 1)
                else:
                    ph.V.tensor_copy(sq[0:64, sl], bk_[0:64, 0:TT]).then_inc(sV, 1)
                banks.release(bi, "sV", cv)
        qkv_done = p.cnt["sV"]
        for grp in range(2):
            if ptr_cv:
                ph.T.wait_ge(sV, ptr_cv)
            ph.T.wait_ge(sV, qkv_done)
            for t8 in range(8):
                ti = grp * 8 + t8
                op = ph.T.transpose(ptr[:, t8 * 128:t8 * 128 + 64],
                                    sq[0:64, ti * 128:(ti + 1) * 128],
                                    ident[0:64, 0:64])
            ct = p.bump("sT")
            op.then_inc(sT, 1)
            ph.V.wait_ge(sT, ct)
            for t8 in range(8):
                ti = grp * 8 + t8
                ph.V.tensor_copy(vtok[:, ti * 72:ti * 72 + 64],
                                 ptr[:, t8 * 128:t8 * 128 + 64])
                ph.V.memset(vtok[:, ti * 72 + 64:ti * 72 + 65], 1.0)
            cv = p.bump("sV")
            ph.V.ops[-1].then_inc(sV, 1)
            ptr_cv = cv
        vtok_done = ptr_cv
        for it in range(NTT):
            for qt in range(it * 4, it * 4 + 4):
                nk = qt // 4 + 1
                for kt in range(nk):
                    bi, bk_ = banks.take(ph, p)
                    ct = p.bump("sT")
                    ph.T.matmul(bk_[:, 0:TT], qst[0:64, qt * 128:(qt + 1) * 128],
                                kst[0:64, kt * TT:(kt + 1) * TT],
                                start=True, stop=True).then_inc(sT, 1)
                    ph.V.wait_ge(sT, ct)
                    cv = p.bump("sV")
                    if kt == qt // 4:
                        ph.V.tensor_add(msW[:, kt * TT:(kt + 1) * TT], bk_[:, 0:TT],
                                        maskc[:, (qt % 4) * TT:(qt % 4 + 1) * TT]
                                        ).then_inc(sV, 1)
                    else:
                        ph.V.tensor_copy(msW[:, kt * TT:(kt + 1) * TT],
                                         bk_[:, 0:TT]).then_inc(sV, 1)
                    banks.release(bi, "sV", cv)
                cvm = p.bump("sV")
                ph.V.reduce_max(colbuf[:, 8:9], msW[:, 0:nk * TT], AX.X,
                                negate=True).then_inc(sV, 1)
                ph.V.wait_ge(sV, cvm)
                bi, bk_ = banks.take(ph, p)
                ct = p.bump("sT")
                ph.T.wait_ge(sV, cvm)
                ph.T.transpose(bk_[0:1, 0:128], colbuf[:, 8:9],
                               identf[:, :]).then_inc(sT, 1)
                ph.V.wait_ge(sT, ct)
                cv = p.bump("sV")
                ph.V.tensor_copy(qst[64:65, qt * 128:(qt + 1) * 128],
                                 bk_[0:1, 0:128]).then_inc(sV, 1)
                banks.release(bi, "sV", cv)
            negm_done = p.cnt["sV"]
            ph.T.wait_ge(sV, negm_done)
            njc = 4 * (it + 1)
            for jc in range(njc):
                bi, bk_ = banks.take(ph, p)
                ct = p.bump("sT")
                ph.T.matmul(bk_[:, 0:TT], kst[0:65, jc * 128:(jc + 1) * 128],
                            qst[0:65, it * TT:(it + 1) * TT],
                            start=True, stop=True).then_inc(sT, 1)
                if jc // 4 == it:
                    ph.V.wait_ge(sT, ct)
                    if diag_cs[0]:
                        ph.V.wait_ge(sS, diag_cs[0])
                    cv = p.bump("sV")
                    ph.V.tensor_add(msW[:, 0:TT], bk_[:, 0:TT],
                                    maskcT[:, (jc % 4) * TT:((jc % 4) + 1) * TT]
                                    ).then_inc(sV, 1)
                    ph.S.wait_ge(sV, cv)
                    cs = p.bump("sS")
                    ph.S.activation(PS[:, jc * TT:(jc + 1) * TT], msW[:, 0:TT],
                                    AF.Exp).then_inc(sS, 1)
                    diag_cs[0] = cs
                    banks.release(bi, "sV", cv)
                else:
                    ph.S.wait_ge(sT, ct)
                    cs = p.bump("sS")
                    ph.S.activation(PS[:, jc * TT:(jc + 1) * TT], bk_[:, 0:TT],
                                    AF.Exp).then_inc(sS, 1)
                    banks.release(bi, "sS", cs)
            exp_done = p.cnt["sS"]
            ph.T.wait_ge(sS, exp_done)
            bi, bk_ = banks.take(ph, p)
            for jc in range(njc):
                op = ph.T.matmul(bk_[0:65, 0:TT], vtok[:, jc * 72:jc * 72 + 65],
                                 PS[:, jc * TT:(jc + 1) * TT],
                                 start=(jc == 0), stop=(jc == njc - 1))
            ct = p.bump("sT")
            op.then_inc(sT, 1)
            ph.V.wait_ge(sT, ct)
            if prev_bcast[0]:
                ph.V.wait_ge(sT, prev_bcast[0])
            cvr = p.bump("sV")
            ph.V.reciprocal(rrow[0:1, 0:TT], bk_[64:65, 0:TT]).then_inc(sV, 1)
            ph.V.wait_ge(sV, cvr)
            cv = p.bump("sV")
            ph.V.tensor_copy(rrow_bf[0:1, 0:TT], rrow[0:1, 0:TT]).then_inc(sV, 1)
            ph.T.wait_ge(sV, cv)
            bi2, bk2 = banks.take(ph, p)
            ct2 = p.bump("sT")
            ph.T.matmul(bk2[0:64, 0:TT], onesr[0:1, :], rrow_bf[0:1, 0:TT],
                        start=True, stop=True).then_inc(sT, 1)
            prev_bcast[0] = ct2
            ph.V.wait_ge(sT, ct2)
            cv = p.bump("sV")
            ph.V.tensor_copy(msW[:, 0:TT], bk_[:, 0:TT]).then_inc(sV, 1)
            banks.release(bi, "sV", cv)
            ph.V.wait_ge(sV, cv)
            hs = slice((h % 2) * 64, (h % 2) * 64 + 64)
            cv = p.bump("sV")
            ph.V.tensor_mul(ao[hs, (h // 2) * T + it * TT: (h // 2) * T + (it + 1) * TT],
                            msW[0:64, 0:TT], bk2[0:64, 0:TT]).then_inc(sV, 1)
            banks.release(bi2, "sV", cv)
    ph.emit()
    banks.reset()

    ph = Ph(p, "wo")
    def wo_dma(nn, slot):
        return wdma(ph, p, wobuf[:, slot * 1024:(slot + 1) * 1024], wod[nn],
                    sem="sW%d" % slot)
    wct = {0: wo_dma(0, 0), 1: wo_dma(1, 1)}
    st_cnt = {}
    last_stt = [0]
    k = 0
    for nn in range(NCH):
        slot = nn % 2
        w3 = wobuf[:, slot * 1024:(slot + 1) * 1024].rearrange("p (k n) -> p k n", n=128)
        ph.T.wait_ge(p.sem("sW%d" % slot), wct[nn])
        for tt in range(NTT):
            if k >= 1:
                ph.G.wait_ge(sV, last_stt[0])
            ldc = iodma(ph, p, xta[:, 0:TT], xi_r[:, nn, tt * TT:(tt + 1) * TT],
                        sem="sXi0")
            bi, bk_ = banks.take(ph, p)
            for kc in range(NCH):
                op = ph.T.matmul(bk_[:, 0:TT], w3[:, kc, :],
                                 ao[:, kc * T + tt * TT: kc * T + (tt + 1) * TT],
                                 start=(kc == 0), stop=(kc == NCH - 1))
            ct = p.bump("sT")
            op.then_inc(sT, 1)
            ph.V.wait_ge(sT, ct)
            ph.V.wait_ge(p.sem("sXi0"), ldc)
            if k >= 1:
                ph.V.wait_ge(p.sem("sXo"), st_cnt[k - 1])
            cv = p.bump("sV")
            ph.V.tensor_add(outsl[:, 0:TT], bk_[:, 0:TT], xta[:, 0:TT]).then_inc(sV, 1)
            last_stt[0] = cv
            banks.release(bi, "sV", cv)
            ph.G.wait_ge(sV, cv)
            st_cnt[k] = iodma(ph, p, xo_r[:, nn, tt * TT:(tt + 1) * TT],
                              outsl[:, 0:TT], sem="sXo")
            k += 1
        if nn + 2 < NCH:
            ph.G.wait_ge(sT, p.cnt["sT"])
            wct[nn + 2] = wo_dma(nn + 2, slot)
    ph.G.wait_ge(p.sem("sXo"), p.cnt["sXo"])
    ph.emit()
    for h in reversed(ctx):
        h.__exit__(None, None, None)
    if finalize:
        nc.finalize()
    return nc


def prep_l1attn_weights(ins):
    f32 = np.float32
    def two(W):
        return np.ascontiguousarray(
            np.asarray(W, f32).reshape(2, 128, 1024).transpose(0, 1, 2)
            if W.shape == (1024, 256) else W)
    W_dq = np.asarray(ins["L1_attn_W_dq"], f32)
    W_dkv = np.asarray(ins["L1_attn_W_dkv"], f32)
    def down_t(W):
        return np.ascontiguousarray(
            W.reshape(8, 128, 2, 128).transpose(2, 1, 0, 3)).reshape(2, 128, 1024)
    def up_t(W):
        return np.ascontiguousarray(
            np.asarray(W, f32).reshape(2, 128, 1024).transpose(1, 0, 2)).reshape(128, 2048)
    WO = np.asarray(ins["L1_attn_W_o"], f32)
    wo_t = np.ascontiguousarray(
        WO.reshape(8, 128, 8, 128).transpose(2, 1, 0, 3)).reshape(8, 128, 1024)
    con = host_constants()
    bf = ml_dtypes.bfloat16
    return dict(
        wdq=down_t(W_dq).astype(bf), wdkv=down_t(W_dkv).astype(bf),
        wuq=up_t(ins["L1_attn_W_uq"]).astype(bf), wuk=up_t(ins["L1_attn_W_uk"]).astype(bf),
        wuv=up_t(ins["L1_attn_W_uv"]).astype(bf),
        wod=wo_t.astype(bf),
        c_ident=con["c_ident"],
        c_mask_causal=np.ascontiguousarray(
            con["c_mask_causal"].reshape(4, 128, 512).transpose(1, 0, 2).reshape(128, 2048)),
        c_mask_causal_T=np.ascontiguousarray(
            con["c_mask_causal_T"].reshape(4, 128, 512).transpose(1, 0, 2).reshape(128, 2048)),
    )


CAPACITY = 5120.0

def build_moe(ncores=8, dbg=None, counts_mode=None,
              nc=None, p=None, seg="", xin_ap=None, xout_ap=None, finalize=True):
    if nc is None:
        nc = bass.Bass("TRN2", target_bir_lowering=False)
        p = Prog(nc)
    E = 8
    p.blkseg = seg
    xin = xin_ap if xin_ap is not None else nc.dram_tensor(
        "xin" + seg, [D, T], F32, kind="ExternalInput").ap()
    xout = None
    if counts_mode != "emit":
        xout = xout_ap if xout_ap is not None else nc.dram_tensor(
            "xout" + seg, [D, T], F32, kind="ExternalOutput").ap()
    wgt = nc.dram_tensor("wgt" + seg, [8, 128, 8], F32, kind="ExternalInput").ap()
    cfc = nc.dram_tensor("cfc" + seg, [8, 32, 128, 1024], BF16, kind="ExternalInput").ap()
    cpj = nc.dram_tensor("cpj" + seg, [8, 8, 128, 4096], BF16, kind="ExternalInput").ap()
    coreid = nc.dram_tensor("coreid" + seg, [1, 128], F32, kind="ExternalInput").ap()
    identd = nc.dram_tensor("c_ident" + seg, [128, 128], BF16, kind="ExternalInput").ap()
    io8d = nc.dram_tensor("c_iota8" + seg, [128, 8], F32, kind="ExternalInput").ap()
    io8m9d = nc.dram_tensor("c_io8m9" + seg, [128, 8], F32, kind="ExternalInput").ap()
    io8rd = nc.dram_tensor("c_iota8r" + seg, [64, 8], F32, kind="ExternalInput").ap()
    if counts_mode == "emit":
        cntout = nc.dram_tensor("cntout" + seg, [1, 64], F32, kind="ExternalOutput").ap()
    elif counts_mode == "input":
        g8in = nc.dram_tensor("g8in" + seg, [8, 64], F32, kind="ExternalInput").ap()
    dbgd = [nc.dram_tensor("dbg%d" % i + seg, [128, T], F32, kind="ExternalOutput").ap()
            for i in range(4)] if dbg else None

    ctx = []
    def alloc(name, shape, dt, psum=False):
        name = name + seg
        h = nc.psum_tensor(name, shape, dt) if psum else nc.sbuf_tensor(name, shape, dt)
        t = h.__enter__()
        ctx.append(h)
        return t

    ln = alloc("ln", [128, NCH * T], BF16)
    xta = alloc("xta", [128, NCH * TT], F32)
    cbuf = alloc("cbuf", [128, NCH * TT], BF16)
    sq4f = alloc("sq4f", [128, TT], F32)
    rows = alloc("rows", [128, 1024], F32)
    lgt = alloc("lgt", [128, T], F32)
    LG = alloc("LG", [128, 128], F32)
    eqf = alloc("eqf", [128, 8], F32)
    LG2 = None
    tmp8 = alloc("tmp8", [128, 8], F32)
    OH0tm = alloc("OH0tm", [128, 128], BF16)
    OH1tm = alloc("OH1tm", [128, 128], BF16)
    OH0e = alloc("OH0e", [128, T], F32)
    OH1e = alloc("OH1e", [128, T], F32)
    SC = alloc("SC", [128, T], F32)
    v0e = alloc("v0e", [128, T], BF16)
    v1e = alloc("v1e", [128, T], BF16)
    m0b = alloc("m0b", [128, T], BF16)
    vrow0 = alloc("vrow0", [1, T], BF16)
    vrow1 = alloc("vrow1", [1, T], BF16)
    wgtb = alloc("wgtb", [128, 64], F32)
    cb2 = alloc("cb2", [128, 160], F32)
    G8 = alloc("G8", [128, 64], F32)
    G8T = alloc("G8T", [128, 8], F32)
    maskT = alloc("maskT", [128, 8], F32)
    io8r = alloc("io8r", [64, 8], F32)
    io8c = alloc("io8c", [128, 8], F32)
    io8m9 = alloc("io8m9", [128, 8], F32)
    Wtm = alloc("Wtm", [128, 128], F32)
    wrow_bf = alloc("wrow_bf", [1, 2048], BF16)
    wbs = alloc("wbs", [128, 512], BF16)
    hbuf = alloc("hbuf", [128, 32 * 512], BF16)
    acc = alloc("acc", [128, NCH * 512], F32)
    cfcb = alloc("cfcb", [128, 2048], BF16)
    cpjb = alloc("cpjb", [128, 8192], BF16)
    msW = alloc("msW", [128, 512], F32)
    LG2 = msW[:, 0:8]
    t1s = alloc("t1s", [128, 512], F32)
    outsl = t1s
    _crow_holder = None
    crow = None
    ident = alloc("ident", [128, 128], BF16)
    identf = alloc("identf", [128, 128], F32)
    ones_bf = alloc("ones_bf", [128, 1], BF16)
    pidb = None
    ones_f = alloc("ones_f", [1, 128], F32)
    onesc_f = alloc("onesc_f", [128, 1], F32)
    onesr = alloc("onesr", [1, 128], BF16)
    pb = [alloc("pb%d" % i, [128, TT], F32, psum=True) for i in range(7)]
    ptr = alloc("ptr", [128, 1024], BF16, psum=True)
    banks = Banks(pb)
    crow = G8[64:65, 0:64]
    pidb = wrow_bf[0:1, 1024:1025]
    sb = dict(ones_bf=ones_bf, ones_f=ones_f)
    sV, sS, sT, sX = (p.sem(n) for n in ("sV", "sS", "sT", "sX"))
    sG = p.sem("sG")
    xi_r = xin.rearrange("(c p) t -> p c t", p=128)
    xo_r = xout.rearrange("(c p) t -> p c t", p=128) if xout is not None else None

    ph = Ph(p, "init")
    ph.G.memset(ones_bf[:, :], 1.0)
    ph.G.memset(ones_f[:, :], 1.0)
    ph.G.memset(onesc_f[:, :], 1.0)
    ph.G.memset(onesr[:, :], 1.0)
    iodma(ph, p, ident[:, :], identd)
    wdma(ph, p, identf[:, :], identd)
    iodma(ph, p, io8c[:, :], io8d)
    iodma(ph, p, io8m9[:, :], io8m9d)
    iodma(ph, p, io8r[:, :], io8rd)
    iodma(ph, p, wgtb[:, :].rearrange("p (k e) -> p k e", e=8),
          wgt.rearrange("k p e -> p k e"))
    iodma(ph, p, xta[0:1, 0:128], coreid)
    ph.G.wait_ge(sX, p.cnt["sX"])
    ph.G.wait_ge(p.sem("sW"), p.cnt["sW"])
    ph.emit()

    ph = Ph(p, "pid")
    cv = p.bump("sV")
    ph.V.tensor_copy(pidb[0:1, 0:1], xta[0:1, 0:1]).then_inc(sV, 1)
    ph.T.wait_ge(sV, cv)
    bi, bk_ = banks.take(ph, p)
    ct = p.bump("sT")
    ph.T.matmul(bk_[0:64, 0:1], onesr[0:1, 0:64], pidb[0:1, 0:1],
                start=True, stop=True).then_inc(sT, 1)
    ph.V.wait_ge(sT, ct)
    cv = p.bump("sV")
    ph.V.tensor_copy(cb2[0:64, 112:113], bk_[0:64, 0:1]).then_inc(sV, 1)
    banks.release(bi, "sV", cv)
    w3 = wgtb[:, :].rearrange("p (k e) -> p k e", e=8)
    bi, bk_ = banks.take(ph, p)
    for kc in range(NCH):
        op = ph.T.matmul(bk_[0:8, 0:1], w3[:, kc, :], onesc_f[:, 0:1],
                         start=(kc == 0), stop=(kc == NCH - 1))
    ct = p.bump("sT")
    op.then_inc(sT, 1)
    ph.V.wait_ge(sT, ct)
    cv = p.bump("sV")
    ph.V.tensor_copy(cb2[0:8, 104:105], bk_[0:8, 0:1]).then_inc(sV, 1)
    banks.release(bi, "sV", cv)
    ph.emit()
    banks.reset()

    ph = Ph(p, "ln")
    def logits_extra(tt, xt, mrow, vrow):
        bi, bk_ = banks.take(ph, p)
        for kc in range(NCH):
            op = ph.T.matmul(bk_[0:8, 0:TT], w3[:, kc, :],
                             xt[:, kc * TT:(kc + 1) * TT],
                             start=(kc == 0), stop=(kc == NCH - 1))
        ct = p.bump("sT")
        op.then_inc(sT, 1)
        bm_i, bm_ = banks.take(ph, p)
        ct2 = p.bump("sT")
        ph.T.matmul(bm_[0:8, 0:TT], ones_f[0:1, 0:8], mrow,
                    start=True, stop=True)
        ph.T.matmul(bm_[0:8, TT if False else 0:TT][0:8, 0:TT] if False else bm_[0:8, 0:TT],
                    ones_f[0:1, 0:8], mrow, start=True, stop=True).then_inc(sT, 1)
        br_i, br_ = banks.take(ph, p)
        ct3 = p.bump("sT")
        ph.T.matmul(br_[0:8, 0:TT], ones_f[0:1, 0:8], vrow,
                    start=True, stop=True).then_inc(sT, 1)
        ph.V.wait_ge(sT, ct3)
        sl = slice(tt * TT, (tt + 1) * TT)
        ph.V.tensor_scalar_mul(t1s[0:8, 0:TT], bm_[0:8, 0:TT], cb2[0:8, 104:105])
        ph.V.tensor_sub(t1s[0:8, 0:TT], bk_[0:8, 0:TT], t1s[0:8, 0:TT])
        cv = p.bump("sV")
        ph.V.tensor_mul(lgt[0:8, sl], t1s[0:8, 0:TT], br_[0:8, 0:TT]).then_inc(sV, 1)
        banks.release(bi, "sV", cv)
        banks.release(bm_i, "sV", cv)
        banks.release(br_i, "sV", cv)
    emit_ln_stream(p, ph, banks, sb, xi_r, rows, [xta, xta], cbuf,
                   hbuf[:, 0:NCH * TT], ln,
                   msW[0:1, 0:TT], extra_fn=logits_extra, fp32_stats=True,
                   sq4f=sq4f, onesc_f=onesc_f)
    ph.emit()
    banks.reset()

    ph = Ph(p, "route")
    lgt_done = p.cnt["sV"]
    ph.T.wait_ge(sV, lgt_done)
    for g in range(16):
        bi, bk_ = banks.take(ph, p)
        ct = p.bump("sT")
        ph.T.transpose(bk_[0:128, 0:8], lgt[0:8, g * 128:(g + 1) * 128],
                       identf[0:8, 0:8]).then_inc(sT, 1)
        ph.V.wait_ge(sT, ct)
        cv = p.bump("sV")
        ph.V.tensor_copy(LG[:, g * 8:(g + 1) * 8], bk_[0:128, 0:8]).then_inc(sV, 1)
        banks.release(bi, "sV", cv)
    def vstep(op):
        cvm = p.bump("sV")
        op.then_inc(sV, 1)
        ph.V.wait_ge(sV, cvm)
    for g in range(16):
        Lg = LG[:, g * 8:(g + 1) * 8]
        vstep(ph.V.reduce_max(cb2[:, 96:97], Lg, AX.X))
        vstep(ph.V.tensor_single_scalar(eqf[:, :], Lg, cb2[:, 96:97], ALU.is_equal))
        vstep(ph.V.tensor_mul(tmp8[:, :], eqf[:, :], io8m9[:, :]))
        vstep(ph.V.tensor_scalar_add(tmp8[:, :], tmp8[:, :], 9.0))
        vstep(ph.V.tensor_reduce(cb2[:, 97:98], tmp8[:, :], AX.X, ALU.min))
        vstep(ph.V.tensor_single_scalar(eqf[:, :], io8c[:, :], cb2[:, 97:98], ALU.is_equal))
        vstep(ph.V.tensor_copy(OH0tm[:, g * 8:(g + 1) * 8], eqf[:, :]))
        vstep(ph.V.tensor_copy(cb2[:, g:g + 1], cb2[:, 96:97]))
        vstep(ph.V.scalar_tensor_tensor(LG2[:, :], eqf[:, :], NEG, Lg,
                                        ALU.mult, ALU.add))
        vstep(ph.V.reduce_max(cb2[:, 96:97], LG2[:, :], AX.X))
        vstep(ph.V.tensor_single_scalar(eqf[:, :], LG2[:, :], cb2[:, 96:97], ALU.is_equal))
        vstep(ph.V.tensor_mul(tmp8[:, :], eqf[:, :], io8m9[:, :]))
        vstep(ph.V.tensor_scalar_add(tmp8[:, :], tmp8[:, :], 9.0))
        vstep(ph.V.tensor_reduce(cb2[:, 98:99], tmp8[:, :], AX.X, ALU.min))
        vstep(ph.V.tensor_single_scalar(eqf[:, :], io8c[:, :], cb2[:, 98:99], ALU.is_equal))
        vstep(ph.V.tensor_copy(OH1tm[:, g * 8:(g + 1) * 8], eqf[:, :]))
        vstep(ph.V.tensor_copy(cb2[:, 16 + g:17 + g], cb2[:, 96:97]))
    cv = p.bump("sV")
    ph.V.tensor_sub(cb2[:, 96:112], cb2[:, 0:16], cb2[:, 16:32]).then_inc(sV, 1)
    ph.S.wait_ge(sV, cv)
    cs = p.bump("sS")
    ph.S.activation(cb2[:, 32:48], cb2[:, 96:112], AF.Sigmoid).then_inc(sS, 1)
    ph.V.wait_ge(sS, cs)
    cv = p.bump("sV")
    ph.V.tensor_scalar(cb2[:, 48:64], cb2[:, 32:48], -1.0, 1.0,
                       ALU.mult, ALU.add).then_inc(sV, 1)
    oh_done = p.cnt["sV"]
    ph.T.wait_ge(sV, oh_done)
    for k, (tm, em) in enumerate(((OH0tm, OH0e), (OH1tm, OH1e))):
        for g in range(16):
            bi, bk_ = banks.take(ph, p)
            ct = p.bump("sT")
            ph.T.transpose(ptr[0:8, 0:128], tm[:, g * 8:(g + 1) * 8],
                           ident[:, :]).then_inc(sT, 1)
            ph.V.wait_ge(sT, ct)
            cv = p.bump("sV")
            ph.V.tensor_copy(em[0:8, g * 128:(g + 1) * 128],
                             ptr[0:8, 0:128]).then_inc(sV, 1)
            banks.release(bi, "sV", cv)
            ph.T.wait_ge(sV, cv)
    ohe_done = p.cnt["sV"]
    cv0 = p.bump("sV")
    ph.V.tensor_tensor_scan(SC[0:8, :], OH0e[0:8, :], OH0e[0:8, :], 0.0,
                            ALU.add, ALU.bypass).then_inc(sV, 1)
    ph.V.wait_ge(sV, cv0)
    bi, bk_ = banks.take(ph, p)
    ct = p.bump("sT")
    ph.T.wait_ge(sV, cv0)
    ph.T.transpose(bk_[0:1, 0:8], SC[0:8, T - 1:T], identf[0:8, 0:8]).then_inc(sT, 1)
    ph.V.wait_ge(sT, ct)
    cv = p.bump("sV")
    ph.V.tensor_copy(crow[0:1, 0:8], bk_[0:1, 0:8]).then_inc(sV, 1)
    banks.release(bi, "sV", cv)
    cv = p.bump("sV")
    ph.V.tensor_sub(SC[0:8, :], SC[0:8, :], OH0e[0:8, :]).then_inc(sV, 1)
    excl0_done = cv
    ph.emit()
    banks.reset()

    ph = Ph(p, "cc")
    cv1 = p.bump("sV")
    ph.V.tensor_tensor_scan(lgt[0:8, :] if False else OH0e[64:72, 0:T] if False else SC[32:40, 0:T],
                            OH1e[0:8, :], OH1e[0:8, :], 0.0,
                            ALU.add, ALU.bypass).then_inc(sV, 1)
    ph.V.wait_ge(sV, cv1)
    ph.V.wait_ge(sV, cv1)
    cvc = p.bump("sV")
    ph.V.tensor_copy(eqf[0:8, 0:1], SC[32:40, T - 1:T]).then_inc(sV, 1)
    bi, bk_ = banks.take(ph, p)
    ct = p.bump("sT")
    ph.T.wait_ge(sV, cvc)
    ph.T.transpose(bk_[0:1, 0:8], eqf[0:8, 0:1], identf[0:8, 0:8]).then_inc(sT, 1)
    ph.V.wait_ge(sT, ct)
    ph.V.memset(crow[0:1, 8:32], 0.0)
    ph.V.memset(crow[0:1, 40:64], 0.0)
    cv = p.bump("sV")
    ph.V.tensor_copy(crow[0:1, 32:40], bk_[0:1, 0:8]).then_inc(sV, 1)
    banks.release(bi, "sV", cv)
    ph.G.wait_ge(sV, cv)
    if counts_mode == "emit":
        cx1 = iodma(ph, p, cntout, crow[0:1, :], sem="sCC")
        ph.G.wait_ge(p.sem("sCC"), cx1)
        ph.emit()
        for h in reversed(ctx):
            h.__exit__(None, None, None)
        if finalize:
            nc.finalize()
        return nc
    if counts_mode == "input":
        cx1 = iodma(ph, p, G8[0:8, :], g8in, sem="sCC")
        ph.G.wait_ge(p.sem("sCC"), cx1)
    else:
        cv = p.bump("sV")
        ph.V.tensor_copy(G8[0:1, :], crow[0:1, :]).then_inc(sV, 1)
        ph.G.wait_ge(sV, cv)
    ph.emit()
    banks.reset()

    ph = Ph(p, "rank")
    bi, bk_ = banks.take(ph, p)
    ct = p.bump("sT")
    ph.T.transpose(bk_[0:64, 0:ncores], G8[0:ncores, 0:64],
                   identf[0:ncores, 0:ncores]).then_inc(sT, 1)
    ph.V.wait_ge(sT, ct)
    cv = p.bump("sV")
    ph.V.tensor_copy(G8T[0:64, 0:ncores], bk_[0:64, 0:ncores]).then_inc(sV, 1)
    banks.release(bi, "sV", cv)
    ph.V.tensor_single_scalar(maskT[0:64, 0:ncores], io8r[0:64, 0:ncores],
                              cb2[0:64, 112:113], ALU.is_lt)
    cv = p.bump("sV")
    ph.V.tensor_mul(maskT[0:64, 0:ncores], maskT[0:64, 0:ncores],
                    G8T[0:64, 0:ncores]).then_inc(sV, 1)
    ph.V.wait_ge(sV, cv)
    cv = p.bump("sV")
    ph.V.reduce_sum(cb2[0:64, 113:114], maskT[0:64, 0:ncores], AX.X).then_inc(sV, 1)
    ph.V.wait_ge(sV, cv)
    cv = p.bump("sV")
    ph.V.reduce_sum(cb2[0:64, 114:115], G8T[0:64, 0:ncores], AX.X).then_inc(sV, 1)
    ph.V.wait_ge(sV, cv)
    ph.V.tensor_scalar_add(SC[0:8, :], SC[0:8, :], cb2[0:8, 113:114])
    cv = p.bump("sV")
    ph.V.tensor_single_scalar(v0e[0:8, :], SC[0:8, :], CAPACITY,
                              ALU.is_lt).then_inc(sV, 1)
    ph.V.wait_ge(sV, cv)
    ph.V.tensor_copy(SC[0:8, :], SC[32:40, 0:T])
    ph.V.tensor_copy(cb2[0:8, 115:116], cb2[32:40, 113:114])
    ph.V.tensor_sub(SC[0:8, :], SC[0:8, :], OH1e[0:8, :])
    ph.V.tensor_scalar_add(SC[0:8, :], SC[0:8, :], cb2[0:8, 115:116])
    cv = p.bump("sV")
    ph.V.tensor_scalar_add(SC[0:8, :], SC[0:8, :],
                           cb2[0:8, 114:115]).then_inc(sV, 1)
    ph.V.wait_ge(sV, cv)
    cv = p.bump("sV")
    ph.V.tensor_single_scalar(v1e[0:8, :], SC[0:8, :], CAPACITY,
                              ALU.is_lt).then_inc(sV, 1)
    valid_done = cv
    for k, (oe, ve, vr) in enumerate(((OH0e, v0e, vrow0), (OH1e, v1e, vrow1))):
        cv = p.bump("sV")
        ph.V.tensor_mul(m0b[0:8, :], oe[0:8, :], ve[0:8, :]).then_inc(sV, 1)
        ph.T.wait_ge(sV, cv)
        for tt in range(NTT):
            bi, bk_ = banks.take(ph, p)
            ct = p.bump("sT")
            ph.T.matmul(bk_[0:1, 0:TT], ones_bf[0:8, :],
                        m0b[0:8, tt * TT:(tt + 1) * TT],
                        start=True, stop=True).then_inc(sT, 1)
            ph.V.wait_ge(sT, ct)
            cv = p.bump("sV")
            ph.V.tensor_copy(vr[0:1, tt * TT:(tt + 1) * TT],
                             bk_[0:1, 0:TT]).then_inc(sV, 1)
            banks.release(bi, "sV", cv)
    ph.T.wait_ge(sV, p.cnt["sV"])
    for k, (vr, col) in enumerate(((vrow0, 64), (vrow1, 80))):
        for g in range(16):
            bi, bk_ = banks.take(ph, p)
            ct = p.bump("sT")
            ph.T.matmul(bk_[0:128, 0:1], vr[0:1, g * 128:(g + 1) * 128],
                        ones_bf[0:1, :], start=True, stop=True).then_inc(sT, 1)
            ph.V.wait_ge(sT, ct)
            cv = p.bump("sV")
            ph.V.tensor_copy(cb2[:, col + g:col + g + 1],
                             bk_[0:128, 0:1]).then_inc(sV, 1)
            banks.release(bi, "sV", cv)
    ph.V.tensor_mul(cb2[:, 32:48], cb2[:, 32:48], cb2[:, 64:80])
    cv = p.bump("sV")
    ph.V.tensor_mul(cb2[:, 48:64], cb2[:, 48:64], cb2[:, 80:96]).then_inc(sV, 1)
    ph.V.wait_ge(sV, cv)
    for g in range(16):
        ph.V.tensor_scalar_mul(Wtm[:, g * 8:(g + 1) * 8],
                               OH0tm[:, g * 8:(g + 1) * 8], cb2[:, 32 + g:33 + g])
        cv = p.bump("sV")
        ph.V.scalar_tensor_tensor(Wtm[:, g * 8:(g + 1) * 8],
                                  OH1tm[:, g * 8:(g + 1) * 8],
                                  cb2[:, 48 + g:49 + g], Wtm[:, g * 8:(g + 1) * 8],
                                  ALU.mult, ALU.add).then_inc(sV, 1)
    W_done = p.cnt["sV"]
    if dbg == "route":
        ph.G.wait_ge(sV, W_done)
        wdma(ph, p, dbgd[0][:, 0:128], Wtm[:, :])
        wdma(ph, p, dbgd[1][:, 0:T], lgt[:, 0:T])
        wdma(ph, p, dbgd[2][:, 0:T], SC[:, 0:T])
        wdma(ph, p, dbgd[3][:, 0:128], LG[:, :])
    ph.emit()
    banks.reset()

    for tq in range(NTT):
        ph = Ph(p, "exp%d" % tq)
        ph.G.memset(acc[:, :], 0.0)
        cg = p.bump("sG")
        ph.G.ops[-1].then_inc(sG, 1)
        ph.V.wait_ge(sG, cg)
        cfc_ct = {}
        cpj_ct = {}
        for e in range(E):
            ph.T.wait_ge(sV, p.cnt["sV"])
            for g in range(4 * tq, 4 * tq + 4):
                bi, bk_ = banks.take(ph, p)
                ct = p.bump("sT")
                ph.T.transpose(bk_[0:1, 0:128], Wtm[:, g * 8 + e:g * 8 + e + 1],
                               identf[:, :]).then_inc(sT, 1)
                ph.V.wait_ge(sT, ct)
                cv = p.bump("sV")
                ph.V.tensor_copy(wrow_bf[0:1, (g - 4 * tq) * 128:(g - 4 * tq + 1) * 128],
                                 bk_[0:1, 0:128]).then_inc(sV, 1)
                banks.release(bi, "sV", cv)
            cv = p.cnt["sV"]
            ph.T.wait_ge(sV, cv)
            bi, bk_ = banks.take(ph, p)
            ct = p.bump("sT")
            ph.T.matmul(bk_[:, 0:TT], onesr[0:1, :], wrow_bf[0:1, 0:TT],
                        start=True, stop=True).then_inc(sT, 1)
            ph.V.wait_ge(sT, ct)
            cv = p.bump("sV")
            ph.V.tensor_copy(wbs[:, 0:TT], bk_[:, 0:TT]).then_inc(sV, 1)
            banks.release(bi, "sV", cv)
            wbs_done = cv
            slot = e % 2
            if e == 0 and tq == 0:
                cfc_ct[0] = wdma(ph, p, cfcb[:, 0:1024], cfc[0, 0], sem="sWa0")
            for j in range(32):
                jslot = j % 2
                if j + 1 < 32 or e + 1 < E:
                    nj = (j + 1) % 32
                    ne = e + 1 if j + 1 == 32 else e
                    if (j, e) == (0, 0) and tq == 0:
                        pass
                key = (e, j)
                if key not in cfc_ct:
                    cfc_ct[key] = wdma(ph, p, cfcb[:, jslot * 1024:(jslot + 1) * 1024],
                                       cfc[e, j], sem="sWa%d" % jslot)
                nkey = (e, j + 1) if j + 1 < 32 else ((e + 1, 0) if e + 1 < E else None)
                wf3 = cfcb[:, jslot * 1024:(jslot + 1) * 1024].rearrange(
                    "p (k n) -> p k n", n=128)
                ph.T.wait_ge(p.sem("sWa%d" % jslot), cfc_ct[key])
                bi, bk_ = banks.take(ph, p)
                for kc in range(NCH):
                    op = ph.T.matmul(bk_[:, 0:TT], wf3[:, kc, :],
                                     ln[:, kc * T + tq * TT: kc * T + (tq + 1) * TT],
                                     start=(kc == 0), stop=(kc == NCH - 1))
                ct = p.bump("sT")
                op.then_inc(sT, 1)
                if nkey is not None:
                    ph.G.wait_ge(sT, ct)
                    cfc_ct[nkey] = wdma(
                        ph, p, cfcb[:, (1 - jslot) * 1024:(2 - jslot) * 1024],
                        cfc[nkey[0], nkey[1]], sem="sWa%d" % (1 - jslot))
                ph.S.wait_ge(sT, ct)
                cs = p.bump("sS")
                ph.S.activation(msW[:, 0:TT], bk_[:, 0:TT], AF.Relu).then_inc(sS, 1)
                banks.release(bi, "sS", cs)
                ph.V.wait_ge(sS, cs)
                cv = p.bump("sV")
                ph.V.tensor_mul(hbuf[:, j * TT:(j + 1) * TT], msW[:, 0:TT],
                                msW[:, 0:TT]).then_inc(sV, 1)
            h_done = p.cnt["sV"]
            ph.T.wait_ge(sV, h_done)
            for nn in range(NCH):
                pslot = nn % 2
                key = (e, nn)
                if key not in cpj_ct:
                    cpj_ct[key] = wdma(ph, p, cpjb[:, pslot * 4096:(pslot + 1) * 4096],
                                       cpj[e, nn], sem="sWb%d" % pslot)
                wp3 = cpjb[:, pslot * 4096:(pslot + 1) * 4096].rearrange(
                    "p (k n) -> p k n", n=128)
                ph.T.wait_ge(p.sem("sWb%d" % pslot), cpj_ct[key])
                bi, bk_ = banks.take(ph, p)
                for kc in range(32):
                    op = ph.T.matmul(bk_[:, 0:TT], wp3[:, kc, :],
                                     hbuf[:, kc * TT:(kc + 1) * TT],
                                     start=(kc == 0), stop=(kc == 31))
                ct = p.bump("sT")
                op.then_inc(sT, 1)
                nkey = (e, nn + 1) if nn + 1 < NCH else ((e + 1, 0) if e + 1 < E else None)
                if nkey is not None:
                    ph.G.wait_ge(sT, ct)
                    cpj_ct[nkey] = wdma(
                        ph, p, cpjb[:, (1 - pslot) * 4096:(2 - pslot) * 4096],
                        cpj[nkey[0], nkey[1]], sem="sWb%d" % (1 - pslot))
                ph.V.wait_ge(sT, ct)
                ph.V.wait_ge(sV, wbs_done)
                ph.V.tensor_mul(t1s[:, 0:TT], bk_[:, 0:TT], wbs[:, 0:TT])
                cv = p.bump("sV")
                ph.V.tensor_add(acc[:, nn * TT:(nn + 1) * TT],
                                acc[:, nn * TT:(nn + 1) * TT],
                                t1s[:, 0:TT]).then_inc(sV, 1)
                banks.release(bi, "sV", cv)
        st = [0]
        for nn in range(NCH):
            if st[0]:
                ph.G.wait_ge(sV, st[0])
            ldc = iodma(ph, p, xta[:, 0:TT], xi_r[:, nn, tq * TT:(tq + 1) * TT],
                        sem="sXi0")
            ph.V.wait_ge(p.sem("sXi0"), ldc)
            if nn >= 1:
                ph.V.wait_ge(p.sem("sXo"), p.cnt["sXo"])
            cv = p.bump("sV")
            ph.V.scalar_tensor_tensor(outsl[:, 0:TT], acc[:, nn * TT:(nn + 1) * TT],
                                      0.5, xta[:, 0:TT],
                                      ALU.mult, ALU.add).then_inc(sV, 1)
            st[0] = cv
            ph.G.wait_ge(sV, cv)
            iodma(ph, p, xo_r[:, nn, tq * TT:(tq + 1) * TT], outsl[:, 0:TT],
                  sem="sXo")
        ph.G.wait_ge(p.sem("sXo"), p.cnt["sXo"])
        ph.emit()
        banks.reset()
    for h in reversed(ctx):
        h.__exit__(None, None, None)
    if finalize:
        nc.finalize()
    return nc


def prep_moe_weights(ins, core_id=0):
    f32 = np.float32
    w_g = np.asarray(ins["L1_ffn_w_g"], f32)
    c_fc = np.asarray(ins["L1_ffn_c_fc"], f32)
    c_proj = np.asarray(ins["L1_ffn_c_proj"], f32)
    wgt = np.ascontiguousarray(w_g.T.reshape(8, 128, 8))
    cfc_t = np.ascontiguousarray(
        c_fc.reshape(8, 8, 128, 32, 128).transpose(0, 3, 2, 1, 4)).reshape(8, 32, 128, 1024)
    cpj_t = np.ascontiguousarray(
        c_proj.reshape(8, 32, 128, 8, 128).transpose(0, 3, 2, 1, 4)).reshape(8, 8, 128, 4096)
    con = host_constants()
    io8m9 = np.broadcast_to(np.arange(8, dtype=f32) - 9.0, (128, 8)).copy()
    io8r = np.broadcast_to(np.arange(8, dtype=f32)[None, :], (64, 8)).copy()
    bf = ml_dtypes.bfloat16
    return dict(
        wgt=wgt, cfc=cfc_t.astype(bf), cpj=cpj_t.astype(bf),
        coreid=np.full((1, 128), float(core_id), f32),
        c_ident=con["c_ident"], c_iota8=con["c_iota8"],
        c_io8m9=io8m9, c_iota8r=io8r,
    )



_PROG_CACHE = {}

def _progs():
    if "ffn" not in _PROG_CACHE:
        _PROG_CACHE["ffn"] = build_l0ffn()
        _PROG_CACHE["attn0"] = build_l0attn()
        _PROG_CACHE["moe_emit"] = build_moe(ncores=8, counts_mode="emit")
        _PROG_CACHE["moe_in"] = build_moe(ncores=8, counts_mode="input")
        _PROG_CACHE["attn1"] = build_l1attn()
    return _PROG_CACHE


def kernel_unfused(**inputs):
    ins = {k: np.asarray(v) for k, v in inputs.items()}
    x = np.asarray(ins["L0_attn_x"], np.float32)
    nco = 8
    progs = _progs()
    w_ffn = prep_l0ffn_weights(ins)
    w_a0 = prep_l0attn_weights(ins)
    w_a1 = prep_l1attn_weights(ins)
    w_moe = prep_moe_weights(ins, core_id=0)
    moe_core = [dict(w_moe, coreid=np.full((1, 128), float(i), np.float32))
                for i in range(nco)]

    xt = [np.ascontiguousarray(x[i].T) for i in range(nco)]

    def launch(nc, w_percore):
        in_maps = [dict(w_percore[i], xin=xt[i]) for i in range(nco)]
        res = run_bass_kernel_spmd(nc, in_maps, core_ids=list(range(nco)))
        for i in range(nco):
            xt[i] = np.ascontiguousarray(np.asarray(res.results[i]["xout"]))

    def launch_moe():
        in_maps = [dict(moe_core[i], xin=xt[i]) for i in range(nco)]
        res = run_bass_kernel_spmd(progs["moe_emit"], in_maps,
                                   core_ids=list(range(nco)))
        g8 = np.ascontiguousarray(np.concatenate(
            [np.asarray(res.results[i]["cntout"]) for i in range(nco)], axis=0))
        in_maps = [dict(moe_core[i], xin=xt[i], g8in=g8) for i in range(nco)]
        res = run_bass_kernel_spmd(progs["moe_in"], in_maps,
                                   core_ids=list(range(nco)))
        for i in range(nco):
            xt[i] = np.ascontiguousarray(np.asarray(res.results[i]["xout"]))

    shared = lambda w: [w] * nco
    launch(progs["ffn"], shared(w_ffn))
    launch(progs["attn0"], shared(w_a0))
    launch(progs["ffn"], shared(w_ffn))
    launch_moe()
    launch(progs["attn1"], shared(w_a1))
    launch_moe()

    out = np.stack([xt[i].T for i in range(nco)], axis=0).astype(np.float32)
    return out



def build_chain1():
    nc = bass.Bass("TRN2", target_bir_lowering=False, dynamic_dma_scratch_size=16384)
    p = Prog(nc)
    xin = nc.dram_tensor("xin", [D, T], F32, kind="ExternalInput").ap()
    xb1 = nc.dram_tensor("xb1", [D, T], F32, kind="Internal").ap()
    xb2 = nc.dram_tensor("xb2", [D, T], F32, kind="Internal").ap()
    xmid = nc.dram_tensor("xmid1", [D, T], F32, kind="ExternalOutput").ap()
    build_l0ffn(nc=nc, p=p, seg="_a", xin_ap=xin, xout_ap=xb1, finalize=False)
    build_l0attn(nc=nc, p=p, seg="_b", xin_ap=xb1, xout_ap=xb2, finalize=False)
    build_l0ffn(nc=nc, p=p, seg="_c", xin_ap=xb2, xout_ap=xmid, finalize=False)
    build_moe(ncores=8, counts_mode="emit", nc=nc, p=p, seg="_d",
              xin_ap=xmid, finalize=False)
    nc.finalize()
    return nc


def build_chain2():
    nc = bass.Bass("TRN2", target_bir_lowering=False, dynamic_dma_scratch_size=16384)
    p = Prog(nc)
    xin = nc.dram_tensor("xin", [D, T], F32, kind="ExternalInput").ap()
    xb1 = nc.dram_tensor("xb1", [D, T], F32, kind="Internal").ap()
    xmid = nc.dram_tensor("xmid2", [D, T], F32, kind="ExternalOutput").ap()
    build_moe(ncores=8, counts_mode="input", nc=nc, p=p, seg="_e",
              xin_ap=xin, xout_ap=xb1, finalize=False)
    build_l1attn(nc=nc, p=p, seg="_f", xin_ap=xb1, xout_ap=xmid, finalize=False)
    build_moe(ncores=8, counts_mode="emit", nc=nc, p=p, seg="_g",
              xin_ap=xmid, finalize=False)
    nc.finalize()
    return nc


def build_chain3():
    nc = bass.Bass("TRN2", target_bir_lowering=False, dynamic_dma_scratch_size=16384)
    p = Prog(nc)
    xin = nc.dram_tensor("xin", [D, T], F32, kind="ExternalInput").ap()
    xout = nc.dram_tensor("xout", [D, T], F32, kind="ExternalOutput").ap()
    build_moe(ncores=8, counts_mode="input", nc=nc, p=p, seg="_h",
              xin_ap=xin, xout_ap=xout, finalize=False)
    nc.finalize()
    return nc


def _seg(w, seg):
    return {k + seg: v for k, v in w.items()}


def kernel_chained(**inputs):
    ins = {k: np.asarray(v) for k, v in inputs.items()}
    x = np.asarray(ins["L0_attn_x"], np.float32)
    nco = 8
    if "c1" not in _PROG_CACHE:
        _PROG_CACHE["c1"] = build_chain1()
        _PROG_CACHE["c2"] = build_chain2()
        _PROG_CACHE["c3"] = build_chain3()
    w_ffn = prep_l0ffn_weights(ins)
    w_a0 = prep_l0attn_weights(ins)
    w_a1 = prep_l1attn_weights(ins)
    w_moe = prep_moe_weights(ins, core_id=0)
    moe_core = [dict(w_moe, coreid=np.full((1, 128), float(i), np.float32))
                for i in range(nco)]
    xt = [np.ascontiguousarray(x[i].T) for i in range(nco)]

    c1_maps = [{**_seg(w_ffn, "_a"), **_seg(w_a0, "_b"), **_seg(w_ffn, "_c"),
                **_seg(moe_core[i], "_d"), "xin": xt[i]} for i in range(nco)]
    res = run_bass_kernel_spmd(_PROG_CACHE["c1"], c1_maps, core_ids=list(range(nco)))
    g8 = np.ascontiguousarray(np.concatenate(
        [np.asarray(res.results[i]["cntout_d"]) for i in range(nco)], axis=0))
    xm = [np.ascontiguousarray(np.asarray(res.results[i]["xmid1"])) for i in range(nco)]

    c2_maps = [{**_seg(dict(moe_core[i], g8in=g8), "_e"), **_seg(w_a1, "_f"),
                **_seg(moe_core[i], "_g"), "xin": xm[i]} for i in range(nco)]
    res = run_bass_kernel_spmd(_PROG_CACHE["c2"], c2_maps, core_ids=list(range(nco)))
    g82 = np.ascontiguousarray(np.concatenate(
        [np.asarray(res.results[i]["cntout_g"]) for i in range(nco)], axis=0))
    xm = [np.ascontiguousarray(np.asarray(res.results[i]["xmid2"])) for i in range(nco)]

    c3_maps = [{**_seg(dict(moe_core[i], g8in=g82), "_h"), "xin": xm[i]}
               for i in range(nco)]
    res = run_bass_kernel_spmd(_PROG_CACHE["c3"], c3_maps, core_ids=list(range(nco)))
    out = np.stack([np.asarray(res.results[i]["xout"]).T for i in range(nco)],
                   axis=0).astype(np.float32)
    return out


kernel = kernel_chained
